# version 05
# baseline (speedup 1.0000x reference)
import numpy as np
import concourse.bass as bass
import concourse.mybir as mybir

F32 = mybir.dt.float32
BF16 = mybir.dt.bfloat16
I32 = mybir.dt.int32
ALU = mybir.AluOpType
AF = mybir.ActivationFunctionType
AX = mybir.AxisListType

ENGINES = ("sync", "vector", "scalar", "tensor", "gpsimd")
C = 2048
TT = 2048
NT = TT // 128
NC = C // 128


def ts(i, n=128):
    return slice(i * n, (i + 1) * n)


class Prog:
    def __init__(self, nc):
        self.nc = nc
        self.steps = {e: [] for e in ENGINES}
        self.count = {e: 0 for e in ENGINES}
        self.last = None
        self.stage = "init"

    def step(self, engine, fn, deps=(), dma=False, tag="", raw=False):
        inc = 0 if raw else (16 if dma else 1)
        waits = [d for d in deps if d is not None]
        self.count[engine] += inc
        tok = (engine, self.count[engine])
        self.steps[engine].append((waits, fn, inc, self.stage))
        self.last = tok
        return tok

    def s(self, engine, fn, extra=(), dma=False, tag="", raw=False):
        return self.step(engine, fn, [self.last] + list(extra), dma=dma, tag=tag, raw=raw)

    def sdma(self, out, in_, engine="sync", extra=(), tag=""):
        return self.s(engine, lambda e: e.dma_start(out=out, in_=in_), extra, dma=True, tag=tag)

    def build(self):
        nc = self.nc
        sem_cms = {e: nc.semaphore("psem_" + e) for e in ENGINES}
        self.sems = {e: cm.__enter__() for e, cm in sem_cms.items()}
        block_cm = nc.Block()
        block = block_cm.__enter__()
        sems = self.sems

        def mk(engine):
            steps = self.steps[engine]

            def body(eng):
                waited = {}
                cur_stage = None
                cur_sid = None
                for waits, fn, inc, stage in steps:
                    if stage != cur_stage:
                        if cur_stage is not None:
                            nc.leave_named_scope(cur_stage, cur_sid, False)
                        cur_sid, _ = nc.enter_named_scope(stage, False)
                        cur_stage = stage
                    for (de, dc) in waits:
                        if waited.get(de, 0) >= dc:
                            continue
                        eng.wait_ge(sems[de], dc)
                        waited[de] = max(waited.get(de, 0), dc)
                    inst = fn(eng)
                    if inc:
                        inst.then_inc(sems[engine], inc)
                if cur_stage is not None:
                    nc.leave_named_scope(cur_stage, cur_sid, False)
            return body

        for e in ENGINES:
            if self.steps[e]:
                getattr(block, e)(mk(e))
        block_cm.__exit__(None, None, None)
        for cm in sem_cms.values():
            cm.__exit__(None, None, None)


class Ctx:
    def __init__(self, nc):
        self.nc = nc
        self._cms = []
        self._scopes = []
        self.p = Prog(nc)

    def sbuf(self, name, shape, dtype):
        cm = self.nc.sbuf_tensor(name, shape, dtype)
        t = cm.__enter__()
        (self._scopes[-1] if self._scopes else self._cms).append(cm)
        return t

    def psum(self, name, shape, dtype):
        cm = self.nc.psum_tensor(name, shape, dtype)
        t = cm.__enter__()
        (self._scopes[-1] if self._scopes else self._cms).append(cm)
        return t

    def push_scope(self):
        self._scopes.append([])

    def pop_scope(self):
        for cm in reversed(self._scopes.pop()):
            cm.__exit__(None, None, None)

    def close(self):
        self.p.s("gpsimd", lambda e: e.engine_nop(), tag="final")
        self.p.build()
        for cm in reversed(self._cms):
            cm.__exit__(None, None, None)
        return self.nc


class Common:
    def __init__(self, cx, ident_ap):
        p = cx.p
        self.identf = cx.sbuf("cm_identf", [128, 128], F32)
        self.idb = cx.sbuf("cm_idb", [128, 128], BF16)
        p.sdma(self.identf[:], ident_ap)
        p.s("vector", lambda e: e.tensor_copy(self.idb[:], self.identf[:]))


def emit_ln_stats_tile(p, xs, sq, st, Cw=C):
    p.s("vector", lambda e: e.reduce_sum(st[:, 0:1], xs[:], axis=AX.X))
    p.s("scalar", lambda e: e.activation(sq[:], xs[:], AF.Square, accum_out=st[:, 1:2]))
    p.s("vector", lambda e: e.tensor_scalar(st[:, 2:3], st[:, 0:1], 1.0 / Cw, None, ALU.mult))
    p.s("vector", lambda e: e.tensor_scalar(st[:, 3:4], st[:, 1:2], 1.0 / Cw, None, ALU.mult))
    p.s("vector", lambda e: e.tensor_tensor(st[:, 4:5], st[:, 2:3], st[:, 2:3], ALU.mult))
    p.s("vector", lambda e: e.tensor_tensor(st[:, 5:6], st[:, 3:4], st[:, 4:5], ALU.subtract))
    p.s("vector", lambda e: e.tensor_scalar(st[:, 5:6], st[:, 5:6], 1e-5, None, ALU.add))
    p.s("scalar", lambda e: e.activation(st[:, 6:7], st[:, 5:6], AF.Ln))
    p.s("scalar", lambda e: e.activation(st[:, 7:8], st[:, 6:7], AF.Exp, scale=-0.5))


def emit_ln(cx, pfx, x_t, o_t, w_ap=None, b_ap=None, r_ts=(), halo_out=None):
    p = cx.p
    cx.push_scope()
    xt = x_t.rearrange("(n p) c -> n p c", p=128)
    rts = [r.rearrange("(n p) c -> n p c", p=128) for r in r_ts]
    ot = o_t.rearrange("(n p) c -> n p c", p=128)
    xs = cx.sbuf(pfx + "xs", [128, C], F32)
    rsb = [cx.sbuf(pfx + f"rs{j}", [128, C], F32) for j in range(len(r_ts))]
    osb = cx.sbuf(pfx + "osb", [128, C], F32)
    sq = cx.sbuf(pfx + "sq", [128, C], F32)
    st = cx.sbuf(pfx + "st", [128, 8], F32)
    if w_ap is not None:
        wt = cx.sbuf(pfx + "wt", [128, C], F32)
        bt = cx.sbuf(pfx + "bt", [128, C], F32)
        p.sdma(wt[:], w_ap)
        p.sdma(bt[:], b_ap)
    for i in range(NT):
        p.sdma(xs[:], xt[i])
        for ri in range(len(r_ts)):
            p.sdma(rsb[ri][:], rts[ri][i])
            p.s("vector", lambda e, ri=ri: e.tensor_tensor(xs[:], xs[:], rsb[ri][:], ALU.add))
        emit_ln_stats_tile(p, xs, sq, st)
        p.s("vector", lambda e: e.tensor_scalar(osb[:], xs[:], st[:, 2:3], st[:, 7:8], ALU.subtract, ALU.mult))
        if w_ap is not None:
            p.s("vector", lambda e: e.tensor_tensor(osb[:], osb[:], wt[:], ALU.mult))
            p.s("vector", lambda e: e.tensor_tensor(osb[:], osb[:], bt[:], ALU.add))
        p.sdma(ot[i], osb[:])
        if halo_out is not None:
            p.sdma(halo_out[i:i+1, :], osb[127:128, :])
    cx.pop_scope()


def emit_cc(cx, kind, in_ap, out_ap, ccsem, ccount, alu=None, groups8=False):
    p = cx.p
    op = alu if alu is not None else ALU.bypass
    groups = [[0, 1, 2, 3, 4, 5, 6, 7]] if groups8 else [[0, 1], [2, 3], [4, 5], [6, 7]]

    def fn(e):
        return e.collective_compute(kind, op, replica_groups=groups,
                                    ins=[in_ap.opt()], outs=[out_ap.opt()]).then_inc(ccsem, 1)
    p.s("gpsimd", fn, tag="cc", raw=True)
    ccount[0] += 1
    need = ccount[0]

    def fn2(e):
        e.wait_ge(ccsem, need)
        return e.engine_nop()
    p.s("gpsimd", fn2, tag="ccwait")


def emit_shift_tile(cx, pfx_bufs, xsrc_dram, halo_ap, i, dst_sb, svec, zmask, hrow):
    p = cx.p
    p.sdma(dst_sb[1:128, :], xsrc_dram[i * 128: i * 128 + 127, :])
    jm1 = max(i - 1, 0)
    p.sdma(hrow[0:1, 0:C], halo_ap[0:1, i, :])
    p.sdma(hrow[0:1, C:2*C], halo_ap[1:2, jm1, :])
    p.s("vector", lambda e: e.tensor_tensor(hrow[0:1, C:2*C], hrow[0:1, C:2*C], hrow[0:1, 0:C], ALU.subtract))
    p.s("vector", lambda e: e.scalar_tensor_tensor(dst_sb[0:1, :], hrow[0:1, C:2*C], svec[0:1, 0:1], hrow[0:1, 0:C], ALU.mult, ALU.add))
    p.s("vector", lambda e, i=i: e.tensor_scalar(dst_sb[0:1, :], dst_sb[0:1, :], zmask[0:1, i:i+1], None, ALU.mult))


def emit_mix_kvr(cx, cm, aps, xn_dram, halo_ap, svec, zmask, k_out, v_out, sr_out):
    p = cx.p
    cx.push_scope()
    TCH = 512
    NCH = TT // TCH
    xn_t = xn_dram.rearrange("(n p) c -> n p c", p=128)
    xnb = cx.sbuf("mk_xnb", [128, C], BF16)
    ldf = cx.sbuf("mk_ldf", [128, C], F32)
    hrow = cx.sbuf("mk_hrow", [1, 2 * C], F32)
    psT = cx.psum("mk_psT", [128, 512], BF16)
    xn_c = cx.sbuf("mk_xn_c", [128, NC, TCH], BF16)
    xx_c = cx.sbuf("mk_xx_c", [128, NC, TCH], BF16)
    mixed = cx.sbuf("mk_mixed", [128, NC, TCH], BF16)
    tmv_s = cx.sbuf("mk_tmv", [128, 3, NC], F32)
    wstrip = cx.sbuf("mk_ws", [128, NC, 128], F32)
    wstripb = cx.sbuf("mk_wsb", [128, NC, 128], BF16)
    psM = cx.psum("mk_psM", [128, TCH], F32)
    ev = cx.sbuf("mk_ev", [128, TCH], F32)

    for wi, tm in enumerate(["tmk", "tmv", "tmr"]):
        p.sdma(tmv_s[:, wi, :], aps[tm])
    for ch in range(NCH):
        for which, dst in [(0, xn_c), (1, xx_c)]:
            for tti in range(TCH // 128):
                i = ch * (TCH // 128) + tti
                if which == 0:
                    p.sdma(ldf[:], xn_t[i])
                else:
                    emit_shift_tile(cx, None, xn_dram, halo_ap, i, ldf, svec, zmask, hrow)
                p.s("vector", lambda e: e.tensor_copy(xnb[:], ldf[:]))
                for c4 in range(NC // 4):
                    for cc in range(4):
                        c = c4 * 4 + cc
                        p.s("tensor", lambda e, c=c, cc=cc: e.transpose(psT[:, ts(cc)], xnb[:, ts(c)], cm.idb[:]))
                    p.s("scalar", lambda e, c4=c4, dst=dst, tti=tti: e.activation(
                        dst[:, c4*4:(c4+1)*4, tti*128:(tti+1)*128],
                        psT[:].rearrange("p (a b) -> p a b", a=4), AF.Copy))
        for wi, (wn, outd, sig) in enumerate([("WkT", k_out, False), ("WvT", v_out, False), ("WrT", sr_out, True)]):
            for c in range(NC):
                p.s("vector", lambda e, c=c: e.tensor_tensor(mixed[:, c, :], xn_c[:, c, :], xx_c[:, c, :], ALU.subtract))
                p.s("vector", lambda e, c=c, wi=wi: e.scalar_tensor_tensor(mixed[:, c, :], mixed[:, c, :], tmv_s[:, wi, c:c+1], xx_c[:, c, :], ALU.mult, ALU.add))
            for dt in range(NC):
                p.sdma(wstrip[:], aps[wn].rearrange("(k p) d -> p k d", p=128)[:, :, ts(dt)])
                p.s("vector", lambda e: e.tensor_copy(wstripb[:], wstrip[:]))
                for kt2 in range(NC):
                    p.s("tensor", lambda e, kt2=kt2: e.matmul(psM[:], wstripb[:, kt2, :], mixed[:, kt2, :], start=(kt2 == 0), stop=(kt2 == NC - 1)))
                if sig:
                    p.s("scalar", lambda e: e.activation(ev[:], psM[:], AF.Sigmoid))
                else:
                    p.s("scalar", lambda e: e.activation(ev[:], psM[:], AF.Copy))
                p.sdma(outd.rearrange("(n p) t -> n p t", p=128)[dt, :, ch*TCH:(ch+1)*TCH], ev[:])
    cx.pop_scope()


def emit_wkv(cx, aps, kga, vga, y_out, svec_sb):
    p = cx.p
    cx.push_scope()
    TF = 4096
    NCT = 8
    kt = kga.rearrange("(two half ct p) (j i) -> half ct p j two i", two=2, half=2, p=128, i=128)
    vt = vga.rearrange("(two half ct p) (j i) -> half ct p j two i", two=2, half=2, p=128, i=128)
    yo = y_out.rearrange("(ct p) t -> ct p t", p=128)
    ks = cx.sbuf("wk_ks", [128, TF], F32)
    vs = cx.sbuf("wk_vs", [128, TF], F32)
    kb2 = cx.sbuf("wk_kb2", [128, TF], F32)
    es = cx.sbuf("wk_es", [128, TF], F32)
    zs = cx.sbuf("wk_zs", [128, TF], F32)
    rz = cx.sbuf("wk_rz", [128, TF + 1], F32)
    rr = cx.sbuf("wk_rr", [128, TF + 1], F32)
    num = cx.sbuf("wk_num", [128, TF], F32)
    den = cx.sbuf("wk_den", [128, TF], F32)
    tdb = cx.sbuf("wk_tdb", [128, NCT], F32)
    tfb = cx.sbuf("wk_tfb", [128, NCT], F32)
    lam = cx.sbuf("wk_lam", [128, NCT], F32)
    eu = cx.sbuf("wk_eu", [128, NCT], F32)
    sc = cx.sbuf("wk_sc", [128, 2], F32)

    p.sdma(tdb[:], aps["td8"])
    p.sdma(tfb[:], aps["tf8"])
    p.s("scalar", lambda e: e.activation(lam[:], tdb[:], AF.Exp))
    p.s("scalar", lambda e: e.activation(lam[:], lam[:], AF.Exp, scale=-1.0))
    p.s("scalar", lambda e: e.activation(eu[:], tfb[:], AF.Exp))
    for c in range(NCT):
        for src_t, dst in [(kt, ks), (vt, vs)]:
            p.sdma(kb2.ap().rearrange("p (j two i) -> p j two i", two=2, i=128), src_t[0, c])
            p.sdma(dst.ap().rearrange("p (j two i) -> p j two i", two=2, i=128), src_t[1, c])
            p.s("vector", lambda e, dst=dst: e.tensor_tensor(dst[:], dst[:], kb2[:], ALU.subtract))
            p.s("vector", lambda e, dst=dst: e.scalar_tensor_tensor(dst[:], dst[:], svec_sb, kb2[:], ALU.mult, ALU.add))
        p.s("vector", lambda e: e.reduce_max(sc[:, 0:1], ks[:], axis=AX.X))
        p.s("vector", lambda e: e.tensor_scalar(sc[:, 1:2], sc[:, 0:1], -1.0, None, ALU.mult))
        p.s("scalar", lambda e: e.activation(es[:], ks[:], AF.Exp, bias=sc[:, 1:2]))
        p.s("vector", lambda e: e.tensor_tensor(zs[:], es[:], vs[:], ALU.mult))
        p.s("vector", lambda e: e.memset(rz[:, 0:1], 0.0))
        p.s("vector", lambda e: e.memset(rr[:, 0:1], 0.0))
        p.s("vector", lambda e, c=c: e.tensor_tensor_scan(
            rz[:, 1:TF + 1], lam[:, c:c + 1].broadcast_to([128, TF]), zs[:], 0.0, ALU.mult, ALU.add))
        p.s("vector", lambda e, c=c: e.tensor_tensor_scan(
            rr[:, 1:TF + 1], lam[:, c:c + 1].broadcast_to([128, TF]), es[:], 0.0, ALU.mult, ALU.add))
        p.s("vector", lambda e, c=c: e.scalar_tensor_tensor(num[:], zs[:], eu[:, c:c + 1], rz[:, 0:TF], ALU.mult, ALU.add))
        p.s("vector", lambda e, c=c: e.scalar_tensor_tensor(den[:], es[:], eu[:, c:c + 1], rr[:, 0:TF], ALU.mult, ALU.add))
        p.s("scalar", lambda e: e.activation(den[:], den[:], AF.Ln))
        p.s("scalar", lambda e: e.activation(den[:], den[:], AF.Exp, scale=-1.0))
        p.s("vector", lambda e: e.tensor_tensor(num[:], num[:], den[:], ALU.mult))
        p.sdma(yo[c], num[:])
    cx.pop_scope()


def emit_wo_res(cx, cm, wo_ap, att_ap, sr_ap, x_ap, x2_ap, use_sr=True, D=C,
                att_is_gather=False, svec_sb=None, pfx="wo_"):
    p = cx.p
    cx.push_scope()
    if att_is_gather:
        att_g = att_ap.rearrange("(n p) (j two i) -> two p n j i", p=128, two=2, i=128)
    else:
        att_r = att_ap.rearrange("(n p) t -> p n t", p=128)
    sr_r = sr_ap.rearrange("(n p) t -> p n t", p=128) if use_sr else None
    xt = x_ap.rearrange("(n p) c -> n p c", p=128)
    x2t = x2_ap.rearrange("(n p) c -> n p c", p=128)
    wof = cx.sbuf(pfx + "wof", [128, D], F32)
    wob = cx.sbuf(pfx + "wob", [128, NC, D], BF16)
    av = cx.sbuf(pfx + "av", [128, NC, 128], F32)
    av2 = cx.sbuf(pfx + "av2", [128, NC, 128], F32)
    sv = cx.sbuf(pfx + "sv", [128, NC, 128], F32)
    sab = cx.sbuf(pfx + "sab", [128, NC, 128], BF16)
    xs = cx.sbuf(pfx + "xs", [128, D], F32)
    ev = cx.sbuf(pfx + "ev", [128, D], F32)
    ps = cx.psum(pfx + "ps", [128, D], F32)

    for kt2 in range(NC):
        p.sdma(wof[:], wo_ap.rearrange("(n p) d -> n p d", p=128)[kt2])
        p.s("vector", lambda e, kt2=kt2: e.tensor_copy(wob[:, kt2, :], wof[:]))
    for i in range(NT):
        if att_is_gather:
            p.sdma(av2[:], att_g[0, :, :, i, :])
            p.sdma(av[:], att_g[1, :, :, i, :])
            p.s("vector", lambda e: e.tensor_tensor(av[:], av[:], av2[:], ALU.subtract))
            p.s("vector", lambda e: e.scalar_tensor_tensor(av[:], av[:], svec_sb, av2[:], ALU.mult, ALU.add))
        else:
            p.sdma(av[:], att_r[:, :, ts(i)])
        if use_sr:
            p.sdma(sv[:], sr_r[:, :, ts(i)])
            p.s("vector", lambda e: e.tensor_tensor(sab[:], av[:], sv[:], ALU.mult))
        else:
            p.s("vector", lambda e: e.tensor_copy(sab[:], av[:]))
        for kt2 in range(NC):
            for dh in range(D // 512):
                p.s("tensor", lambda e, kt2=kt2, dh=dh: e.matmul(
                    ps[:, dh*512:(dh+1)*512], sab[:, kt2, :], wob[:, kt2, dh*512:(dh+1)*512],
                    start=(kt2 == 0), stop=(kt2 == NC - 1)))
        p.sdma(xs[:], xt[i])
        p.s("vector", lambda e: e.tensor_tensor(ev[:], ps[:], xs[:], ALU.add))
        p.sdma(x2t[i], ev[:])
    cx.pop_scope()


def emit_cmix(cx, cm, aps, xn2_dram, halo_ap, svec, zmask, rkvt_out, H=8192):
    p = cx.p
    cx.push_scope()
    TCH = 512
    NCH = TT // TCH
    NH = H // 128
    NSLAB = 4
    SL = NH // NSLAB
    ldf = cx.sbuf("cmx_ldf", [128, C], F32)
    bfa = cx.sbuf("cmx_bfa", [128, C], BF16)
    hrow = cx.sbuf("cmx_hrow", [1, 2 * C], F32)
    psT = cx.psum("cmx_psT", [128, 512], BF16)
    A = cx.sbuf("cmx_A", [128, NC, TCH], BF16)
    Bb = cx.sbuf("cmx_B", [128, NC, TCH], BF16)
    ck = cx.sbuf("cmx_ck", [128, NC, TCH], BF16)
    cr = cx.sbuf("cmx_cr", [128, NC, TCH], BF16)
    cmks = cx.sbuf("cmx_cmks", [128, 2, NC], F32)
    wstrip = cx.sbuf("cmx_ws", [128, NC, 128], F32)
    wstripb = cx.sbuf("cmx_wsb", [128, NC, 128], BF16)
    hslab = cx.sbuf("cmx_hslab", [128, SL, TCH], BF16)
    kvacc = cx.sbuf("cmx_kvacc", [128, NC, TCH], F32)
    psM = cx.psum("cmx_psM", [128, TCH], F32)
    ev = cx.sbuf("cmx_ev", [128, TCH], F32)
    sg = cx.sbuf("cmx_sg", [128, TCH], F32)
    psTT = cx.psum("cmx_psTT", [128, TCH], F32)
    evT = cx.sbuf("cmx_evT", [128, TCH], F32)
    rk_t = rkvt_out.rearrange("(n p) c -> n p c", p=128)

    p.sdma(cmks[:, 0, :], aps["cmk"])
    p.sdma(cmks[:, 1, :], aps["cmr"])
    for ch in range(NCH):
        for which, dst in [(0, A), (1, Bb)]:
            for tti in range(TCH // 128):
                i = ch * (TCH // 128) + tti
                if which == 0:
                    p.sdma(ldf[:], xn2_dram[i*128:(i+1)*128, :])
                else:
                    emit_shift_tile(cx, None, xn2_dram, halo_ap, i, ldf, svec, zmask, hrow)
                p.s("vector", lambda e: e.tensor_copy(bfa[:], ldf[:]))
                for c4 in range(NC // 4):
                    for cc in range(4):
                        c = c4 * 4 + cc
                        p.s("tensor", lambda e, c=c, cc=cc: e.transpose(psT[:, ts(cc)], bfa[:, ts(c)], cm.idb[:]))
                    p.s("scalar", lambda e, c4=c4, dst=dst, tti=tti: e.activation(
                        dst[:, c4*4:(c4+1)*4, tti*128:(tti+1)*128],
                        psT[:].rearrange("p (a b) -> p a b", a=4), AF.Copy))
        for c in range(NC):
            p.s("vector", lambda e, c=c: e.tensor_tensor(cr[:, c, :], A[:, c, :], Bb[:, c, :], ALU.subtract))
            p.s("vector", lambda e, c=c: e.scalar_tensor_tensor(ck[:, c, :], cr[:, c, :], cmks[:, 0, c:c+1], Bb[:, c, :], ALU.mult, ALU.add))
            p.s("vector", lambda e, c=c: e.scalar_tensor_tensor(cr[:, c, :], cr[:, c, :], cmks[:, 1, c:c+1], Bb[:, c, :], ALU.mult, ALU.add))
        p.s("vector", lambda e: e.memset(kvacc[:], 0.0))
        for sl in range(NSLAB):
            for hj in range(SL):
                ht = sl * SL + hj
                p.sdma(wstrip[:], aps["WckT"].rearrange("(n p) h -> p n h", p=128)[:, :, ts(ht)])
                p.s("vector", lambda e: e.tensor_copy(wstripb[:], wstrip[:]))
                for kt2 in range(NC):
                    p.s("tensor", lambda e, kt2=kt2: e.matmul(psM[:], wstripb[:, kt2, :], ck[:, kt2, :], start=(kt2 == 0), stop=(kt2 == NC - 1)))
                p.s("vector", lambda e: e.tensor_scalar_max(ev[:], psM[:], 0.0))
                p.s("scalar", lambda e, hj=hj: e.activation(hslab[:, hj, :], ev[:], AF.Square))
            for ct in range(NC):
                p.sdma(wstrip[:, 0:SL, :], aps["WcvT"].rearrange("(n p) c -> p n c", p=128)[:, sl*SL:(sl+1)*SL, ts(ct)])
                p.s("vector", lambda e: e.tensor_copy(wstripb[:, 0:SL, :], wstrip[:, 0:SL, :]))
                for hj in range(SL):
                    p.s("tensor", lambda e, hj=hj: e.matmul(psM[:], wstripb[:, hj, :], hslab[:, hj, :], start=(hj == 0), stop=(hj == SL - 1)))
                p.s("vector", lambda e, ct=ct: e.tensor_tensor(kvacc[:, ct, :], kvacc[:, ct, :], psM[:], ALU.add))
        for ct in range(NC):
            p.sdma(wstrip[:], aps["WcrT"].rearrange("(n p) d -> p n d", p=128)[:, :, ts(ct)])
            p.s("vector", lambda e: e.tensor_copy(wstripb[:], wstrip[:]))
            for kt2 in range(NC):
                p.s("tensor", lambda e, kt2=kt2: e.matmul(psM[:], wstripb[:, kt2, :], cr[:, kt2, :], start=(kt2 == 0), stop=(kt2 == NC - 1)))
            p.s("scalar", lambda e: e.activation(sg[:], psM[:], AF.Sigmoid))
            p.s("vector", lambda e, ct=ct: e.tensor_tensor(ev[:], sg[:], kvacc[:, ct, :], ALU.mult))
            for tt in range(4):
                p.s("tensor", lambda e, tt=tt: e.transpose(psTT[:, ts(tt)], ev[:, ts(tt)], cm.identf[:]))
            p.s("scalar", lambda e: e.activation(evT[:], psTT[:], AF.Copy))
            for tt in range(4):
                i = ch * 4 + tt
                p.sdma(rk_t[i][:, ts(ct)], evT[:, ts(tt)])
    cx.pop_scope()


def emit_l0ffn(cx, cm, aps, x_dram, ffot_out, I=5632):
    p = cx.p
    cx.push_scope()
    NI = I // 128
    TCH = 512
    NCH = TT // TCH
    x_t = x_dram.rearrange("(n p) c -> n p c", p=128)
    ldf = cx.sbuf("ff_ldf", [128, C], F32)
    bfa = cx.sbuf("ff_bfa", [128, C], BF16)
    psT = cx.psum("ff_psT", [128, 512], BF16)
    xc = cx.sbuf("ff_xc", [128, NC, TCH], BF16)
    wstrip = cx.sbuf("ff_ws", [128, NC, 128], F32)
    wstripb = cx.sbuf("ff_wsb", [128, NC, 128], BF16)
    w2strip = cx.sbuf("ff_w2s", [128, NI, 128], F32)
    w2stripb = cx.sbuf("ff_w2sb", [128, NI, 128], BF16)
    hdn = cx.sbuf("ff_hdn", [128, NI, TCH], BF16)
    scl = cx.sbuf("ff_scl", [128, 3, max(NI, NC)], F32)
    psM = cx.psum("ff_psM", [128, TCH], F32)
    ev = cx.sbuf("ff_ev", [128, TCH], F32)
    ev2 = cx.sbuf("ff_ev2", [128, TCH], F32)
    psTT = cx.psum("ff_psTT", [128, TCH], F32)
    evT = cx.sbuf("ff_evT", [128, TCH], F32)
    ff_t = ffot_out.rearrange("(n p) c -> n p c", p=128)

    p.sdma(scl[:, 0, 0:NI], aps["gs"])
    p.sdma(scl[:, 1, 0:NI], aps["us"])
    p.sdma(scl[:, 2, 0:NC], aps["ds"])
    for ch in range(NCH):
        for tti in range(TCH // 128):
            i = ch * (TCH // 128) + tti
            p.sdma(ldf[:], x_t[i])
            p.s("vector", lambda e: e.tensor_copy(bfa[:], ldf[:]))
            for c4 in range(NC // 4):
                for cc in range(4):
                    c = c4 * 4 + cc
                    p.s("tensor", lambda e, c=c, cc=cc: e.transpose(psT[:, ts(cc)], bfa[:, ts(c)], cm.idb[:]))
                p.s("scalar", lambda e, c4=c4, tti=tti: e.activation(
                    xc[:, c4*4:(c4+1)*4, tti*128:(tti+1)*128],
                    psT[:].rearrange("p (a b) -> p a b", a=4), AF.Copy))
        for it in range(NI):
            p.sdma(wstrip[:], aps["gateT"].rearrange("(n p) i -> p n i", p=128)[:, :, ts(it)])
            p.s("vector", lambda e: e.tensor_copy(wstripb[:], wstrip[:]))
            for kt2 in range(NC):
                p.s("tensor", lambda e, kt2=kt2: e.matmul(psM[:], wstripb[:, kt2, :], xc[:, kt2, :], start=(kt2 == 0), stop=(kt2 == NC - 1)))
            p.s("vector", lambda e, it=it: e.tensor_scalar(ev[:], psM[:], scl[:, 0, it:it+1], None, ALU.mult))
            p.s("scalar", lambda e: e.activation(ev[:], ev[:], AF.Silu))
            p.sdma(wstrip[:], aps["upT"].rearrange("(n p) i -> p n i", p=128)[:, :, ts(it)])
            p.s("vector", lambda e: e.tensor_copy(wstripb[:], wstrip[:]))
            for kt2 in range(NC):
                p.s("tensor", lambda e, kt2=kt2: e.matmul(psM[:], wstripb[:, kt2, :], xc[:, kt2, :], start=(kt2 == 0), stop=(kt2 == NC - 1)))
            p.s("vector", lambda e, it=it: e.tensor_scalar(ev2[:], psM[:], scl[:, 1, it:it+1], None, ALU.mult))
            p.s("vector", lambda e, it=it: e.tensor_tensor(hdn[:, it, :], ev[:], ev2[:], ALU.mult))
        for ct in range(NC):
            p.sdma(w2strip[:], aps["downT"].rearrange("(n p) c -> p n c", p=128)[:, :, ts(ct)])
            p.s("vector", lambda e: e.tensor_copy(w2stripb[:], w2strip[:]))
            for kt2 in range(NI):
                p.s("tensor", lambda e, kt2=kt2: e.matmul(psM[:], w2stripb[:, kt2, :], hdn[:, kt2, :], start=(kt2 == 0), stop=(kt2 == NI - 1)))
            p.s("vector", lambda e, ct=ct: e.tensor_scalar(ev[:], psM[:], scl[:, 2, ct:ct+1], None, ALU.mult))
            for tt in range(4):
                p.s("tensor", lambda e, tt=tt: e.transpose(psTT[:, ts(tt)], ev[:, ts(tt)], cm.identf[:]))
            p.s("scalar", lambda e: e.activation(evT[:], psTT[:], AF.Copy))
            for tt in range(4):
                i = ch * 4 + tt
                p.sdma(ff_t[i][:, ts(ct)], evT[:, ts(tt)])
    cx.pop_scope()


def emit_proj3(cx, cm, aps, x_dram, q_out, kin_out, vin_out):
    p = cx.p
    cx.push_scope()
    TCH = 512
    NCH = TT // TCH
    x_t = x_dram.rearrange("(n p) c -> n p c", p=128)
    ldf = cx.sbuf("p3_ldf", [128, C], F32)
    bfa = cx.sbuf("p3_bfa", [128, C], BF16)
    psT = cx.psum("p3_psT", [128, 512], BF16)
    xc = cx.sbuf("p3_xc", [128, NC, TCH], BF16)
    wstrip = cx.sbuf("p3_ws", [128, NC, 128], F32)
    wstripb = cx.sbuf("p3_wsb", [128, NC, 128], BF16)
    psM = cx.psum("p3_psM", [128, TCH], F32)
    ev = cx.sbuf("p3_ev", [128, TCH], F32)

    for ch in range(NCH):
        for tti in range(TCH // 128):
            i = ch * (TCH // 128) + tti
            p.sdma(ldf[:], x_t[i])
            p.s("vector", lambda e: e.tensor_copy(bfa[:], ldf[:]))
            for c4 in range(NC // 4):
                for cc in range(4):
                    c = c4 * 4 + cc
                    p.s("tensor", lambda e, c=c, cc=cc: e.transpose(psT[:, ts(cc)], bfa[:, ts(c)], cm.idb[:]))
                p.s("scalar", lambda e, c4=c4, tti=tti: e.activation(
                    xc[:, c4*4:(c4+1)*4, tti*128:(tti+1)*128],
                    psT[:].rearrange("p (a b) -> p a b", a=4), AF.Copy))
        for wn, outd in [("Wq1", q_out), ("Wk1", kin_out), ("Wv1", vin_out)]:
            for dt in range(NC):
                p.sdma(wstrip[:], aps[wn].rearrange("(k p) d -> p k d", p=128)[:, :, ts(dt)])
                p.s("vector", lambda e: e.tensor_copy(wstripb[:], wstrip[:]))
                for kt2 in range(NC):
                    p.s("tensor", lambda e, kt2=kt2: e.matmul(psM[:], wstripb[:, kt2, :], xc[:, kt2, :], start=(kt2 == 0), stop=(kt2 == NC - 1)))
                p.s("scalar", lambda e: e.activation(ev[:], psM[:], AF.Copy))
                p.sdma(outd.rearrange("(n p) t -> n p t", p=128)[dt, :, ch*TCH:(ch+1)*TCH], ev[:])
    cx.pop_scope()


def emit_attn(cx, cm, aps, q_dram, kg_dram, vg_dram, o_out, TF=4096, HEADS=16, QG=4):
    p = cx.p
    cx.push_scope()
    NKB = TF // 128
    NQG = NT // QG
    W = QG * 128
    Qr = q_dram.rearrange("(h p) t -> h p t", p=128)
    Kr = kg_dram.rearrange("(two hh p) (j i) -> hh p j two i", two=2, p=128, i=128)
    Vr = vg_dram.rearrange("(two hh p) (j i) -> hh p j two i", two=2, p=128, i=128)
    Or = o_out.rearrange("(h p) t -> h p t", p=128)

    ldf = cx.sbuf("at_ldf", [128, TF], F32)
    vhb = cx.sbuf("at_vhb", [128, TF], BF16)
    vt = cx.sbuf("at_vt", [128, NKB, 128], BF16)
    khb = cx.sbuf("at_khb", [128, TF], BF16)
    qhb = cx.sbuf("at_qhb", [128, TT], BF16)
    m1s = cx.sbuf("at_m1s", [128, 8, W], F32)
    m2s = cx.sbuf("at_m2s", [128, 8, W], BF16)
    m2f = cx.sbuf("at_m2f", [128, 8, W], F32)
    selb = cx.sbuf("at_selb", [32, NT], BF16)
    self_ = cx.sbuf("at_self", [32, NT], F32)
    cnts = cx.sbuf("at_cnts", [1, NT], F32)
    onesk = cx.sbuf("at_onesk", [128, 1], BF16)
    onesr = cx.sbuf("at_onesr", [1, 128], BF16)
    bsum = cx.sbuf("at_bsum", [128, NKB], F32)
    bsumtb = cx.sbuf("at_bsumtb", [32, 128], BF16)
    sufq = cx.sbuf("at_sufq", [128, NT], F32)
    sufev = cx.sbuf("at_sufev", [16, 128], F32)
    pexp = cx.sbuf("at_pexp", [128, W], BF16)
    den_s = cx.sbuf("at_dens", [1, W], F32)
    denb = cx.sbuf("at_denb", [1, W], BF16)
    obuf = cx.sbuf("at_obuf", [128, W], F32)
    psS = cx.psum("at_psS", [128, W], F32)
    psO = cx.psum("at_psO", [128, W], F32)
    psD = cx.psum("at_psD", [1, W], F32)
    psT2 = cx.psum("at_psT2", [128, 128], BF16)
    psTf = cx.psum("at_psTf", [32, 128], F32)
    psSuf = cx.psum("at_psSuf", [16, 128], F32)
    psSufT = cx.psum("at_psSufT", [128, 16], F32)
    psB = cx.psum("at_psB", [128, W], F32)

    p.s("vector", lambda e: e.memset(onesk[:], 1.0))
    p.s("vector", lambda e: e.memset(onesr[:], 1.0))
    p.sdma(m1s[:], aps["m1"].rearrange("m p w -> p m w"))
    p.sdma(m2f[:], aps["m2"].rearrange("m p w -> p m w"))
    p.s("vector", lambda e: e.tensor_copy(m2s[:], m2f[:]))
    p.sdma(self_[:], aps["Sel"])
    p.s("vector", lambda e: e.tensor_copy(selb[:], self_[:]))
    p.sdma(cnts[:], aps["cnt"])
    ldf4 = ldf.ap().rearrange("p (j two i) -> p j two i", two=2, i=128)
    for h in range(HEADS):
        p.sdma(ldf4, Kr[h])
        p.s("vector", lambda e: e.tensor_copy(khb[:], ldf[:]))
        p.sdma(ldf[:, 0:TT], Qr[h])
        p.s("vector", lambda e: e.tensor_scalar(qhb[:], ldf[:, 0:TT], 1.0 / 64.0, None, ALU.mult))
        p.sdma(ldf4, Vr[h])
        p.s("vector", lambda e: e.tensor_copy(vhb[:], ldf[:]))
        p.s("vector", lambda e: e.reduce_sum(bsum[:], ldf.ap().rearrange("p (a b) -> p a b", a=NKB), axis=AX.X))
        for kb in range(NKB):
            p.s("tensor", lambda e, kb=kb: e.transpose(psT2[:], vhb[:, ts(kb)], cm.idb[:]))
            p.s("scalar", lambda e, kb=kb: e.activation(vt[:, kb, :], psT2[:], AF.Copy))
        p.s("tensor", lambda e: e.transpose(psTf[:], bsum[:], cm.identf[:]))
        p.s("vector", lambda e: e.tensor_copy(bsumtb[:], psTf[:]))
        p.s("tensor", lambda e: e.matmul(psSuf[:], selb[:], bsumtb[:], start=True, stop=True))
        p.s("scalar", lambda e: e.activation(sufev[:], psSuf[:], AF.Copy))
        p.s("tensor", lambda e: e.transpose(psSufT[:], sufev[:], cm.identf[0:16, 0:16]))
        p.s("scalar", lambda e: e.activation(sufq[:, 0:NT], psSufT[:], AF.Copy))
        for gi in range(NQG):
            qj0 = gi * QG
            qcols = slice(qj0 * 128, qj0 * 128 + W)
            nkb_c = 2 * qj0
            total_kb = nkb_c + 8
            for kb in range(total_kb):
                mi = kb - nkb_c
                p.s("tensor", lambda e, kb=kb, qcols=qcols: e.matmul(psS[:], khb[:, ts(kb)], qhb[:, qcols], start=True, stop=True))
                if mi >= 0:
                    p.s("vector", lambda e, mi=mi: e.tensor_tensor(psS[:], psS[:], m1s[:, mi, :], ALU.mult))
                p.s("scalar", lambda e: e.activation(pexp[:], psS[:], AF.Exp))
                if mi >= 0:
                    p.s("vector", lambda e, mi=mi: e.tensor_tensor(pexp[:], pexp[:], m2s[:, mi, :], ALU.mult))
                p.s("tensor", lambda e, kb=kb, total_kb=total_kb: e.matmul(psO[:], vt[:, kb, :], pexp[:], start=(kb == 0), stop=(kb == total_kb - 1), skip_group_check=True))
                p.s("tensor", lambda e, kb=kb, total_kb=total_kb: e.matmul(psD[:], onesk[:], pexp[:], start=(kb == 0), stop=(kb == total_kb - 1), skip_group_check=True))
            p.s("vector", lambda e: e.tensor_copy(den_s[:], psD[:]))
            for t in range(QG):
                p.s("vector", lambda e, t=t, qj0=qj0: e.tensor_scalar(den_s[:, ts(t)], den_s[:, ts(t)], cnts[:, qj0 + t: qj0 + t + 1], None, ALU.add))
            p.s("scalar", lambda e: e.activation(den_s[:], den_s[:], AF.Ln))
            p.s("scalar", lambda e: e.activation(den_s[:], den_s[:], AF.Exp, scale=-1.0))
            p.s("vector", lambda e: e.tensor_copy(denb[:], den_s[:]))
            p.s("tensor", lambda e: e.matmul(psB[:], onesr[:], denb[:], start=True, stop=True))
            for t in range(QG):
                p.s("vector", lambda e, t=t, qj0=qj0: e.tensor_scalar(obuf[:, ts(t)], psO[:, ts(t)], sufq[:, qj0 + t: qj0 + t + 1], None, ALU.add))
            p.s("vector", lambda e: e.tensor_tensor(obuf[:], obuf[:], psB[:], ALU.mult))
            p.sdma(Or[h][:, qcols], obuf[:])
    cx.pop_scope()


def emit_round_chain(p, x, s127, z, m, oi):
    p.s("vector", lambda e: e.tensor_scalar(z[:], x[:], s127, 128.5, ALU.mult, ALU.add))
    p.s("vector", lambda e: e.tensor_copy(oi[:], z[:]))
    p.s("vector", lambda e: e.tensor_copy(m[:], oi[:]))
    p.s("vector", lambda e: e.tensor_tensor(z[:], m[:], z[:], ALU.is_gt))
    p.s("vector", lambda e: e.tensor_tensor(m[:], m[:], z[:], ALU.subtract))


def emit_quant(p, out_bf, x, s127, z, m, oi):
    emit_round_chain(p, x, s127, z, m, oi)
    p.s("vector", lambda e: e.tensor_scalar(out_bf[:], m[:], -128.0, None, ALU.add))


def emit_wquant(p, out_bf, w, swv, z, m, oi):
    emit_round_chain(p, w, swv, z, m, oi)
    p.s("vector", lambda e: e.tensor_scalar(m[:], m[:], -128.0, None, ALU.add))
    p.s("vector", lambda e: e.tensor_scalar(out_bf[:], m[:], 1.0, -1.0, ALU.min, ALU.max))


def emit_bitffn(cx, cm, aps, x5_dram, h1_dram, w1t_dram, w2t_dram, x6_out, H=8192):
    emit_bitffn_p1(cx, cm, aps, x5_dram, h1_dram, w1t_dram, H=H)
    emit_bitffn_p2(cx, cm, aps, x5_dram, h1_dram, w2t_dram, x6_out, H=H)


def emit_bitffn_p1(cx, cm, aps, x5_dram, h1_dram, w1t_dram, H=8192):
    p = cx.p
    NH = H // 128
    cx.push_scope()
    x_t = x5_dram.rearrange("(n p) c -> n p c", p=128)
    w_r = aps["W1T"].rearrange("(n p) h -> n p h", p=128)
    wt_r = w1t_dram.rearrange("(n p) h -> n p h", p=128)
    h1_t = h1_dram.rearrange("(n p) h -> n p h", p=128)
    WCH = 2048
    ldw = cx.sbuf("b1_ldw", [128, WCH], F32)
    zb = cx.sbuf("b1_zb", [128, WCH], F32)
    mb = cx.sbuf("b1_mb", [128, WCH], F32)
    oib = cx.sbuf("b1_oib", [128, WCH], I32)
    wtb = cx.sbuf("b1_wtb", [128, WCH], BF16)
    acc = cx.sbuf("b1_acc", [128, 2], F32)
    onescol = cx.sbuf("b1_onesc", [128, 1], F32)
    onesrow = cx.sbuf("b1_onesr", [1, 128], F32)
    sw1 = cx.sbuf("b1_sw1", [1, 2], F32)
    swv = cx.sbuf("b1_swv", [128, 1], F32)
    mnv = cx.sbuf("b1_mnv", [128, 1], F32)
    fct = cx.sbuf("b1_fct", [128, NT], F32)
    psS1 = cx.psum("b1_psS1", [1, 1], F32)
    psRep = cx.psum("b1_psRep", [128, 1], F32)
    ldf = cx.sbuf("b1_ldf", [128, C], F32)
    xqb = cx.sbuf("b1_xqb", [128, C], BF16)
    xc = cx.sbuf("b1_xc", [128, NC, TT], BF16)
    st = cx.sbuf("b1_st", [128, 8], F32)
    psT = cx.psum("b1_psT", [128, 512], BF16)
    wblk = cx.sbuf("b1_wblk", [128, NC, 512], BF16)
    psM = cx.psum("b1_psM", [128, 512], F32)
    ev = cx.sbuf("b1_ev", [128, 512], F32)

    p.s("vector", lambda e: e.memset(onescol[:], 1.0))
    p.s("vector", lambda e: e.memset(onesrow[:], 1.0))
    p.s("vector", lambda e: e.memset(acc[:, 0:1], 0.0))
    for ct in range(NC):
        for hch in range(H // WCH):
            p.sdma(ldw[:], w_r[ct][:, hch*WCH:(hch+1)*WCH])
            p.s("vector", lambda e: e.reduce_sum(acc[:, 1:2], ldw[:], axis=AX.X, apply_absolute_value=True))
            p.s("vector", lambda e: e.tensor_tensor(acc[:, 0:1], acc[:, 0:1], acc[:, 1:2], ALU.add))
    p.s("tensor", lambda e: e.matmul(psS1[:], acc[:, 0:1], onescol[:], start=True, stop=True))
    p.s("vector", lambda e: e.tensor_scalar(sw1[:, 0:1], psS1[:], 1.0 / (C * H), 1e-5, ALU.mult, ALU.max))
    p.s("vector", lambda e: e.reciprocal(sw1[:, 1:2], sw1[:, 0:1]))
    p.s("tensor", lambda e: e.matmul(psRep[:], onesrow[:], sw1[:, 1:2], start=True, stop=True))
    p.s("scalar", lambda e: e.activation(swv[:], psRep[:], AF.Copy))
    p.s("tensor", lambda e: e.matmul(psRep[:], onesrow[:], sw1[:, 0:1], start=True, stop=True))
    p.s("scalar", lambda e: e.activation(mnv[:], psRep[:], AF.Copy))
    for ct in range(NC):
        for hch in range(H // WCH):
            p.sdma(ldw[:], w_r[ct][:, hch*WCH:(hch+1)*WCH])
            emit_wquant(p, wtb, ldw, swv[:, 0:1], zb, mb, oib)
            p.sdma(wt_r[ct][:, hch*WCH:(hch+1)*WCH], wtb[:])
    for i in range(NT):
        p.sdma(ldf[:], x_t[i])
        p.s("vector", lambda e: e.reduce_max(st[:, 0:1], ldf[:], axis=AX.X, apply_absolute_value=True))
        p.s("vector", lambda e: e.tensor_scalar(st[:, 0:1], st[:, 0:1], 1e-5, None, ALU.max))
        p.s("vector", lambda e: e.reciprocal(st[:, 1:2], st[:, 0:1]))
        p.s("vector", lambda e: e.tensor_scalar(st[:, 2:3], st[:, 1:2], 127.0, None, ALU.mult))
        emit_quant(p, xqb, ldf, st[:, 2:3], zb[:, 0:C], mb[:, 0:C], oib[:, 0:C])
        p.s("vector", lambda e, i=i: e.tensor_scalar(fct[:, i:i+1], st[:, 0:1], 1.0 / 127.0, None, ALU.mult))
        p.s("vector", lambda e, i=i: e.tensor_tensor(fct[:, i:i+1], fct[:, i:i+1], mnv[:], ALU.mult))
        for c4 in range(NC // 4):
            for cc in range(4):
                c = c4 * 4 + cc
                p.s("tensor", lambda e, c=c, cc=cc: e.transpose(psT[:, ts(cc)], xqb[:, ts(c)], cm.idb[:]))
            p.s("scalar", lambda e, c4=c4, i=i: e.activation(
                xc[:, c4*4:(c4+1)*4, i*128:(i+1)*128],
                psT[:].rearrange("p (a b) -> p a b", a=4), AF.Copy))
    for hcki in range(H // 512):
        p.sdma(wblk[:], wt_r[:, :, hcki*512:(hcki+1)*512].rearrange("n p h -> p n h"))
        for i in range(NT):
            for kt2 in range(NC):
                p.s("tensor", lambda e, kt2=kt2, i=i: e.matmul(psM[:], xc[:, kt2, i*128:(i+1)*128], wblk[:, kt2, :], start=(kt2 == 0), stop=(kt2 == NC - 1)))
            p.s("vector", lambda e, i=i: e.tensor_scalar(ev[:], psM[:], fct[:, i:i+1], None, ALU.mult))
            p.s("scalar", lambda e: e.activation(ev[:], ev[:], AF.Silu))
            p.sdma(h1_t[i][:, hcki*512:(hcki+1)*512], ev[:])
    cx.pop_scope()


def emit_bitffn_p2(cx, cm, aps, x5_dram, h1_dram, w2t_dram, x6_out, H=8192):
    p = cx.p
    NH = H // 128
    cx.push_scope()
    x_t = x5_dram.rearrange("(n p) c -> n p c", p=128)
    h_t = h1_dram.rearrange("(n p) h -> n p h", p=128)
    w2_r = aps["W2T"].rearrange("(n p) c -> n p c", p=128)
    wt2_r = w2t_dram.rearrange("(n p) c -> n p c", p=128)
    x6_t = x6_out.rearrange("(n p) c -> n p c", p=128)
    QT = 2
    WCH = 2048
    ldw = cx.sbuf("b2_ldw", [128, C], F32)
    zb = cx.sbuf("b2_zb", [128, WCH], F32)
    mb = cx.sbuf("b2_mb", [128, WCH], F32)
    oib = cx.sbuf("b2_oib", [128, WCH], I32)
    wtb = cx.sbuf("b2_wtb", [128, C], BF16)
    acc = cx.sbuf("b2_acc", [128, 2], F32)
    onescol = cx.sbuf("b2_onesc", [128, 1], F32)
    onesrow = cx.sbuf("b2_onesr", [1, 128], F32)
    sw2 = cx.sbuf("b2_sw2", [1, 2], F32)
    swv = cx.sbuf("b2_swv", [128, 1], F32)
    mnv = cx.sbuf("b2_mnv", [128, 1], F32)
    psS1 = cx.psum("b2_psS1", [1, 1], F32)
    psRep = cx.psum("b2_psRep", [128, 1], F32)
    ldh = cx.sbuf("b2_ldh", [128, H], F32)
    xqb = cx.sbuf("b2_xqb", [128, H], BF16)
    xc2 = cx.sbuf("b2_xc2", [128, NH, QT * 128], BF16)
    fct = cx.sbuf("b2_fct", [128, QT], F32)
    st = cx.sbuf("b2_st", [128, 8], F32)
    psT = cx.psum("b2_psT", [128, 512], BF16)
    wblk = cx.sbuf("b2_wblk", [128, 512], BF16)
    psM = [cx.psum(f"b2_psM{t}", [128, 512], F32) for t in range(QT)]
    obuf = cx.sbuf("b2_obuf", [128, QT, C], F32)
    xs = cx.sbuf("b2_xs", [128, C], F32)
    sq = cx.sbuf("b2_sq", [128, C], F32)
    stl = cx.sbuf("b2_stl", [128, 8], F32)

    p.s("vector", lambda e: e.memset(onescol[:], 1.0))
    p.s("vector", lambda e: e.memset(onesrow[:], 1.0))
    p.s("vector", lambda e: e.memset(acc[:, 0:1], 0.0))
    for ht in range(NH):
        p.sdma(ldw[:], w2_r[ht])
        p.s("vector", lambda e: e.reduce_sum(acc[:, 1:2], ldw[:], axis=AX.X, apply_absolute_value=True))
        p.s("vector", lambda e: e.tensor_tensor(acc[:, 0:1], acc[:, 0:1], acc[:, 1:2], ALU.add))
    p.s("tensor", lambda e: e.matmul(psS1[:], acc[:, 0:1], onescol[:], start=True, stop=True))
    p.s("vector", lambda e: e.tensor_scalar(sw2[:, 0:1], psS1[:], 1.0 / (C * H), 1e-5, ALU.mult, ALU.max))
    p.s("vector", lambda e: e.reciprocal(sw2[:, 1:2], sw2[:, 0:1]))
    p.s("tensor", lambda e: e.matmul(psRep[:], onesrow[:], sw2[:, 1:2], start=True, stop=True))
    p.s("scalar", lambda e: e.activation(swv[:], psRep[:], AF.Copy))
    p.s("tensor", lambda e: e.matmul(psRep[:], onesrow[:], sw2[:, 0:1], start=True, stop=True))
    p.s("scalar", lambda e: e.activation(mnv[:], psRep[:], AF.Copy))
    for ht in range(NH):
        p.sdma(ldw[:], w2_r[ht])
        emit_wquant(p, wtb, ldw, swv[:, 0:1], zb[:, 0:C], mb[:, 0:C], oib[:, 0:C])
        p.sdma(wt2_r[ht], wtb[:])
    for qu in range(NT // QT):
        for t in range(QT):
            i = qu * QT + t
            p.sdma(ldh[:], h_t[i])
            p.s("vector", lambda e: e.reduce_max(st[:, 0:1], ldh[:], axis=AX.X, apply_absolute_value=True))
            p.s("vector", lambda e: e.tensor_scalar(st[:, 0:1], st[:, 0:1], 1e-5, None, ALU.max))
            p.s("vector", lambda e: e.reciprocal(st[:, 1:2], st[:, 0:1]))
            p.s("vector", lambda e: e.tensor_scalar(st[:, 2:3], st[:, 1:2], 127.0, None, ALU.mult))
            for hch in range(H // WCH):
                sl2 = slice(hch*WCH, (hch+1)*WCH)
                emit_quant(p, xqb[:, sl2], ldh[:, sl2], st[:, 2:3], zb, mb, oib)
            p.s("vector", lambda e, t=t: e.tensor_scalar(fct[:, t:t+1], st[:, 0:1], 1.0 / 127.0, None, ALU.mult))
            p.s("vector", lambda e, t=t: e.tensor_tensor(fct[:, t:t+1], fct[:, t:t+1], mnv[:], ALU.mult))
            for c4 in range(NH // 4):
                for cc in range(4):
                    c = c4 * 4 + cc
                    p.s("tensor", lambda e, c=c, cc=cc: e.transpose(psT[:, ts(cc)], xqb[:, ts(c)], cm.idb[:]))
                p.s("scalar", lambda e, c4=c4, t=t: e.activation(
                    xc2[:, c4*4:(c4+1)*4, t*128:(t+1)*128],
                    psT[:].rearrange("p (a b) -> p a b", a=4), AF.Copy))
        for cck in range(C // 512):
            for kt2 in range(NH):
                p.sdma(wblk[:], wt2_r[kt2][:, cck*512:(cck+1)*512])
                for t in range(QT):
                    p.s("tensor", lambda e, kt2=kt2, t=t: e.matmul(psM[t][:], xc2[:, kt2, t*128:(t+1)*128], wblk[:], start=(kt2 == 0), stop=(kt2 == NH - 1), skip_group_check=True))
            for t in range(QT):
                p.s("vector", lambda e, t=t, cck=cck: e.tensor_scalar(obuf[:, t, cck*512:(cck+1)*512], psM[t][:], fct[:, t:t+1], None, ALU.mult))
        for t in range(QT):
            i = qu * QT + t
            p.sdma(xs[:], x_t[i])
            p.s("vector", lambda e, t=t: e.tensor_tensor(xs[:], xs[:], obuf[:, t, :], ALU.add))
            emit_ln_stats_tile(p, xs, sq, stl)
            p.s("vector", lambda e: e.tensor_scalar(xs[:], xs[:], stl[:, 2:3], stl[:, 7:8], ALU.subtract, ALU.mult))
            p.sdma(x6_t[i], xs[:])
    cx.pop_scope()


def build_fused(H=8192, I=5632):
    nc = bass.Bass("TRN2", target_bir_lowering=False, debug=False)
    ap = {}
    ins_spec = [
        ("x", [TT, C]), ("ident", [128, 128]), ("pv", [1, 2]), ("zmask", [1, NT]),
        ("ln1w", [128, C]), ("ln1b", [128, C]), ("ln2w", [128, C]), ("ln2b", [128, C]),
        ("tmk", [128, NC]), ("tmv", [128, NC]), ("tmr", [128, NC]),
        ("WkT", [C, C]), ("WvT", [C, C]), ("WrT", [C, C]), ("WoT", [C, C]),
        ("td8", [128, 8]), ("tf8", [128, 8]),
        ("cmk", [128, NC]), ("cmr", [128, NC]),
        ("WckT", [C, H]), ("WcvT", [H, C]), ("WcrT", [C, C]),
        ("gateT", [C, I]), ("upT", [C, I]), ("downT", [I, C]),
        ("gs", [128, I // 128]), ("us", [128, I // 128]), ("ds", [128, NC]),
        ("Wq1", [C, C]), ("Wk1", [C, C]), ("Wv1", [C, C]), ("Wo1", [C, C]),
        ("m1", [8, 128, 512]), ("m2", [8, 128, 512]), ("Sel", [32, NT]), ("cnt", [1, NT]),
        ("W1T", [C, H]), ("W2T", [H, C]),
    ]
    for n, sh in ins_spec:
        ap[n] = nc.dram_tensor(n, sh, F32, kind="ExternalInput").ap()
    out = nc.dram_tensor("out", [TT, C], F32, kind="ExternalOutput").ap()
    dr = {}
    for n, sh, dt in [
        ("xn", [TT, C], F32), ("halo1_in", [16, C], F32), ("halo1_out", [32, C], F32),
        ("k_c", [C, TT], F32), ("v_c", [C, TT], F32), ("sr_c", [C, TT], F32),
        ("kga", [2 * C, TT], F32), ("vga", [2 * C, TT], F32),
        ("y_my", [1024, 2 * TT], F32), ("yga", [C, 2 * TT], F32),
        ("x2", [TT, C], F32), ("xn2", [TT, C], F32),
        ("halo2_in", [16, C], F32), ("halo2_out", [32, C], F32),
        ("rkv_t", [TT, C], F32), ("x3", [TT, C], F32), ("ffo_t", [TT, C], F32),
        ("x4", [TT, C], F32), ("Q_c", [C, TT], F32),
        ("K_in", [C, TT], F32), ("V_in", [C, TT], F32),
        ("KG", [2 * C, TT], F32), ("VG", [2 * C, TT], F32),
        ("O_c", [C, TT], F32), ("x5pre", [TT, C], F32), ("x5", [TT, C], F32),
        ("h1", [TT, H], F32), ("w1t", [C, H], BF16), ("w2t", [H, C], BF16),
    ]:
        dr[n] = nc.dram_tensor("i_" + n, sh, dt).ap()

    cx = Ctx(nc)
    p = cx.p
    ccsem_cm = nc.semaphore("ccsem")
    ccsem = ccsem_cm.__enter__()
    ccount = [0]
    cm = Common(cx, ap["ident"])
    pvs = cx.sbuf("cm_pv", [1, 2], F32)
    zms = cx.sbuf("cm_zmask", [1, NT], F32)
    onesr_f = cx.sbuf("cm_onesrf", [1, 128], F32)
    pv128 = cx.sbuf("cm_pv128", [128, 2], F32)
    p.sdma(pvs[:], ap["pv"])
    p.sdma(zms[:], ap["zmask"])
    cx.push_scope()
    psPv = cx.psum("cm_psPv", [128, 2], F32)
    p.s("vector", lambda e: e.memset(onesr_f[:], 1.0))
    p.s("tensor", lambda e: e.matmul(psPv[:], onesr_f[:], pvs[:], start=True, stop=True))
    p.s("scalar", lambda e: e.activation(pv128[:], psPv[:], AF.Copy))
    cx.pop_scope()
    sv_halo = pvs[0:1, 0:1]
    sv_par = pv128[:, 1:2]

    p.stage = "ln1"
    emit_ln(cx, "ln1_", ap["x"], dr["xn"], ap["ln1w"], ap["ln1b"], halo_out=dr["halo1_in"])
    emit_cc(cx, "AllGather", dr["halo1_in"], dr["halo1_out"], ccsem, ccount)
    halo1 = dr["halo1_out"].rearrange("(two n) c -> two n c", two=2)
    p.stage = "mix_kvr"
    emit_mix_kvr(cx, cm, ap, dr["xn"], halo1, sv_halo, zms, dr["k_c"], dr["v_c"], dr["sr_c"])
    emit_cc(cx, "AllGather", dr["k_c"], dr["kga"], ccsem, ccount)
    emit_cc(cx, "AllGather", dr["v_c"], dr["vga"], ccsem, ccount)
    p.stage = "wkv"
    emit_wkv(cx, ap, dr["kga"], dr["vga"], dr["y_my"], sv_par)
    emit_cc(cx, "AllGather", dr["y_my"], dr["yga"], ccsem, ccount)
    p.stage = "wo1"
    emit_wo_res(cx, cm, ap["WoT"], dr["yga"], dr["sr_c"], ap["x"], dr["x2"],
                use_sr=True, att_is_gather=True, svec_sb=sv_par)
    p.stage = "ln2"
    emit_ln(cx, "ln2_", dr["x2"], dr["xn2"], ap["ln2w"], ap["ln2b"], halo_out=dr["halo2_in"])
    emit_cc(cx, "AllGather", dr["halo2_in"], dr["halo2_out"], ccsem, ccount)
    halo2 = dr["halo2_out"].rearrange("(two n) c -> two n c", two=2)
    p.stage = "cmix"
    emit_cmix(cx, cm, ap, dr["xn2"], halo2, sv_halo, zms, dr["rkv_t"], H=H)
    p.stage = "ln3"
    emit_ln(cx, "ln3_", ap["x"], dr["x3"], r_ts=[dr["x2"], dr["rkv_t"]])
    p.stage = "l0ffn"
    emit_l0ffn(cx, cm, ap, dr["x3"], dr["ffo_t"], I=I)
    p.stage = "ln4"
    emit_ln(cx, "ln4_", dr["x3"], dr["x4"], r_ts=[dr["ffo_t"]])
    p.stage = "proj3"
    emit_proj3(cx, cm, ap, dr["x4"], dr["Q_c"], dr["K_in"], dr["V_in"])
    emit_cc(cx, "AllGather", dr["K_in"], dr["KG"], ccsem, ccount)
    emit_cc(cx, "AllGather", dr["V_in"], dr["VG"], ccsem, ccount)
    p.stage = "attn"
    emit_attn(cx, cm, ap, dr["Q_c"], dr["KG"], dr["VG"], dr["O_c"])
    p.stage = "wo2"
    emit_wo_res(cx, cm, ap["Wo1"], dr["O_c"], None, dr["x4"], dr["x5pre"], use_sr=False, pfx="wo2_")
    p.stage = "ln5"
    emit_ln(cx, "ln5_", dr["x5pre"], dr["x5"])
    p.stage = "bitffn"
    emit_bitffn(cx, cm, ap, dr["x5"], dr["h1"], dr["w1t"], dr["w2t"], out, H=H)
    nc_ret = cx.close()
    ccsem_cm.__exit__(None, None, None)
    return nc_ret


import concourse.bass_utils as bass_utils


def shard_tokens(x):
    B, T, F = x.shape
    xb = x.reshape(B, 32, 128, F)
    out = []
    for p2 in range(B):
        for r in range(2):
            out.append(np.ascontiguousarray(xb[p2, r::2].reshape(2048, F)))
    return out


def unshard_tokens(shards):
    F = shards[0].shape[-1]
    out = np.empty((4, 32, 128, F), np.float32)
    for p2 in range(4):
        for r in range(2):
            out[p2, r::2] = np.asarray(shards[2 * p2 + r], dtype=np.float32).reshape(16, 128, F)
    return out.reshape(4, 4096, F)


def rep128(v):
    return np.ascontiguousarray(np.broadcast_to(np.asarray(v, np.float32)[None, :], (128, len(v))))


def cvec(v, n=16):
    return np.ascontiguousarray(np.asarray(v, np.float32).reshape(n, 128).T)


_PROGS = {}


def kernel(**ins):
    T_ = lambda a: np.ascontiguousarray(np.asarray(a, np.float32).T)
    A_ = lambda a: np.ascontiguousarray(np.asarray(a, np.float32))
    ident = np.eye(128, dtype=np.float32)
    x_s = shard_tokens(np.asarray(ins["L0_attn_x"], np.float32))
    tri = (np.arange(128)[:, None] <= np.arange(128)[None, :]).astype(np.float32)

    def masks(rr):
        m1 = np.zeros((8, 128, 512), np.float32)
        m2 = np.zeros((8, 128, 512), np.float32)
        for mi in range(8):
            for t in range(4):
                g2 = 2 * t + rr
                if mi < g2:
                    m1[mi, :, t*128:(t+1)*128] = 1.0
                    m2[mi, :, t*128:(t+1)*128] = 1.0
                elif mi == g2:
                    m1[mi, :, t*128:(t+1)*128] = tri
                    m2[mi, :, t*128:(t+1)*128] = 1.0
        return m1, m2

    def sel_cnt(rr):
        Sel = np.zeros((32, 16), np.float32)
        cnt = np.zeros((1, 16), np.float32)
        for qj in range(16):
            g = 2 * qj + rr
            Sel[g+1:, qj] = 1.0
            cnt[0, qj] = 4096 - (g + 1) * 128
        return Sel, cnt

    in_maps = []
    for i in range(8):
        r = i % 2
        m1, m2 = masks(r)
        Sel, cnt = sel_cnt(r)
        zmask = np.ones((1, 16), np.float32)
        if r == 0:
            zmask[0, 0] = 0.0
        sl = slice(r * 1024, (r + 1) * 1024)
        in_maps.append({
            "x": x_s[i], "ident": ident, "pv": np.array([[1.0 - r, float(r)]], np.float32), "zmask": zmask,
            "ln1w": rep128(ins["L0_attn_ln1_w"]), "ln1b": rep128(ins["L0_attn_ln1_b"]),
            "ln2w": rep128(ins["L0_attn_ln2_w"]), "ln2b": rep128(ins["L0_attn_ln2_b"]),
            "tmk": cvec(ins["L0_attn_tmk"]), "tmv": cvec(ins["L0_attn_tmv"]), "tmr": cvec(ins["L0_attn_tmr"]),
            "WkT": T_(ins["L0_attn_Wk"]), "WvT": T_(ins["L0_attn_Wv"]), "WrT": T_(ins["L0_attn_Wr"]),
            "WoT": T_(ins["L0_attn_Wo"]),
            "td8": cvec(ins["L0_attn_time_decay"][sl], 8), "tf8": cvec(ins["L0_attn_time_first"][sl], 8),
            "cmk": cvec(ins["L0_attn_cmk"]), "cmr": cvec(ins["L0_attn_cmr"]),
            "WckT": T_(ins["L0_attn_Wck"]), "WcvT": T_(ins["L0_attn_Wcv"]), "WcrT": T_(ins["L0_attn_Wcr"]),
            "gateT": T_(ins["L0_ffn_gate_w"]), "upT": T_(ins["L0_ffn_up_w"]), "downT": T_(ins["L0_ffn_down_w"]),
            "gs": cvec(ins["L0_ffn_gate_s"], 44), "us": cvec(ins["L0_ffn_up_s"], 44), "ds": cvec(ins["L0_ffn_down_s"], 16),
            "Wq1": A_(ins["L1_attn_Wq"]), "Wk1": A_(ins["L1_attn_Wk"]), "Wv1": A_(ins["L1_attn_Wv"]),
            "Wo1": A_(ins["L1_attn_Wo"]),
            "m1": m1, "m2": m2, "Sel": Sel, "cnt": cnt,
            "W1T": T_(ins["L1_ffn_W1"]), "W2T": T_(ins["L1_ffn_W2"]),
        })
    if "nc" not in _PROGS:
        _PROGS["nc"] = build_fused()
    res = bass_utils.run_bass_kernel_spmd(_PROGS["nc"], in_maps, core_ids=list(range(8)))
    return unshard_tokens([res.results[i]["out"] for i in range(8)])


# version 06
# speedup vs baseline: 1.0482x; 1.0482x over previous
import numpy as np
import concourse.bass as bass
import concourse.mybir as mybir

F32 = mybir.dt.float32
BF16 = mybir.dt.bfloat16
I32 = mybir.dt.int32
ALU = mybir.AluOpType
AF = mybir.ActivationFunctionType
AX = mybir.AxisListType

ENGINES = ("sync", "vector", "scalar", "tensor", "gpsimd")
C = 2048
TT = 2048
NT = TT // 128
NC = C // 128


def ts(i, n=128):
    return slice(i * n, (i + 1) * n)


class Prog:
    def __init__(self, nc):
        self.nc = nc
        self.steps = {e: [] for e in ENGINES}
        self.count = {e: 0 for e in ENGINES}
        self.last = None
        self.stage = "init"

    def step(self, engine, fn, deps=(), dma=False, tag="", raw=False):
        inc = 0 if raw else (16 if dma else 1)
        waits = [d for d in deps if d is not None]
        self.count[engine] += inc
        tok = (engine, self.count[engine])
        self.steps[engine].append((waits, fn, inc, self.stage))
        self.last = tok
        return tok

    def s(self, engine, fn, extra=(), dma=False, tag="", raw=False):
        return self.step(engine, fn, [self.last] + list(extra), dma=dma, tag=tag, raw=raw)

    def sdma(self, out, in_, engine="sync", extra=(), tag=""):
        return self.s(engine, lambda e: e.dma_start(out=out, in_=in_), extra, dma=True, tag=tag)

    def build(self):
        nc = self.nc
        sem_cms = {e: nc.semaphore("psem_" + e) for e in ENGINES}
        self.sems = {e: cm.__enter__() for e, cm in sem_cms.items()}
        block_cm = nc.Block()
        block = block_cm.__enter__()
        sems = self.sems

        def mk(engine):
            steps = self.steps[engine]

            def body(eng):
                waited = {}
                cur_stage = None
                cur_sid = None
                for waits, fn, inc, stage in steps:
                    if stage != cur_stage:
                        if cur_stage is not None:
                            nc.leave_named_scope(cur_stage, cur_sid, False)
                        cur_sid, _ = nc.enter_named_scope(stage, False)
                        cur_stage = stage
                    for (de, dc) in waits:
                        if waited.get(de, 0) >= dc:
                            continue
                        eng.wait_ge(sems[de], dc)
                        waited[de] = max(waited.get(de, 0), dc)
                    inst = fn(eng)
                    if inc:
                        inst.then_inc(sems[engine], inc)
                if cur_stage is not None:
                    nc.leave_named_scope(cur_stage, cur_sid, False)
            return body

        for e in ENGINES:
            if self.steps[e]:
                getattr(block, e)(mk(e))
        block_cm.__exit__(None, None, None)
        for cm in sem_cms.values():
            cm.__exit__(None, None, None)


class Ctx:
    def __init__(self, nc):
        self.nc = nc
        self._cms = []
        self._scopes = []
        self.p = Prog(nc)

    def sbuf(self, name, shape, dtype):
        cm = self.nc.sbuf_tensor(name, shape, dtype)
        t = cm.__enter__()
        (self._scopes[-1] if self._scopes else self._cms).append(cm)
        return t

    def psum(self, name, shape, dtype):
        cm = self.nc.psum_tensor(name, shape, dtype)
        t = cm.__enter__()
        (self._scopes[-1] if self._scopes else self._cms).append(cm)
        return t

    def push_scope(self):
        self._scopes.append([])

    def pop_scope(self):
        for cm in reversed(self._scopes.pop()):
            cm.__exit__(None, None, None)

    def close(self):
        self.p.s("gpsimd", lambda e: e.engine_nop(), tag="final")
        self.p.build()
        for cm in reversed(self._cms):
            cm.__exit__(None, None, None)
        return self.nc


class Common:
    def __init__(self, cx, ident_ap):
        p = cx.p
        self.identf = cx.sbuf("cm_identf", [128, 128], F32)
        self.idb = cx.sbuf("cm_idb", [128, 128], BF16)
        p.sdma(self.identf[:], ident_ap)
        p.s("vector", lambda e: e.tensor_copy(self.idb[:], self.identf[:]))


def emit_ln_stats_tile(p, xs, sq, st, Cw=C):
    p.s("vector", lambda e: e.reduce_sum(st[:, 0:1], xs[:], axis=AX.X))
    p.s("scalar", lambda e: e.activation(sq[:], xs[:], AF.Square, accum_out=st[:, 1:2]))
    p.s("vector", lambda e: e.tensor_scalar(st[:, 2:3], st[:, 0:1], 1.0 / Cw, None, ALU.mult))
    p.s("vector", lambda e: e.tensor_scalar(st[:, 3:4], st[:, 1:2], 1.0 / Cw, None, ALU.mult))
    p.s("vector", lambda e: e.tensor_tensor(st[:, 4:5], st[:, 2:3], st[:, 2:3], ALU.mult))
    p.s("vector", lambda e: e.tensor_tensor(st[:, 5:6], st[:, 3:4], st[:, 4:5], ALU.subtract))
    p.s("vector", lambda e: e.tensor_scalar(st[:, 5:6], st[:, 5:6], 1e-5, None, ALU.add))
    p.s("scalar", lambda e: e.activation(st[:, 6:7], st[:, 5:6], AF.Ln))
    p.s("scalar", lambda e: e.activation(st[:, 7:8], st[:, 6:7], AF.Exp, scale=-0.5))


def emit_ln(cx, pfx, x_t, o_t, w_ap=None, b_ap=None, r_ts=(), halo_out=None):
    p = cx.p
    cx.push_scope()
    xt = x_t.rearrange("(n p) c -> n p c", p=128)
    rts = [r.rearrange("(n p) c -> n p c", p=128) for r in r_ts]
    ot = o_t.rearrange("(n p) c -> n p c", p=128)
    xs = cx.sbuf(pfx + "xs", [128, C], F32)
    rsb = [cx.sbuf(pfx + f"rs{j}", [128, C], F32) for j in range(len(r_ts))]
    osb = cx.sbuf(pfx + "osb", [128, C], F32)
    sq = cx.sbuf(pfx + "sq", [128, C], F32)
    st = cx.sbuf(pfx + "st", [128, 8], F32)
    if w_ap is not None:
        wt = cx.sbuf(pfx + "wt", [128, C], F32)
        bt = cx.sbuf(pfx + "bt", [128, C], F32)
        p.sdma(wt[:], w_ap)
        p.sdma(bt[:], b_ap)
    for i in range(NT):
        p.sdma(xs[:], xt[i])
        for ri in range(len(r_ts)):
            p.sdma(rsb[ri][:], rts[ri][i])
            p.s("vector", lambda e, ri=ri: e.tensor_tensor(xs[:], xs[:], rsb[ri][:], ALU.add))
        emit_ln_stats_tile(p, xs, sq, st)
        p.s("vector", lambda e: e.tensor_scalar(osb[:], xs[:], st[:, 2:3], st[:, 7:8], ALU.subtract, ALU.mult))
        if w_ap is not None:
            p.s("vector", lambda e: e.tensor_tensor(osb[:], osb[:], wt[:], ALU.mult))
            p.s("vector", lambda e: e.tensor_tensor(osb[:], osb[:], bt[:], ALU.add))
        p.sdma(ot[i], osb[:])
        if halo_out is not None:
            p.sdma(halo_out[i:i+1, :], osb[127:128, :])
    cx.pop_scope()


def emit_cc(cx, kind, in_ap, out_ap, ccsem, ccount, alu=None, groups8=False):
    p = cx.p
    op = alu if alu is not None else ALU.bypass
    groups = [[0, 1, 2, 3, 4, 5, 6, 7]] if groups8 else [[0, 1], [2, 3], [4, 5], [6, 7]]

    def fn(e):
        return e.collective_compute(kind, op, replica_groups=groups,
                                    ins=[in_ap.opt()], outs=[out_ap.opt()]).then_inc(ccsem, 1)
    p.s("gpsimd", fn, tag="cc", raw=True)
    ccount[0] += 1
    need = ccount[0]

    def fn2(e):
        e.wait_ge(ccsem, need)
        return e.engine_nop()
    p.s("gpsimd", fn2, tag="ccwait")


def emit_shift_tile(cx, pfx_bufs, xsrc_dram, halo_ap, i, dst_sb, svec, zmask, hrow):
    p = cx.p
    p.sdma(dst_sb[1:128, :], xsrc_dram[i * 128: i * 128 + 127, :])
    jm1 = max(i - 1, 0)
    p.sdma(hrow[0:1, 0:C], halo_ap[0:1, i, :])
    p.sdma(hrow[0:1, C:2*C], halo_ap[1:2, jm1, :])
    p.s("vector", lambda e: e.tensor_tensor(hrow[0:1, C:2*C], hrow[0:1, C:2*C], hrow[0:1, 0:C], ALU.subtract))
    p.s("vector", lambda e: e.scalar_tensor_tensor(dst_sb[0:1, :], hrow[0:1, C:2*C], svec[0:1, 0:1], hrow[0:1, 0:C], ALU.mult, ALU.add))
    p.s("vector", lambda e, i=i: e.tensor_scalar(dst_sb[0:1, :], dst_sb[0:1, :], zmask[0:1, i:i+1], None, ALU.mult))


def emit_mix_kvr(cx, cm, aps, xn_dram, halo_ap, svec, zmask, k_out, v_out, sr_out):
    p = cx.p
    cx.push_scope()
    TCH = 512
    NCH = TT // TCH
    xn_t = xn_dram.rearrange("(n p) c -> n p c", p=128)
    hrow = cx.sbuf("mk_hrow", [1, 2 * C], F32)
    xn_c = cx.sbuf("mk_xn_c", [128, NC, TCH], BF16)
    xx_c = cx.sbuf("mk_xx_c", [128, NC, TCH], BF16)
    mixed = cx.sbuf("mk_mixed", [128, NC, TCH], BF16)
    tmv_s = cx.sbuf("mk_tmv", [128, 3, NC], F32)
    ev = [cx.sbuf(f"mk_ev{j}", [128, TCH], F32) for j in range(2)]

    for wi, tm in enumerate(["tmk", "tmv", "tmr"]):
        p.sdma(tmv_s[:, wi, :], aps[tm])
    tpA = TransPipe(cx, "mkT_")
    gp = GemmPipe(cx, "mkG_", NC)
    prev_ev = [None, None]

    def srcfn_xn(base):
        def f(i, buf, deps):
            return p.step("sync", lambda e, i=i, buf=buf: e.dma_start(out=buf[:], in_=xn_t[base + i]),
                          deps + [p.last], dma=True)
        return f

    def srcfn_xx(base):
        def f(i, buf, deps):
            ii = base + i
            t1 = p.step("sync", lambda e, ii=ii, buf=buf: e.dma_start(out=buf[1:128, :], in_=xn_dram[ii*128: ii*128+127, :]), deps + [p.last], dma=True)
            jm1 = max(ii - 1, 0)
            t2 = p.step("sync", lambda e, ii=ii: e.dma_start(out=hrow[0:1, 0:C], in_=halo_ap[0:1, ii, :]), [t1], dma=True)
            t3 = p.step("sync", lambda e, jm1=jm1: e.dma_start(out=hrow[0:1, C:2*C], in_=halo_ap[1:2, jm1, :]), [t2], dma=True)
            t4 = p.step("vector", lambda e: e.tensor_tensor(hrow[0:1, C:2*C], hrow[0:1, C:2*C], hrow[0:1, 0:C], ALU.subtract), [t3])
            t5 = p.step("vector", lambda e, buf=buf: e.scalar_tensor_tensor(buf[0:1, :], hrow[0:1, C:2*C], svec, hrow[0:1, 0:C], ALU.mult, ALU.add), [t4])
            t6 = p.step("vector", lambda e, ii=ii, buf=buf: e.tensor_scalar(buf[0:1, :], buf[0:1, :], zmask[0:1, ii:ii+1], None, ALU.mult), [t5])
            p.last = t6
            return t6
        return f

    for ch in range(NCH):
        base = ch * (TCH // 128)
        tpA.run(p, cm, srcfn_xn(base), xn_c, TCH // 128)
        tpA.run(p, cm, srcfn_xx(base), xx_c, TCH // 128)
        for wi, (wn, outd, sig) in enumerate([("WkT", k_out, False), ("WvT", v_out, False), ("WrT", sr_out, True)]):
            for c in range(NC):
                p.s("vector", lambda e, c=c: e.tensor_tensor(mixed[:, c, :], xn_c[:, c, :], xx_c[:, c, :], ALU.subtract))
                p.s("vector", lambda e, c=c, wi=wi: e.scalar_tensor_tensor(mixed[:, c, :], mixed[:, c, :], tmv_s[:, wi, c:c+1], xx_c[:, c, :], ALU.mult, ALU.add))
            act_tok = p.last
            outr = outd.rearrange("(n p) t -> n p t", p=128)

            def evac_fn(dt, ps, deps, sig=sig, outr=outr, ch=ch):
                j = dt % 2
                fn = AF.Sigmoid if sig else AF.Copy
                t1 = p.step("scalar", lambda e, j=j, ps=ps, fn=fn: e.activation(ev[j][:], ps[:], fn), deps + [prev_ev[j]])
                t2 = p.step("gpsimd", lambda e, dt=dt, j=j: e.dma_start(out=outr[dt, :, ch*TCH:(ch+1)*TCH], in_=ev[j][:]), [t1], dma=True)
                prev_ev[j] = t2
                return t2
            gp.run(p, lambda dt, wn=wn: aps[wn].rearrange("(k p) d -> p k d", p=128)[:, :, ts(dt)],
                   NC, mixed, TCH, evac_fn, act_ready=act_tok)
    cx.pop_scope()


def emit_wkv(cx, aps, kga, vga, y_out, svec_sb):
    p = cx.p
    cx.push_scope()
    TF = 4096
    NCT = 8
    kt = kga.rearrange("(two half ct p) (j i) -> half ct p j two i", two=2, half=2, p=128, i=128)
    vt = vga.rearrange("(two half ct p) (j i) -> half ct p j two i", two=2, half=2, p=128, i=128)
    yo = y_out.rearrange("(ct p) t -> ct p t", p=128)
    ks = cx.sbuf("wk_ks", [128, TF], F32)
    vs = cx.sbuf("wk_vs", [128, TF], F32)
    kb2 = cx.sbuf("wk_kb2", [128, TF], F32)
    es = cx.sbuf("wk_es", [128, TF], F32)
    zs = cx.sbuf("wk_zs", [128, TF], F32)
    rz = cx.sbuf("wk_rz", [128, TF + 1], F32)
    rr = cx.sbuf("wk_rr", [128, TF + 1], F32)
    num = cx.sbuf("wk_num", [128, TF], F32)
    den = cx.sbuf("wk_den", [128, TF], F32)
    tdb = cx.sbuf("wk_tdb", [128, NCT], F32)
    tfb = cx.sbuf("wk_tfb", [128, NCT], F32)
    lam = cx.sbuf("wk_lam", [128, NCT], F32)
    eu = cx.sbuf("wk_eu", [128, NCT], F32)
    sc = cx.sbuf("wk_sc", [128, 2], F32)

    p.sdma(tdb[:], aps["td8"])
    p.sdma(tfb[:], aps["tf8"])
    p.s("scalar", lambda e: e.activation(lam[:], tdb[:], AF.Exp))
    p.s("scalar", lambda e: e.activation(lam[:], lam[:], AF.Exp, scale=-1.0))
    p.s("scalar", lambda e: e.activation(eu[:], tfb[:], AF.Exp))
    for c in range(NCT):
        for src_t, dst in [(kt, ks), (vt, vs)]:
            p.sdma(kb2.ap().rearrange("p (j two i) -> p j two i", two=2, i=128), src_t[0, c])
            p.sdma(dst.ap().rearrange("p (j two i) -> p j two i", two=2, i=128), src_t[1, c])
            p.s("vector", lambda e, dst=dst: e.tensor_tensor(dst[:], dst[:], kb2[:], ALU.subtract))
            p.s("vector", lambda e, dst=dst: e.scalar_tensor_tensor(dst[:], dst[:], svec_sb, kb2[:], ALU.mult, ALU.add))
        p.s("vector", lambda e: e.reduce_max(sc[:, 0:1], ks[:], axis=AX.X))
        p.s("vector", lambda e: e.tensor_scalar(sc[:, 1:2], sc[:, 0:1], -1.0, None, ALU.mult))
        p.s("scalar", lambda e: e.activation(es[:], ks[:], AF.Exp, bias=sc[:, 1:2]))
        p.s("vector", lambda e: e.tensor_tensor(zs[:], es[:], vs[:], ALU.mult))
        p.s("vector", lambda e: e.memset(rz[:, 0:1], 0.0))
        p.s("vector", lambda e: e.memset(rr[:, 0:1], 0.0))
        p.s("vector", lambda e, c=c: e.tensor_tensor_scan(
            rz[:, 1:TF + 1], lam[:, c:c + 1].broadcast_to([128, TF]), zs[:], 0.0, ALU.mult, ALU.add))
        p.s("vector", lambda e, c=c: e.tensor_tensor_scan(
            rr[:, 1:TF + 1], lam[:, c:c + 1].broadcast_to([128, TF]), es[:], 0.0, ALU.mult, ALU.add))
        p.s("vector", lambda e, c=c: e.scalar_tensor_tensor(num[:], zs[:], eu[:, c:c + 1], rz[:, 0:TF], ALU.mult, ALU.add))
        p.s("vector", lambda e, c=c: e.scalar_tensor_tensor(den[:], es[:], eu[:, c:c + 1], rr[:, 0:TF], ALU.mult, ALU.add))
        p.s("scalar", lambda e: e.activation(den[:], den[:], AF.Ln))
        p.s("scalar", lambda e: e.activation(den[:], den[:], AF.Exp, scale=-1.0))
        p.s("vector", lambda e: e.tensor_tensor(num[:], num[:], den[:], ALU.mult))
        p.sdma(yo[c], num[:])
    cx.pop_scope()


def emit_wo_res(cx, cm, wo_ap, att_ap, sr_ap, x_ap, x2_ap, use_sr=True, D=C,
                att_is_gather=False, svec_sb=None, pfx="wo_"):
    p = cx.p
    cx.push_scope()
    if att_is_gather:
        att_g = att_ap.rearrange("(n p) (j two i) -> two p n j i", p=128, two=2, i=128)
    else:
        att_r = att_ap.rearrange("(n p) t -> p n t", p=128)
    sr_r = sr_ap.rearrange("(n p) t -> p n t", p=128) if use_sr else None
    xt = x_ap.rearrange("(n p) c -> n p c", p=128)
    x2t = x2_ap.rearrange("(n p) c -> n p c", p=128)
    wof = cx.sbuf(pfx + "wof", [128, D], F32)
    wob = cx.sbuf(pfx + "wob", [128, NC, D], BF16)
    av = cx.sbuf(pfx + "av", [128, NC, 128], F32)
    av2 = cx.sbuf(pfx + "av2", [128, NC, 128], F32)
    sv = cx.sbuf(pfx + "sv", [128, NC, 128], F32)
    sab = cx.sbuf(pfx + "sab", [128, NC, 128], BF16)
    xs = cx.sbuf(pfx + "xs", [128, D], F32)
    ev = cx.sbuf(pfx + "ev", [128, D], F32)
    ps = cx.psum(pfx + "ps", [128, D], F32)

    for kt2 in range(NC):
        p.sdma(wof[:], wo_ap.rearrange("(n p) d -> n p d", p=128)[kt2])
        p.s("vector", lambda e, kt2=kt2: e.tensor_copy(wob[:, kt2, :], wof[:]))
    for i in range(NT):
        if att_is_gather:
            p.sdma(av2[:], att_g[0, :, :, i, :])
            p.sdma(av[:], att_g[1, :, :, i, :])
            p.s("vector", lambda e: e.tensor_tensor(av[:], av[:], av2[:], ALU.subtract))
            p.s("vector", lambda e: e.scalar_tensor_tensor(av[:], av[:], svec_sb, av2[:], ALU.mult, ALU.add))
        else:
            p.sdma(av[:], att_r[:, :, ts(i)])
        if use_sr:
            p.sdma(sv[:], sr_r[:, :, ts(i)])
            p.s("vector", lambda e: e.tensor_tensor(sab[:], av[:], sv[:], ALU.mult))
        else:
            p.s("vector", lambda e: e.tensor_copy(sab[:], av[:]))
        for kt2 in range(NC):
            for dh in range(D // 512):
                p.s("tensor", lambda e, kt2=kt2, dh=dh: e.matmul(
                    ps[:, dh*512:(dh+1)*512], sab[:, kt2, :], wob[:, kt2, dh*512:(dh+1)*512],
                    start=(kt2 == 0), stop=(kt2 == NC - 1)))
        p.sdma(xs[:], xt[i])
        p.s("vector", lambda e: e.tensor_tensor(ev[:], ps[:], xs[:], ALU.add))
        p.sdma(x2t[i], ev[:])
    cx.pop_scope()


def emit_cmix(cx, cm, aps, xn2_dram, halo_ap, svec, zmask, rkvt_out, H=8192):
    p = cx.p
    cx.push_scope()
    TCH = 512
    NCH = TT // TCH
    NH = H // 128
    NSLAB = 4
    SL = NH // NSLAB
    ldf = cx.sbuf("cmx_ldf", [128, C], F32)
    bfa = cx.sbuf("cmx_bfa", [128, C], BF16)
    hrow = cx.sbuf("cmx_hrow", [1, 2 * C], F32)
    psT = cx.psum("cmx_psT", [128, 512], BF16)
    A = cx.sbuf("cmx_A", [128, NC, TCH], BF16)
    Bb = cx.sbuf("cmx_B", [128, NC, TCH], BF16)
    ck = cx.sbuf("cmx_ck", [128, NC, TCH], BF16)
    cr = cx.sbuf("cmx_cr", [128, NC, TCH], BF16)
    cmks = cx.sbuf("cmx_cmks", [128, 2, NC], F32)
    wstrip = cx.sbuf("cmx_ws", [128, NC, 128], F32)
    wstripb = cx.sbuf("cmx_wsb", [128, NC, 128], BF16)
    hslab = cx.sbuf("cmx_hslab", [128, SL, TCH], BF16)
    kvacc = cx.sbuf("cmx_kvacc", [128, NC, TCH], F32)
    psM = cx.psum("cmx_psM", [128, TCH], F32)
    ev = cx.sbuf("cmx_ev", [128, TCH], F32)
    sg = cx.sbuf("cmx_sg", [128, TCH], F32)
    psTT = cx.psum("cmx_psTT", [128, TCH], F32)
    evT = cx.sbuf("cmx_evT", [128, TCH], F32)
    rk_t = rkvt_out.rearrange("(n p) c -> n p c", p=128)

    p.sdma(cmks[:, 0, :], aps["cmk"])
    p.sdma(cmks[:, 1, :], aps["cmr"])
    for ch in range(NCH):
        for which, dst in [(0, A), (1, Bb)]:
            for tti in range(TCH // 128):
                i = ch * (TCH // 128) + tti
                if which == 0:
                    p.sdma(ldf[:], xn2_dram[i*128:(i+1)*128, :])
                else:
                    emit_shift_tile(cx, None, xn2_dram, halo_ap, i, ldf, svec, zmask, hrow)
                p.s("vector", lambda e: e.tensor_copy(bfa[:], ldf[:]))
                for c4 in range(NC // 4):
                    for cc in range(4):
                        c = c4 * 4 + cc
                        p.s("tensor", lambda e, c=c, cc=cc: e.transpose(psT[:, ts(cc)], bfa[:, ts(c)], cm.idb[:]))
                    p.s("scalar", lambda e, c4=c4, dst=dst, tti=tti: e.activation(
                        dst[:, c4*4:(c4+1)*4, tti*128:(tti+1)*128],
                        psT[:].rearrange("p (a b) -> p a b", a=4), AF.Copy))
        for c in range(NC):
            p.s("vector", lambda e, c=c: e.tensor_tensor(cr[:, c, :], A[:, c, :], Bb[:, c, :], ALU.subtract))
            p.s("vector", lambda e, c=c: e.scalar_tensor_tensor(ck[:, c, :], cr[:, c, :], cmks[:, 0, c:c+1], Bb[:, c, :], ALU.mult, ALU.add))
            p.s("vector", lambda e, c=c: e.scalar_tensor_tensor(cr[:, c, :], cr[:, c, :], cmks[:, 1, c:c+1], Bb[:, c, :], ALU.mult, ALU.add))
        p.s("vector", lambda e: e.memset(kvacc[:], 0.0))
        for sl in range(NSLAB):
            for hj in range(SL):
                ht = sl * SL + hj
                p.sdma(wstrip[:], aps["WckT"].rearrange("(n p) h -> p n h", p=128)[:, :, ts(ht)])
                p.s("vector", lambda e: e.tensor_copy(wstripb[:], wstrip[:]))
                for kt2 in range(NC):
                    p.s("tensor", lambda e, kt2=kt2: e.matmul(psM[:], wstripb[:, kt2, :], ck[:, kt2, :], start=(kt2 == 0), stop=(kt2 == NC - 1)))
                p.s("vector", lambda e: e.tensor_scalar_max(ev[:], psM[:], 0.0))
                p.s("scalar", lambda e, hj=hj: e.activation(hslab[:, hj, :], ev[:], AF.Square))
            for ct in range(NC):
                p.sdma(wstrip[:, 0:SL, :], aps["WcvT"].rearrange("(n p) c -> p n c", p=128)[:, sl*SL:(sl+1)*SL, ts(ct)])
                p.s("vector", lambda e: e.tensor_copy(wstripb[:, 0:SL, :], wstrip[:, 0:SL, :]))
                for hj in range(SL):
                    p.s("tensor", lambda e, hj=hj: e.matmul(psM[:], wstripb[:, hj, :], hslab[:, hj, :], start=(hj == 0), stop=(hj == SL - 1)))
                p.s("vector", lambda e, ct=ct: e.tensor_tensor(kvacc[:, ct, :], kvacc[:, ct, :], psM[:], ALU.add))
        for ct in range(NC):
            p.sdma(wstrip[:], aps["WcrT"].rearrange("(n p) d -> p n d", p=128)[:, :, ts(ct)])
            p.s("vector", lambda e: e.tensor_copy(wstripb[:], wstrip[:]))
            for kt2 in range(NC):
                p.s("tensor", lambda e, kt2=kt2: e.matmul(psM[:], wstripb[:, kt2, :], cr[:, kt2, :], start=(kt2 == 0), stop=(kt2 == NC - 1)))
            p.s("scalar", lambda e: e.activation(sg[:], psM[:], AF.Sigmoid))
            p.s("vector", lambda e, ct=ct: e.tensor_tensor(ev[:], sg[:], kvacc[:, ct, :], ALU.mult))
            for tt in range(4):
                p.s("tensor", lambda e, tt=tt: e.transpose(psTT[:, ts(tt)], ev[:, ts(tt)], cm.identf[:]))
            p.s("scalar", lambda e: e.activation(evT[:], psTT[:], AF.Copy))
            for tt in range(4):
                i = ch * 4 + tt
                p.sdma(rk_t[i][:, ts(ct)], evT[:, ts(tt)])
    cx.pop_scope()


def emit_l0ffn(cx, cm, aps, x_dram, ffot_out, I=5632):
    p = cx.p
    cx.push_scope()
    NI = I // 128
    TCH = 512
    NCH = TT // TCH
    x_t = x_dram.rearrange("(n p) c -> n p c", p=128)
    ldf = cx.sbuf("ff_ldf", [128, C], F32)
    bfa = cx.sbuf("ff_bfa", [128, C], BF16)
    psT = cx.psum("ff_psT", [128, 512], BF16)
    xc = cx.sbuf("ff_xc", [128, NC, TCH], BF16)
    wstrip = cx.sbuf("ff_ws", [128, NC, 128], F32)
    wstripb = cx.sbuf("ff_wsb", [128, NC, 128], BF16)
    w2strip = cx.sbuf("ff_w2s", [128, NI, 128], F32)
    w2stripb = cx.sbuf("ff_w2sb", [128, NI, 128], BF16)
    hdn = cx.sbuf("ff_hdn", [128, NI, TCH], BF16)
    scl = cx.sbuf("ff_scl", [128, 3, max(NI, NC)], F32)
    psM = cx.psum("ff_psM", [128, TCH], F32)
    ev = cx.sbuf("ff_ev", [128, TCH], F32)
    ev2 = cx.sbuf("ff_ev2", [128, TCH], F32)
    psTT = cx.psum("ff_psTT", [128, TCH], F32)
    evT = cx.sbuf("ff_evT", [128, TCH], F32)
    ff_t = ffot_out.rearrange("(n p) c -> n p c", p=128)

    p.sdma(scl[:, 0, 0:NI], aps["gs"])
    p.sdma(scl[:, 1, 0:NI], aps["us"])
    p.sdma(scl[:, 2, 0:NC], aps["ds"])
    for ch in range(NCH):
        for tti in range(TCH // 128):
            i = ch * (TCH // 128) + tti
            p.sdma(ldf[:], x_t[i])
            p.s("vector", lambda e: e.tensor_copy(bfa[:], ldf[:]))
            for c4 in range(NC // 4):
                for cc in range(4):
                    c = c4 * 4 + cc
                    p.s("tensor", lambda e, c=c, cc=cc: e.transpose(psT[:, ts(cc)], bfa[:, ts(c)], cm.idb[:]))
                p.s("scalar", lambda e, c4=c4, tti=tti: e.activation(
                    xc[:, c4*4:(c4+1)*4, tti*128:(tti+1)*128],
                    psT[:].rearrange("p (a b) -> p a b", a=4), AF.Copy))
        for it in range(NI):
            p.sdma(wstrip[:], aps["gateT"].rearrange("(n p) i -> p n i", p=128)[:, :, ts(it)])
            p.s("vector", lambda e: e.tensor_copy(wstripb[:], wstrip[:]))
            for kt2 in range(NC):
                p.s("tensor", lambda e, kt2=kt2: e.matmul(psM[:], wstripb[:, kt2, :], xc[:, kt2, :], start=(kt2 == 0), stop=(kt2 == NC - 1)))
            p.s("vector", lambda e, it=it: e.tensor_scalar(ev[:], psM[:], scl[:, 0, it:it+1], None, ALU.mult))
            p.s("scalar", lambda e: e.activation(ev[:], ev[:], AF.Silu))
            p.sdma(wstrip[:], aps["upT"].rearrange("(n p) i -> p n i", p=128)[:, :, ts(it)])
            p.s("vector", lambda e: e.tensor_copy(wstripb[:], wstrip[:]))
            for kt2 in range(NC):
                p.s("tensor", lambda e, kt2=kt2: e.matmul(psM[:], wstripb[:, kt2, :], xc[:, kt2, :], start=(kt2 == 0), stop=(kt2 == NC - 1)))
            p.s("vector", lambda e, it=it: e.tensor_scalar(ev2[:], psM[:], scl[:, 1, it:it+1], None, ALU.mult))
            p.s("vector", lambda e, it=it: e.tensor_tensor(hdn[:, it, :], ev[:], ev2[:], ALU.mult))
        for ct in range(NC):
            p.sdma(w2strip[:], aps["downT"].rearrange("(n p) c -> p n c", p=128)[:, :, ts(ct)])
            p.s("vector", lambda e: e.tensor_copy(w2stripb[:], w2strip[:]))
            for kt2 in range(NI):
                p.s("tensor", lambda e, kt2=kt2: e.matmul(psM[:], w2stripb[:, kt2, :], hdn[:, kt2, :], start=(kt2 == 0), stop=(kt2 == NI - 1)))
            p.s("vector", lambda e, ct=ct: e.tensor_scalar(ev[:], psM[:], scl[:, 2, ct:ct+1], None, ALU.mult))
            for tt in range(4):
                p.s("tensor", lambda e, tt=tt: e.transpose(psTT[:, ts(tt)], ev[:, ts(tt)], cm.identf[:]))
            p.s("scalar", lambda e: e.activation(evT[:], psTT[:], AF.Copy))
            for tt in range(4):
                i = ch * 4 + tt
                p.sdma(ff_t[i][:, ts(ct)], evT[:, ts(tt)])
    cx.pop_scope()


def emit_proj3(cx, cm, aps, x_dram, q_out, kin_out, vin_out):
    p = cx.p
    cx.push_scope()
    TCH = 512
    NCH = TT // TCH
    x_t = x_dram.rearrange("(n p) c -> n p c", p=128)
    ldf = cx.sbuf("p3_ldf", [128, C], F32)
    bfa = cx.sbuf("p3_bfa", [128, C], BF16)
    psT = cx.psum("p3_psT", [128, 512], BF16)
    xc = cx.sbuf("p3_xc", [128, NC, TCH], BF16)
    wstrip = cx.sbuf("p3_ws", [128, NC, 128], F32)
    wstripb = cx.sbuf("p3_wsb", [128, NC, 128], BF16)
    psM = cx.psum("p3_psM", [128, TCH], F32)
    ev = cx.sbuf("p3_ev", [128, TCH], F32)

    for ch in range(NCH):
        for tti in range(TCH // 128):
            i = ch * (TCH // 128) + tti
            p.sdma(ldf[:], x_t[i])
            p.s("vector", lambda e: e.tensor_copy(bfa[:], ldf[:]))
            for c4 in range(NC // 4):
                for cc in range(4):
                    c = c4 * 4 + cc
                    p.s("tensor", lambda e, c=c, cc=cc: e.transpose(psT[:, ts(cc)], bfa[:, ts(c)], cm.idb[:]))
                p.s("scalar", lambda e, c4=c4, tti=tti: e.activation(
                    xc[:, c4*4:(c4+1)*4, tti*128:(tti+1)*128],
                    psT[:].rearrange("p (a b) -> p a b", a=4), AF.Copy))
        for wn, outd in [("Wq1", q_out), ("Wk1", kin_out), ("Wv1", vin_out)]:
            for dt in range(NC):
                p.sdma(wstrip[:], aps[wn].rearrange("(k p) d -> p k d", p=128)[:, :, ts(dt)])
                p.s("vector", lambda e: e.tensor_copy(wstripb[:], wstrip[:]))
                for kt2 in range(NC):
                    p.s("tensor", lambda e, kt2=kt2: e.matmul(psM[:], wstripb[:, kt2, :], xc[:, kt2, :], start=(kt2 == 0), stop=(kt2 == NC - 1)))
                p.s("scalar", lambda e: e.activation(ev[:], psM[:], AF.Copy))
                p.sdma(outd.rearrange("(n p) t -> n p t", p=128)[dt, :, ch*TCH:(ch+1)*TCH], ev[:])
    cx.pop_scope()


def emit_attn(cx, cm, aps, q_dram, kg_dram, vg_dram, o_out, TF=4096, HEADS=16, QG=4):
    p = cx.p
    cx.push_scope()
    NKB = TF // 128
    NQG = NT // QG
    W = QG * 128
    Qr = q_dram.rearrange("(h p) t -> h p t", p=128)
    Kr = kg_dram.rearrange("(two hh p) (j i) -> hh p j two i", two=2, p=128, i=128)
    Vr = vg_dram.rearrange("(two hh p) (j i) -> hh p j two i", two=2, p=128, i=128)
    Or = o_out.rearrange("(h p) t -> h p t", p=128)

    ldf = cx.sbuf("at_ldf", [128, TF], F32)
    vhb = cx.sbuf("at_vhb", [128, TF], BF16)
    vt = cx.sbuf("at_vt", [128, NKB, 128], BF16)
    khb = cx.sbuf("at_khb", [128, TF], BF16)
    qhb = cx.sbuf("at_qhb", [128, TT], BF16)
    m1s = cx.sbuf("at_m1s", [128, 8, W], F32)
    m2s = cx.sbuf("at_m2s", [128, 8, W], BF16)
    m2f = cx.sbuf("at_m2f", [128, 8, W], F32)
    selb = cx.sbuf("at_selb", [32, NT], BF16)
    self_ = cx.sbuf("at_self", [32, NT], F32)
    cnts = cx.sbuf("at_cnts", [1, NT], F32)
    onesk = cx.sbuf("at_onesk", [128, 1], BF16)
    onesr = cx.sbuf("at_onesr", [1, 128], BF16)
    bsum = cx.sbuf("at_bsum", [128, NKB], F32)
    bsumtb = cx.sbuf("at_bsumtb", [32, 128], BF16)
    sufq = cx.sbuf("at_sufq", [128, NT], F32)
    sufev = cx.sbuf("at_sufev", [16, 128], F32)
    pexp = cx.sbuf("at_pexp", [128, W], BF16)
    den_s = cx.sbuf("at_dens", [1, W], F32)
    denb = cx.sbuf("at_denb", [1, W], BF16)
    obuf = cx.sbuf("at_obuf", [128, W], F32)
    psS = cx.psum("at_psS", [128, W], F32)
    psO = cx.psum("at_psO", [128, W], F32)
    psD = cx.psum("at_psD", [1, W], F32)
    psT2 = cx.psum("at_psT2", [128, 128], BF16)
    psTf = cx.psum("at_psTf", [32, 128], F32)
    psSuf = cx.psum("at_psSuf", [16, 128], F32)
    psSufT = cx.psum("at_psSufT", [128, 16], F32)
    psB = cx.psum("at_psB", [128, W], F32)

    p.s("vector", lambda e: e.memset(onesk[:], 1.0))
    p.s("vector", lambda e: e.memset(onesr[:], 1.0))
    p.sdma(m1s[:], aps["m1"].rearrange("m p w -> p m w"))
    p.sdma(m2f[:], aps["m2"].rearrange("m p w -> p m w"))
    p.s("vector", lambda e: e.tensor_copy(m2s[:], m2f[:]))
    p.sdma(self_[:], aps["Sel"])
    p.s("vector", lambda e: e.tensor_copy(selb[:], self_[:]))
    p.sdma(cnts[:], aps["cnt"])
    ldf4 = ldf.ap().rearrange("p (j two i) -> p j two i", two=2, i=128)
    for h in range(HEADS):
        p.sdma(ldf4, Kr[h])
        p.s("vector", lambda e: e.tensor_copy(khb[:], ldf[:]))
        p.sdma(ldf[:, 0:TT], Qr[h])
        p.s("vector", lambda e: e.tensor_scalar(qhb[:], ldf[:, 0:TT], 1.0 / 64.0, None, ALU.mult))
        p.sdma(ldf4, Vr[h])
        p.s("vector", lambda e: e.tensor_copy(vhb[:], ldf[:]))
        p.s("vector", lambda e: e.reduce_sum(bsum[:], ldf.ap().rearrange("p (a b) -> p a b", a=NKB), axis=AX.X))
        for kb in range(NKB):
            p.s("tensor", lambda e, kb=kb: e.transpose(psT2[:], vhb[:, ts(kb)], cm.idb[:]))
            p.s("scalar", lambda e, kb=kb: e.activation(vt[:, kb, :], psT2[:], AF.Copy))
        p.s("tensor", lambda e: e.transpose(psTf[:], bsum[:], cm.identf[:]))
        p.s("vector", lambda e: e.tensor_copy(bsumtb[:], psTf[:]))
        p.s("tensor", lambda e: e.matmul(psSuf[:], selb[:], bsumtb[:], start=True, stop=True))
        p.s("scalar", lambda e: e.activation(sufev[:], psSuf[:], AF.Copy))
        p.s("tensor", lambda e: e.transpose(psSufT[:], sufev[:], cm.identf[0:16, 0:16]))
        p.s("scalar", lambda e: e.activation(sufq[:, 0:NT], psSufT[:], AF.Copy))
        for gi in range(NQG):
            qj0 = gi * QG
            qcols = slice(qj0 * 128, qj0 * 128 + W)
            nkb_c = 2 * qj0
            total_kb = nkb_c + 8
            for kb in range(total_kb):
                mi = kb - nkb_c
                p.s("tensor", lambda e, kb=kb, qcols=qcols: e.matmul(psS[:], khb[:, ts(kb)], qhb[:, qcols], start=True, stop=True))
                if mi >= 0:
                    p.s("vector", lambda e, mi=mi: e.tensor_tensor(psS[:], psS[:], m1s[:, mi, :], ALU.mult))
                p.s("scalar", lambda e: e.activation(pexp[:], psS[:], AF.Exp))
                if mi >= 0:
                    p.s("vector", lambda e, mi=mi: e.tensor_tensor(pexp[:], pexp[:], m2s[:, mi, :], ALU.mult))
                p.s("tensor", lambda e, kb=kb, total_kb=total_kb: e.matmul(psO[:], vt[:, kb, :], pexp[:], start=(kb == 0), stop=(kb == total_kb - 1), skip_group_check=True))
                p.s("tensor", lambda e, kb=kb, total_kb=total_kb: e.matmul(psD[:], onesk[:], pexp[:], start=(kb == 0), stop=(kb == total_kb - 1), skip_group_check=True))
            p.s("vector", lambda e: e.tensor_copy(den_s[:], psD[:]))
            for t in range(QG):
                p.s("vector", lambda e, t=t, qj0=qj0: e.tensor_scalar(den_s[:, ts(t)], den_s[:, ts(t)], cnts[:, qj0 + t: qj0 + t + 1], None, ALU.add))
            p.s("scalar", lambda e: e.activation(den_s[:], den_s[:], AF.Ln))
            p.s("scalar", lambda e: e.activation(den_s[:], den_s[:], AF.Exp, scale=-1.0))
            p.s("vector", lambda e: e.tensor_copy(denb[:], den_s[:]))
            p.s("tensor", lambda e: e.matmul(psB[:], onesr[:], denb[:], start=True, stop=True))
            for t in range(QG):
                p.s("vector", lambda e, t=t, qj0=qj0: e.tensor_scalar(obuf[:, ts(t)], psO[:, ts(t)], sufq[:, qj0 + t: qj0 + t + 1], None, ALU.add))
            p.s("vector", lambda e: e.tensor_tensor(obuf[:], obuf[:], psB[:], ALU.mult))
            p.sdma(Or[h][:, qcols], obuf[:])
    cx.pop_scope()


def emit_round_chain(p, x, s127, z, m, oi):
    p.s("vector", lambda e: e.tensor_scalar(z[:], x[:], s127, 128.5, ALU.mult, ALU.add))
    p.s("vector", lambda e: e.tensor_copy(oi[:], z[:]))
    p.s("vector", lambda e: e.tensor_copy(m[:], oi[:]))
    p.s("vector", lambda e: e.tensor_tensor(z[:], m[:], z[:], ALU.is_gt))
    p.s("vector", lambda e: e.tensor_tensor(m[:], m[:], z[:], ALU.subtract))


def emit_quant(p, out_bf, x, s127, z, m, oi):
    emit_round_chain(p, x, s127, z, m, oi)
    p.s("vector", lambda e: e.tensor_scalar(out_bf[:], m[:], -128.0, None, ALU.add))


def emit_wquant(p, out_bf, w, swv, z, m, oi):
    emit_round_chain(p, w, swv, z, m, oi)
    p.s("vector", lambda e: e.tensor_scalar(m[:], m[:], -128.0, None, ALU.add))
    p.s("vector", lambda e: e.tensor_scalar(out_bf[:], m[:], 1.0, -1.0, ALU.min, ALU.max))


def emit_bitffn(cx, cm, aps, x5_dram, h1_dram, w1t_dram, w2t_dram, x6_out, H=8192):
    emit_bitffn_p1(cx, cm, aps, x5_dram, h1_dram, w1t_dram, H=H)
    emit_bitffn_p2(cx, cm, aps, x5_dram, h1_dram, w2t_dram, x6_out, H=H)


def emit_bitffn_p1(cx, cm, aps, x5_dram, h1_dram, w1t_dram, H=8192):
    p = cx.p
    NH = H // 128
    cx.push_scope()
    x_t = x5_dram.rearrange("(n p) c -> n p c", p=128)
    w_r = aps["W1T"].rearrange("(n p) h -> n p h", p=128)
    wt_r = w1t_dram.rearrange("(n p) h -> n p h", p=128)
    h1_t = h1_dram.rearrange("(n p) h -> n p h", p=128)
    WCH = 2048
    ldw = cx.sbuf("b1_ldw", [128, WCH], F32)
    zb = cx.sbuf("b1_zb", [128, WCH], F32)
    mb = cx.sbuf("b1_mb", [128, WCH], F32)
    oib = cx.sbuf("b1_oib", [128, WCH], I32)
    wtb = cx.sbuf("b1_wtb", [128, WCH], BF16)
    acc = cx.sbuf("b1_acc", [128, 2], F32)
    onescol = cx.sbuf("b1_onesc", [128, 1], F32)
    onesrow = cx.sbuf("b1_onesr", [1, 128], F32)
    sw1 = cx.sbuf("b1_sw1", [1, 2], F32)
    swv = cx.sbuf("b1_swv", [128, 1], F32)
    mnv = cx.sbuf("b1_mnv", [128, 1], F32)
    fct = cx.sbuf("b1_fct", [128, NT], F32)
    psS1 = cx.psum("b1_psS1", [1, 1], F32)
    psRep = cx.psum("b1_psRep", [128, 1], F32)
    ldf = cx.sbuf("b1_ldf", [128, C], F32)
    xqb = cx.sbuf("b1_xqb", [128, C], BF16)
    xc = cx.sbuf("b1_xc", [128, NC, TT], BF16)
    st = cx.sbuf("b1_st", [128, 8], F32)
    psT = cx.psum("b1_psT", [128, 512], BF16)
    wblk = cx.sbuf("b1_wblk", [128, NC, 512], BF16)
    psM = cx.psum("b1_psM", [128, 512], F32)
    ev = cx.sbuf("b1_ev", [128, 512], F32)

    p.s("vector", lambda e: e.memset(onescol[:], 1.0))
    p.s("vector", lambda e: e.memset(onesrow[:], 1.0))
    p.s("vector", lambda e: e.memset(acc[:, 0:1], 0.0))
    for ct in range(NC):
        for hch in range(H // WCH):
            p.sdma(ldw[:], w_r[ct][:, hch*WCH:(hch+1)*WCH])
            p.s("vector", lambda e: e.reduce_sum(acc[:, 1:2], ldw[:], axis=AX.X, apply_absolute_value=True))
            p.s("vector", lambda e: e.tensor_tensor(acc[:, 0:1], acc[:, 0:1], acc[:, 1:2], ALU.add))
    p.s("tensor", lambda e: e.matmul(psS1[:], acc[:, 0:1], onescol[:], start=True, stop=True))
    p.s("vector", lambda e: e.tensor_scalar(sw1[:, 0:1], psS1[:], 1.0 / (C * H), 1e-5, ALU.mult, ALU.max))
    p.s("vector", lambda e: e.reciprocal(sw1[:, 1:2], sw1[:, 0:1]))
    p.s("tensor", lambda e: e.matmul(psRep[:], onesrow[:], sw1[:, 1:2], start=True, stop=True))
    p.s("scalar", lambda e: e.activation(swv[:], psRep[:], AF.Copy))
    p.s("tensor", lambda e: e.matmul(psRep[:], onesrow[:], sw1[:, 0:1], start=True, stop=True))
    p.s("scalar", lambda e: e.activation(mnv[:], psRep[:], AF.Copy))
    for ct in range(NC):
        for hch in range(H // WCH):
            p.sdma(ldw[:], w_r[ct][:, hch*WCH:(hch+1)*WCH])
            emit_wquant(p, wtb, ldw, swv[:, 0:1], zb, mb, oib)
            p.sdma(wt_r[ct][:, hch*WCH:(hch+1)*WCH], wtb[:])
    for i in range(NT):
        p.sdma(ldf[:], x_t[i])
        p.s("vector", lambda e: e.reduce_max(st[:, 0:1], ldf[:], axis=AX.X, apply_absolute_value=True))
        p.s("vector", lambda e: e.tensor_scalar(st[:, 0:1], st[:, 0:1], 1e-5, None, ALU.max))
        p.s("vector", lambda e: e.reciprocal(st[:, 1:2], st[:, 0:1]))
        p.s("vector", lambda e: e.tensor_scalar(st[:, 2:3], st[:, 1:2], 127.0, None, ALU.mult))
        emit_quant(p, xqb, ldf, st[:, 2:3], zb[:, 0:C], mb[:, 0:C], oib[:, 0:C])
        p.s("vector", lambda e, i=i: e.tensor_scalar(fct[:, i:i+1], st[:, 0:1], 1.0 / 127.0, None, ALU.mult))
        p.s("vector", lambda e, i=i: e.tensor_tensor(fct[:, i:i+1], fct[:, i:i+1], mnv[:], ALU.mult))
        for c4 in range(NC // 4):
            for cc in range(4):
                c = c4 * 4 + cc
                p.s("tensor", lambda e, c=c, cc=cc: e.transpose(psT[:, ts(cc)], xqb[:, ts(c)], cm.idb[:]))
            p.s("scalar", lambda e, c4=c4, i=i: e.activation(
                xc[:, c4*4:(c4+1)*4, i*128:(i+1)*128],
                psT[:].rearrange("p (a b) -> p a b", a=4), AF.Copy))
    for hcki in range(H // 512):
        p.sdma(wblk[:], wt_r[:, :, hcki*512:(hcki+1)*512].rearrange("n p h -> p n h"))
        for i in range(NT):
            for kt2 in range(NC):
                p.s("tensor", lambda e, kt2=kt2, i=i: e.matmul(psM[:], xc[:, kt2, i*128:(i+1)*128], wblk[:, kt2, :], start=(kt2 == 0), stop=(kt2 == NC - 1)))
            p.s("vector", lambda e, i=i: e.tensor_scalar(ev[:], psM[:], fct[:, i:i+1], None, ALU.mult))
            p.s("scalar", lambda e: e.activation(ev[:], ev[:], AF.Silu))
            p.sdma(h1_t[i][:, hcki*512:(hcki+1)*512], ev[:])
    cx.pop_scope()


def emit_bitffn_p2(cx, cm, aps, x5_dram, h1_dram, w2t_dram, x6_out, H=8192):
    p = cx.p
    NH = H // 128
    cx.push_scope()
    x_t = x5_dram.rearrange("(n p) c -> n p c", p=128)
    h_t = h1_dram.rearrange("(n p) h -> n p h", p=128)
    w2_r = aps["W2T"].rearrange("(n p) c -> n p c", p=128)
    wt2_r = w2t_dram.rearrange("(n p) c -> n p c", p=128)
    x6_t = x6_out.rearrange("(n p) c -> n p c", p=128)
    QT = 2
    WCH = 2048
    ldw = cx.sbuf("b2_ldw", [128, C], F32)
    zb = cx.sbuf("b2_zb", [128, WCH], F32)
    mb = cx.sbuf("b2_mb", [128, WCH], F32)
    oib = cx.sbuf("b2_oib", [128, WCH], I32)
    wtb = cx.sbuf("b2_wtb", [128, C], BF16)
    acc = cx.sbuf("b2_acc", [128, 2], F32)
    onescol = cx.sbuf("b2_onesc", [128, 1], F32)
    onesrow = cx.sbuf("b2_onesr", [1, 128], F32)
    sw2 = cx.sbuf("b2_sw2", [1, 2], F32)
    swv = cx.sbuf("b2_swv", [128, 1], F32)
    mnv = cx.sbuf("b2_mnv", [128, 1], F32)
    psS1 = cx.psum("b2_psS1", [1, 1], F32)
    psRep = cx.psum("b2_psRep", [128, 1], F32)
    ldh = cx.sbuf("b2_ldh", [128, H], F32)
    xqb = cx.sbuf("b2_xqb", [128, H], BF16)
    xc2 = cx.sbuf("b2_xc2", [128, NH, QT * 128], BF16)
    fct = cx.sbuf("b2_fct", [128, QT], F32)
    st = cx.sbuf("b2_st", [128, 8], F32)
    psT = cx.psum("b2_psT", [128, 512], BF16)
    wblk = cx.sbuf("b2_wblk", [128, 512], BF16)
    psM = [cx.psum(f"b2_psM{t}", [128, 512], F32) for t in range(QT)]
    obuf = cx.sbuf("b2_obuf", [128, QT, C], F32)
    xs = cx.sbuf("b2_xs", [128, C], F32)
    sq = cx.sbuf("b2_sq", [128, C], F32)
    stl = cx.sbuf("b2_stl", [128, 8], F32)

    p.s("vector", lambda e: e.memset(onescol[:], 1.0))
    p.s("vector", lambda e: e.memset(onesrow[:], 1.0))
    p.s("vector", lambda e: e.memset(acc[:, 0:1], 0.0))
    for ht in range(NH):
        p.sdma(ldw[:], w2_r[ht])
        p.s("vector", lambda e: e.reduce_sum(acc[:, 1:2], ldw[:], axis=AX.X, apply_absolute_value=True))
        p.s("vector", lambda e: e.tensor_tensor(acc[:, 0:1], acc[:, 0:1], acc[:, 1:2], ALU.add))
    p.s("tensor", lambda e: e.matmul(psS1[:], acc[:, 0:1], onescol[:], start=True, stop=True))
    p.s("vector", lambda e: e.tensor_scalar(sw2[:, 0:1], psS1[:], 1.0 / (C * H), 1e-5, ALU.mult, ALU.max))
    p.s("vector", lambda e: e.reciprocal(sw2[:, 1:2], sw2[:, 0:1]))
    p.s("tensor", lambda e: e.matmul(psRep[:], onesrow[:], sw2[:, 1:2], start=True, stop=True))
    p.s("scalar", lambda e: e.activation(swv[:], psRep[:], AF.Copy))
    p.s("tensor", lambda e: e.matmul(psRep[:], onesrow[:], sw2[:, 0:1], start=True, stop=True))
    p.s("scalar", lambda e: e.activation(mnv[:], psRep[:], AF.Copy))
    for ht in range(NH):
        p.sdma(ldw[:], w2_r[ht])
        emit_wquant(p, wtb, ldw, swv[:, 0:1], zb[:, 0:C], mb[:, 0:C], oib[:, 0:C])
        p.sdma(wt2_r[ht], wtb[:])
    for qu in range(NT // QT):
        for t in range(QT):
            i = qu * QT + t
            p.sdma(ldh[:], h_t[i])
            p.s("vector", lambda e: e.reduce_max(st[:, 0:1], ldh[:], axis=AX.X, apply_absolute_value=True))
            p.s("vector", lambda e: e.tensor_scalar(st[:, 0:1], st[:, 0:1], 1e-5, None, ALU.max))
            p.s("vector", lambda e: e.reciprocal(st[:, 1:2], st[:, 0:1]))
            p.s("vector", lambda e: e.tensor_scalar(st[:, 2:3], st[:, 1:2], 127.0, None, ALU.mult))
            for hch in range(H // WCH):
                sl2 = slice(hch*WCH, (hch+1)*WCH)
                emit_quant(p, xqb[:, sl2], ldh[:, sl2], st[:, 2:3], zb, mb, oib)
            p.s("vector", lambda e, t=t: e.tensor_scalar(fct[:, t:t+1], st[:, 0:1], 1.0 / 127.0, None, ALU.mult))
            p.s("vector", lambda e, t=t: e.tensor_tensor(fct[:, t:t+1], fct[:, t:t+1], mnv[:], ALU.mult))
            for c4 in range(NH // 4):
                for cc in range(4):
                    c = c4 * 4 + cc
                    p.s("tensor", lambda e, c=c, cc=cc: e.transpose(psT[:, ts(cc)], xqb[:, ts(c)], cm.idb[:]))
                p.s("scalar", lambda e, c4=c4, t=t: e.activation(
                    xc2[:, c4*4:(c4+1)*4, t*128:(t+1)*128],
                    psT[:].rearrange("p (a b) -> p a b", a=4), AF.Copy))
        for cck in range(C // 512):
            for kt2 in range(NH):
                p.sdma(wblk[:], wt2_r[kt2][:, cck*512:(cck+1)*512])
                for t in range(QT):
                    p.s("tensor", lambda e, kt2=kt2, t=t: e.matmul(psM[t][:], xc2[:, kt2, t*128:(t+1)*128], wblk[:], start=(kt2 == 0), stop=(kt2 == NH - 1), skip_group_check=True))
            for t in range(QT):
                p.s("vector", lambda e, t=t, cck=cck: e.tensor_scalar(obuf[:, t, cck*512:(cck+1)*512], psM[t][:], fct[:, t:t+1], None, ALU.mult))
        for t in range(QT):
            i = qu * QT + t
            p.sdma(xs[:], x_t[i])
            p.s("vector", lambda e, t=t: e.tensor_tensor(xs[:], xs[:], obuf[:, t, :], ALU.add))
            emit_ln_stats_tile(p, xs, sq, stl)
            p.s("vector", lambda e: e.tensor_scalar(xs[:], xs[:], stl[:, 2:3], stl[:, 7:8], ALU.subtract, ALU.mult))
            p.sdma(x6_t[i], xs[:])
    cx.pop_scope()


def build_fused(H=8192, I=5632):
    nc = bass.Bass("TRN2", target_bir_lowering=False, debug=False)
    ap = {}
    ins_spec = [
        ("x", [TT, C]), ("ident", [128, 128]), ("pv", [1, 2]), ("zmask", [1, NT]),
        ("ln1w", [128, C]), ("ln1b", [128, C]), ("ln2w", [128, C]), ("ln2b", [128, C]),
        ("tmk", [128, NC]), ("tmv", [128, NC]), ("tmr", [128, NC]),
        ("WkT", [C, C]), ("WvT", [C, C]), ("WrT", [C, C]), ("WoT", [C, C]),
        ("td8", [128, 8]), ("tf8", [128, 8]),
        ("cmk", [128, NC]), ("cmr", [128, NC]),
        ("WckT", [C, H]), ("WcvT", [H, C]), ("WcrT", [C, C]),
        ("gateT", [C, I]), ("upT", [C, I]), ("downT", [I, C]),
        ("gs", [128, I // 128]), ("us", [128, I // 128]), ("ds", [128, NC]),
        ("Wq1", [C, C]), ("Wk1", [C, C]), ("Wv1", [C, C]), ("Wo1", [C, C]),
        ("m1", [8, 128, 512]), ("m2", [8, 128, 512]), ("Sel", [32, NT]), ("cnt", [1, NT]),
        ("W1T", [C, H]), ("W2T", [H, C]),
    ]
    for n, sh in ins_spec:
        ap[n] = nc.dram_tensor(n, sh, F32, kind="ExternalInput").ap()
    out = nc.dram_tensor("out", [TT, C], F32, kind="ExternalOutput").ap()
    dr = {}
    for n, sh, dt in [
        ("xn", [TT, C], F32), ("halo1_in", [16, C], F32), ("halo1_out", [32, C], F32),
        ("k_c", [C, TT], F32), ("v_c", [C, TT], F32), ("sr_c", [C, TT], F32),
        ("kga", [2 * C, TT], F32), ("vga", [2 * C, TT], F32),
        ("y_my", [1024, 2 * TT], F32), ("yga", [C, 2 * TT], F32),
        ("x2", [TT, C], F32), ("xn2", [TT, C], F32),
        ("halo2_in", [16, C], F32), ("halo2_out", [32, C], F32),
        ("rkv_t", [TT, C], F32), ("x3", [TT, C], F32), ("ffo_t", [TT, C], F32),
        ("x4", [TT, C], F32), ("Q_c", [C, TT], F32),
        ("K_in", [C, TT], F32), ("V_in", [C, TT], F32),
        ("KG", [2 * C, TT], F32), ("VG", [2 * C, TT], F32),
        ("O_c", [C, TT], F32), ("x5pre", [TT, C], F32), ("x5", [TT, C], F32),
        ("h1", [TT, H], F32), ("w1t", [C, H], BF16), ("w2t", [H, C], BF16),
    ]:
        dr[n] = nc.dram_tensor("i_" + n, sh, dt).ap()

    cx = Ctx(nc)
    p = cx.p
    ccsem_cm = nc.semaphore("ccsem")
    ccsem = ccsem_cm.__enter__()
    ccount = [0]
    cm = Common(cx, ap["ident"])
    pvs = cx.sbuf("cm_pv", [1, 2], F32)
    zms = cx.sbuf("cm_zmask", [1, NT], F32)
    onesr_f = cx.sbuf("cm_onesrf", [1, 128], F32)
    pv128 = cx.sbuf("cm_pv128", [128, 2], F32)
    p.sdma(pvs[:], ap["pv"])
    p.sdma(zms[:], ap["zmask"])
    cx.push_scope()
    psPv = cx.psum("cm_psPv", [128, 2], F32)
    p.s("vector", lambda e: e.memset(onesr_f[:], 1.0))
    p.s("tensor", lambda e: e.matmul(psPv[:], onesr_f[:], pvs[:], start=True, stop=True))
    p.s("scalar", lambda e: e.activation(pv128[:], psPv[:], AF.Copy))
    cx.pop_scope()
    sv_halo = pvs[0:1, 0:1]
    sv_par = pv128[:, 1:2]

    p.stage = "ln1"
    emit_ln(cx, "ln1_", ap["x"], dr["xn"], ap["ln1w"], ap["ln1b"], halo_out=dr["halo1_in"])
    emit_cc(cx, "AllGather", dr["halo1_in"], dr["halo1_out"], ccsem, ccount)
    halo1 = dr["halo1_out"].rearrange("(two n) c -> two n c", two=2)
    p.stage = "mix_kvr"
    emit_mix_kvr(cx, cm, ap, dr["xn"], halo1, sv_halo, zms, dr["k_c"], dr["v_c"], dr["sr_c"])
    emit_cc(cx, "AllGather", dr["k_c"], dr["kga"], ccsem, ccount)
    emit_cc(cx, "AllGather", dr["v_c"], dr["vga"], ccsem, ccount)
    p.stage = "wkv"
    emit_wkv(cx, ap, dr["kga"], dr["vga"], dr["y_my"], sv_par)
    emit_cc(cx, "AllGather", dr["y_my"], dr["yga"], ccsem, ccount)
    p.stage = "wo1"
    emit_wo_res(cx, cm, ap["WoT"], dr["yga"], dr["sr_c"], ap["x"], dr["x2"],
                use_sr=True, att_is_gather=True, svec_sb=sv_par)
    p.stage = "ln2"
    emit_ln(cx, "ln2_", dr["x2"], dr["xn2"], ap["ln2w"], ap["ln2b"], halo_out=dr["halo2_in"])
    emit_cc(cx, "AllGather", dr["halo2_in"], dr["halo2_out"], ccsem, ccount)
    halo2 = dr["halo2_out"].rearrange("(two n) c -> two n c", two=2)
    p.stage = "cmix"
    emit_cmix(cx, cm, ap, dr["xn2"], halo2, sv_halo, zms, dr["rkv_t"], H=H)
    p.stage = "ln3"
    emit_ln(cx, "ln3_", ap["x"], dr["x3"], r_ts=[dr["x2"], dr["rkv_t"]])
    p.stage = "l0ffn"
    emit_l0ffn(cx, cm, ap, dr["x3"], dr["ffo_t"], I=I)
    p.stage = "ln4"
    emit_ln(cx, "ln4_", dr["x3"], dr["x4"], r_ts=[dr["ffo_t"]])
    p.stage = "proj3"
    emit_proj3(cx, cm, ap, dr["x4"], dr["Q_c"], dr["K_in"], dr["V_in"])
    emit_cc(cx, "AllGather", dr["K_in"], dr["KG"], ccsem, ccount)
    emit_cc(cx, "AllGather", dr["V_in"], dr["VG"], ccsem, ccount)
    p.stage = "attn"
    emit_attn(cx, cm, ap, dr["Q_c"], dr["KG"], dr["VG"], dr["O_c"])
    p.stage = "wo2"
    emit_wo_res(cx, cm, ap["Wo1"], dr["O_c"], None, dr["x4"], dr["x5pre"], use_sr=False, pfx="wo2_")
    p.stage = "ln5"
    emit_ln(cx, "ln5_", dr["x5pre"], dr["x5"])
    p.stage = "bitffn"
    emit_bitffn(cx, cm, ap, dr["x5"], dr["h1"], dr["w1t"], dr["w2t"], out, H=H)
    nc_ret = cx.close()
    ccsem_cm.__exit__(None, None, None)
    return nc_ret


class GemmPipe:

    def __init__(self, cx, pfx, NKT, TCH=512):
        self.pfx, self.NKT, self.TCH = pfx, NKT, TCH
        self.ws = [cx.sbuf(f"{pfx}ws{j}", [128, NKT, 128], F32) for j in range(2)]
        self.wsb = [cx.sbuf(f"{pfx}wsb{j}", [128, NKT, 128], BF16) for j in range(2)]
        self.psM = [cx.psum(f"{pfx}psM{j}", [128, TCH], F32) for j in range(2)]
        self.tok_conv = [None, None]
        self.tok_mm = [None, None]
        self.tok_evac = [None, None]

    def run(self, p, w_strip_ap, NDT, act_sb, act_cols, evac_fn, wrows=None, act_ready=None):
        nkt = wrows if wrows is not None else self.NKT
        last = None
        for dt in range(NDT):
            j = dt % 2
            tl = p.step("sync", lambda e, dt=dt, j=j, nkt=nkt: e.dma_start(out=self.ws[j][:, 0:nkt, :], in_=w_strip_ap(dt)),
                        [self.tok_conv[j]], dma=True, tag=self.pfx + "ld")
            self.tok_conv[j] = p.step("vector", lambda e, j=j, nkt=nkt: e.tensor_copy(self.wsb[j][:, 0:nkt, :], self.ws[j][:, 0:nkt, :]),
                                      [tl, self.tok_mm[j]], tag=self.pfx + "cv")
            t0 = None
            for kt in range(nkt):
                deps = [self.tok_conv[j], self.tok_evac[j], act_ready] if kt == 0 else []
                t0 = p.step("tensor", lambda e, kt=kt, j=j, act_cols=act_cols, nkt=nkt: e.matmul(
                    self.psM[j][:, 0:act_cols], self.wsb[j][:, kt, :], act_sb[:, kt, 0:act_cols],
                    start=(kt == 0), stop=(kt == nkt - 1), skip_group_check=True), deps, tag=self.pfx + "mm")
            self.tok_mm[j] = t0
            self.tok_evac[j] = evac_fn(dt, self.psM[j], [t0])
            last = self.tok_evac[j]
        p.last = last
        return last


class TransPipe:
    def __init__(self, cx, pfx):
        self.ldf = [cx.sbuf(f"{pfx}ldf{j}", [128, C], F32) for j in range(2)]
        self.bfa = [cx.sbuf(f"{pfx}bfa{j}", [128, C], BF16) for j in range(2)]
        self.psT = [cx.psum(f"{pfx}psT{j}", [128, 512], BF16) for j in range(2)]
        self.pfx = pfx
        self.tok_cv = [None, None]
        self.tok_tr = [None, None]
        self.tok_ev = [None, None]

    def run(self, p, cm, src_rows_fn, dst_sb, ntiles, dst_col0=0):
        last = None
        for i in range(ntiles):
            j = i % 2
            tl = src_rows_fn(i, self.ldf[j], [self.tok_cv[j]])
            self.tok_cv[j] = p.step("vector", lambda e, j=j: e.tensor_copy(self.bfa[j][:], self.ldf[j][:]),
                                    [tl, self.tok_tr[j]], tag=self.pfx + "cv")
            ev_tok = None
            for c4 in range(NC // 4):
                first_tr = None
                last_tr = None
                for cc in range(4):
                    c = c4 * 4 + cc
                    deps = [self.tok_cv[j], self.tok_ev[j]] if cc == 0 else []
                    t = p.step("tensor", lambda e, c=c, cc=cc, j=j: e.transpose(
                        self.psT[j][:, ts(cc)], self.bfa[j][:, ts(c)], cm.idb[:]), deps, tag=self.pfx + "tr")
                    if cc == 0:
                        first_tr = t
                    last_tr = t
                ev_tok = p.step("scalar", lambda e, c4=c4, i=i, j=j, dst_col0=dst_col0, dst_sb=dst_sb: e.activation(
                    dst_sb[:, c4*4:(c4+1)*4, dst_col0 + i*128: dst_col0 + (i+1)*128],
                    self.psT[j][:].rearrange("p (a b) -> p a b", a=4), AF.Copy), [last_tr], tag=self.pfx + "ev")
                self.tok_tr[j] = first_tr
                self.tok_ev[j] = ev_tok
            last = ev_tok
            p.last = last
        return last


import concourse.bass_utils as bass_utils


def shard_tokens(x):
    B, T, F = x.shape
    xb = x.reshape(B, 32, 128, F)
    out = []
    for p2 in range(B):
        for r in range(2):
            out.append(np.ascontiguousarray(xb[p2, r::2].reshape(2048, F)))
    return out


def unshard_tokens(shards):
    F = shards[0].shape[-1]
    out = np.empty((4, 32, 128, F), np.float32)
    for p2 in range(4):
        for r in range(2):
            out[p2, r::2] = np.asarray(shards[2 * p2 + r], dtype=np.float32).reshape(16, 128, F)
    return out.reshape(4, 4096, F)


def rep128(v):
    return np.ascontiguousarray(np.broadcast_to(np.asarray(v, np.float32)[None, :], (128, len(v))))


def cvec(v, n=16):
    return np.ascontiguousarray(np.asarray(v, np.float32).reshape(n, 128).T)


_PROGS = {}


def kernel(**ins):
    T_ = lambda a: np.ascontiguousarray(np.asarray(a, np.float32).T)
    A_ = lambda a: np.ascontiguousarray(np.asarray(a, np.float32))
    ident = np.eye(128, dtype=np.float32)
    x_s = shard_tokens(np.asarray(ins["L0_attn_x"], np.float32))
    tri = (np.arange(128)[:, None] <= np.arange(128)[None, :]).astype(np.float32)

    def masks(rr):
        m1 = np.zeros((8, 128, 512), np.float32)
        m2 = np.zeros((8, 128, 512), np.float32)
        for mi in range(8):
            for t in range(4):
                g2 = 2 * t + rr
                if mi < g2:
                    m1[mi, :, t*128:(t+1)*128] = 1.0
                    m2[mi, :, t*128:(t+1)*128] = 1.0
                elif mi == g2:
                    m1[mi, :, t*128:(t+1)*128] = tri
                    m2[mi, :, t*128:(t+1)*128] = 1.0
        return m1, m2

    def sel_cnt(rr):
        Sel = np.zeros((32, 16), np.float32)
        cnt = np.zeros((1, 16), np.float32)
        for qj in range(16):
            g = 2 * qj + rr
            Sel[g+1:, qj] = 1.0
            cnt[0, qj] = 4096 - (g + 1) * 128
        return Sel, cnt

    in_maps = []
    for i in range(8):
        r = i % 2
        m1, m2 = masks(r)
        Sel, cnt = sel_cnt(r)
        zmask = np.ones((1, 16), np.float32)
        if r == 0:
            zmask[0, 0] = 0.0
        sl = slice(r * 1024, (r + 1) * 1024)
        in_maps.append({
            "x": x_s[i], "ident": ident, "pv": np.array([[1.0 - r, float(r)]], np.float32), "zmask": zmask,
            "ln1w": rep128(ins["L0_attn_ln1_w"]), "ln1b": rep128(ins["L0_attn_ln1_b"]),
            "ln2w": rep128(ins["L0_attn_ln2_w"]), "ln2b": rep128(ins["L0_attn_ln2_b"]),
            "tmk": cvec(ins["L0_attn_tmk"]), "tmv": cvec(ins["L0_attn_tmv"]), "tmr": cvec(ins["L0_attn_tmr"]),
            "WkT": T_(ins["L0_attn_Wk"]), "WvT": T_(ins["L0_attn_Wv"]), "WrT": T_(ins["L0_attn_Wr"]),
            "WoT": T_(ins["L0_attn_Wo"]),
            "td8": cvec(ins["L0_attn_time_decay"][sl], 8), "tf8": cvec(ins["L0_attn_time_first"][sl], 8),
            "cmk": cvec(ins["L0_attn_cmk"]), "cmr": cvec(ins["L0_attn_cmr"]),
            "WckT": T_(ins["L0_attn_Wck"]), "WcvT": T_(ins["L0_attn_Wcv"]), "WcrT": T_(ins["L0_attn_Wcr"]),
            "gateT": T_(ins["L0_ffn_gate_w"]), "upT": T_(ins["L0_ffn_up_w"]), "downT": T_(ins["L0_ffn_down_w"]),
            "gs": cvec(ins["L0_ffn_gate_s"], 44), "us": cvec(ins["L0_ffn_up_s"], 44), "ds": cvec(ins["L0_ffn_down_s"], 16),
            "Wq1": A_(ins["L1_attn_Wq"]), "Wk1": A_(ins["L1_attn_Wk"]), "Wv1": A_(ins["L1_attn_Wv"]),
            "Wo1": A_(ins["L1_attn_Wo"]),
            "m1": m1, "m2": m2, "Sel": Sel, "cnt": cnt,
            "W1T": T_(ins["L1_ffn_W1"]), "W2T": T_(ins["L1_ffn_W2"]),
        })
    if "nc" not in _PROGS:
        _PROGS["nc"] = build_fused()
    res = bass_utils.run_bass_kernel_spmd(_PROGS["nc"], in_maps, core_ids=list(range(8)))
    return unshard_tokens([res.results[i]["out"] for i in range(8)])


# version 09
# speedup vs baseline: 1.9308x; 1.8420x over previous
import numpy as np
import concourse.bass as bass
import concourse.mybir as mybir

F32 = mybir.dt.float32
BF16 = mybir.dt.bfloat16
I32 = mybir.dt.int32
ALU = mybir.AluOpType
AF = mybir.ActivationFunctionType
AX = mybir.AxisListType

ENGINES = ("sync", "vector", "scalar", "tensor", "gpsimd")
C = 2048
TT = 2048
NT = TT // 128
NC = C // 128


def ts(i, n=128):
    return slice(i * n, (i + 1) * n)


class Prog:
    def __init__(self, nc):
        self.nc = nc
        self.steps = {e: [] for e in ENGINES}
        self.count = {e: 0 for e in ENGINES}
        self.last = None
        self.stage = "init"

    def step(self, engine, fn, deps=(), dma=False, tag="", raw=False):
        inc = 0 if raw else (16 if dma else 1)
        waits = [d for d in deps if d is not None]
        self.count[engine] += inc
        tok = (engine, self.count[engine])
        self.steps[engine].append((waits, fn, inc, self.stage))
        self.last = tok
        return tok

    def s(self, engine, fn, extra=(), dma=False, tag="", raw=False):
        return self.step(engine, fn, [self.last] + list(extra), dma=dma, tag=tag, raw=raw)

    def sdma(self, out, in_, engine="sync", extra=(), tag=""):
        return self.s(engine, lambda e: e.dma_start(out=out, in_=in_), extra, dma=True, tag=tag)

    def build(self):
        nc = self.nc
        sem_cms = {e: nc.semaphore("psem_" + e) for e in ENGINES}
        self.sems = {e: cm.__enter__() for e, cm in sem_cms.items()}
        block_cm = nc.Block()
        block = block_cm.__enter__()
        sems = self.sems

        def mk(engine):
            steps = self.steps[engine]

            def body(eng):
                waited = {}
                cur_stage = None
                cur_sid = None
                for waits, fn, inc, stage in steps:
                    if stage != cur_stage:
                        if cur_stage is not None:
                            nc.leave_named_scope(cur_stage, cur_sid, False)
                        cur_sid, _ = nc.enter_named_scope(stage, False)
                        cur_stage = stage
                    for (de, dc) in waits:
                        if waited.get(de, 0) >= dc:
                            continue
                        eng.wait_ge(sems[de], dc)
                        waited[de] = max(waited.get(de, 0), dc)
                    inst = fn(eng)
                    if inc:
                        inst.then_inc(sems[engine], inc)
                if cur_stage is not None:
                    nc.leave_named_scope(cur_stage, cur_sid, False)
            return body

        for e in ENGINES:
            if self.steps[e]:
                getattr(block, e)(mk(e))
        block_cm.__exit__(None, None, None)
        for cm in sem_cms.values():
            cm.__exit__(None, None, None)


class Ctx:
    def __init__(self, nc):
        self.nc = nc
        self._cms = []
        self._scopes = []
        self.p = Prog(nc)

    def sbuf(self, name, shape, dtype):
        cm = self.nc.sbuf_tensor(name, shape, dtype)
        t = cm.__enter__()
        (self._scopes[-1] if self._scopes else self._cms).append(cm)
        return t

    def psum(self, name, shape, dtype):
        cm = self.nc.psum_tensor(name, shape, dtype)
        t = cm.__enter__()
        (self._scopes[-1] if self._scopes else self._cms).append(cm)
        return t

    def push_scope(self):
        self._scopes.append([])

    def pop_scope(self):
        for cm in reversed(self._scopes.pop()):
            cm.__exit__(None, None, None)

    def close(self):
        self.p.s("gpsimd", lambda e: e.engine_nop(), tag="final")
        self.p.build()
        for cm in reversed(self._cms):
            cm.__exit__(None, None, None)
        return self.nc


class Common:
    def __init__(self, cx, ident_ap):
        p = cx.p
        self.identf = cx.sbuf("cm_identf", [128, 128], F32)
        self.idb = cx.sbuf("cm_idb", [128, 128], BF16)
        p.sdma(self.identf[:], ident_ap)
        p.s("vector", lambda e: e.tensor_copy(self.idb[:], self.identf[:]))


def emit_ln_stats_tile(p, xs, sq, st, Cw=C):
    p.s("vector", lambda e: e.reduce_sum(st[:, 0:1], xs[:], axis=AX.X))
    p.s("scalar", lambda e: e.activation(sq[:], xs[:], AF.Square, accum_out=st[:, 1:2]))
    p.s("vector", lambda e: e.tensor_scalar(st[:, 2:3], st[:, 0:1], 1.0 / Cw, None, ALU.mult))
    p.s("vector", lambda e: e.tensor_scalar(st[:, 3:4], st[:, 1:2], 1.0 / Cw, None, ALU.mult))
    p.s("vector", lambda e: e.tensor_tensor(st[:, 4:5], st[:, 2:3], st[:, 2:3], ALU.mult))
    p.s("vector", lambda e: e.tensor_tensor(st[:, 5:6], st[:, 3:4], st[:, 4:5], ALU.subtract))
    p.s("vector", lambda e: e.tensor_scalar(st[:, 5:6], st[:, 5:6], 1e-5, None, ALU.add))
    p.s("scalar", lambda e: e.activation(st[:, 6:7], st[:, 5:6], AF.Ln))
    p.s("scalar", lambda e: e.activation(st[:, 7:8], st[:, 6:7], AF.Exp, scale=-0.5))


def emit_ln(cx, pfx, x_t, o_t, w_ap=None, b_ap=None, r_ts=(), halo_out=None):
    p = cx.p
    cx.push_scope()
    xt = x_t.rearrange("(n p) c -> n p c", p=128)
    rts = [r.rearrange("(n p) c -> n p c", p=128) for r in r_ts]
    ot = o_t.rearrange("(n p) c -> n p c", p=128)
    xs = cx.sbuf(pfx + "xs", [128, C], F32)
    rsb = [cx.sbuf(pfx + f"rs{j}", [128, C], F32) for j in range(len(r_ts))]
    osb = cx.sbuf(pfx + "osb", [128, C], F32)
    sq = cx.sbuf(pfx + "sq", [128, C], F32)
    st = cx.sbuf(pfx + "st", [128, 8], F32)
    if w_ap is not None:
        wt = cx.sbuf(pfx + "wt", [128, C], F32)
        bt = cx.sbuf(pfx + "bt", [128, C], F32)
        p.sdma(wt[:], w_ap)
        p.sdma(bt[:], b_ap)
    for i in range(NT):
        p.sdma(xs[:], xt[i])
        for ri in range(len(r_ts)):
            p.sdma(rsb[ri][:], rts[ri][i])
            p.s("vector", lambda e, ri=ri: e.tensor_tensor(xs[:], xs[:], rsb[ri][:], ALU.add))
        emit_ln_stats_tile(p, xs, sq, st)
        p.s("vector", lambda e: e.tensor_scalar(osb[:], xs[:], st[:, 2:3], st[:, 7:8], ALU.subtract, ALU.mult))
        if w_ap is not None:
            p.s("vector", lambda e: e.tensor_tensor(osb[:], osb[:], wt[:], ALU.mult))
            p.s("vector", lambda e: e.tensor_tensor(osb[:], osb[:], bt[:], ALU.add))
        p.sdma(ot[i], osb[:])
        if halo_out is not None:
            p.sdma(halo_out[i:i+1, :], osb[127:128, :])
    cx.pop_scope()


def emit_cc(cx, kind, in_ap, out_ap, ccsem, ccount, alu=None, groups8=False):
    p = cx.p
    op = alu if alu is not None else ALU.bypass
    groups = [[0, 1, 2, 3, 4, 5, 6, 7]] if groups8 else [[0, 1], [2, 3], [4, 5], [6, 7]]

    def fn(e):
        return e.collective_compute(kind, op, replica_groups=groups,
                                    ins=[in_ap.opt()], outs=[out_ap.opt()]).then_inc(ccsem, 1)
    p.s("gpsimd", fn, tag="cc", raw=True)
    ccount[0] += 1
    need = ccount[0]

    def fn2(e):
        e.wait_ge(ccsem, need)
        return e.engine_nop()
    p.s("gpsimd", fn2, tag="ccwait")


def emit_shift_tile(cx, pfx_bufs, xsrc_dram, halo_ap, i, dst_sb, svec, zmask, hrow):
    p = cx.p
    p.sdma(dst_sb[1:128, :], xsrc_dram[i * 128: i * 128 + 127, :])
    jm1 = max(i - 1, 0)
    p.sdma(hrow[0:1, 0:C], halo_ap[0:1, i, :])
    p.sdma(hrow[0:1, C:2*C], halo_ap[1:2, jm1, :])
    p.s("vector", lambda e: e.tensor_tensor(hrow[0:1, C:2*C], hrow[0:1, C:2*C], hrow[0:1, 0:C], ALU.subtract))
    p.s("vector", lambda e: e.scalar_tensor_tensor(dst_sb[0:1, :], hrow[0:1, C:2*C], svec[0:1, 0:1], hrow[0:1, 0:C], ALU.mult, ALU.add))
    p.s("vector", lambda e, i=i: e.tensor_scalar(dst_sb[0:1, :], dst_sb[0:1, :], zmask[0:1, i:i+1], None, ALU.mult))


def emit_mix_kvr(cx, cm, aps, xn_dram, halo_ap, svec, zmask, k_out, v_out, sr_out):
    p = cx.p
    cx.push_scope()
    TCH = 512
    NCH = TT // TCH
    xn_t = xn_dram.rearrange("(n p) c -> n p c", p=128)
    hrow = cx.sbuf("mk_hrow", [1, 2 * C], F32)
    xn_c = cx.sbuf("mk_xn_c", [128, NC, TCH], BF16)
    xx_c = cx.sbuf("mk_xx_c", [128, NC, TCH], BF16)
    mixed = cx.sbuf("mk_mixed", [128, NC, TCH], BF16)
    tmv_s = cx.sbuf("mk_tmv", [128, 3, NC], F32)
    ev = [cx.sbuf(f"mk_ev{j}", [128, TCH], F32) for j in range(2)]

    for wi, tm in enumerate(["tmk", "tmv", "tmr"]):
        p.sdma(tmv_s[:, wi, :], aps[tm])
    tpA = TransPipe(cx, "mkT_")
    gp = GemmPipe(cx, "mkG_", NC)
    prev_ev = [None, None]

    def srcfn_xn(base):
        def f(i, buf, deps):
            return p.step("sync", lambda e, i=i, buf=buf: e.dma_start(out=buf[:], in_=xn_t[base + i]),
                          deps + [p.last], dma=True)
        return f

    def srcfn_xx(base):
        def f(i, buf, deps):
            ii = base + i
            t1 = p.step("sync", lambda e, ii=ii, buf=buf: e.dma_start(out=buf[1:128, :], in_=xn_dram[ii*128: ii*128+127, :]), deps + [p.last], dma=True)
            jm1 = max(ii - 1, 0)
            t2 = p.step("sync", lambda e, ii=ii: e.dma_start(out=hrow[0:1, 0:C], in_=halo_ap[0:1, ii, :]), [t1], dma=True)
            t3 = p.step("sync", lambda e, jm1=jm1: e.dma_start(out=hrow[0:1, C:2*C], in_=halo_ap[1:2, jm1, :]), [t2], dma=True)
            t4 = p.step("vector", lambda e: e.tensor_tensor(hrow[0:1, C:2*C], hrow[0:1, C:2*C], hrow[0:1, 0:C], ALU.subtract), [t3])
            t5 = p.step("vector", lambda e, buf=buf: e.scalar_tensor_tensor(buf[0:1, :], hrow[0:1, C:2*C], svec, hrow[0:1, 0:C], ALU.mult, ALU.add), [t4])
            t6 = p.step("vector", lambda e, ii=ii, buf=buf: e.tensor_scalar(buf[0:1, :], buf[0:1, :], zmask[0:1, ii:ii+1], None, ALU.mult), [t5])
            p.last = t6
            return t6
        return f

    for ch in range(NCH):
        base = ch * (TCH // 128)
        tpA.run(p, cm, srcfn_xn(base), xn_c, TCH // 128)
        tpA.run(p, cm, srcfn_xx(base), xx_c, TCH // 128)
        for wi, (wn, outd, sig) in enumerate([("WkT", k_out, False), ("WvT", v_out, False), ("WrT", sr_out, True)]):
            for c in range(NC):
                p.s("vector", lambda e, c=c: e.tensor_tensor(mixed[:, c, :], xn_c[:, c, :], xx_c[:, c, :], ALU.subtract))
                p.s("vector", lambda e, c=c, wi=wi: e.scalar_tensor_tensor(mixed[:, c, :], mixed[:, c, :], tmv_s[:, wi, c:c+1], xx_c[:, c, :], ALU.mult, ALU.add))
            act_tok = p.last
            outr = outd.rearrange("(n p) t -> n p t", p=128)

            def evac_fn(dt, ps, deps, sig=sig, outr=outr, ch=ch):
                j = dt % 2
                fn = AF.Sigmoid if sig else AF.Copy
                t1 = p.step("scalar", lambda e, j=j, ps=ps, fn=fn: e.activation(ev[j][:], ps[:], fn), deps + [prev_ev[j]])
                t2 = p.step("gpsimd", lambda e, dt=dt, j=j: e.dma_start(out=outr[dt, :, ch*TCH:(ch+1)*TCH], in_=ev[j][:]), [t1], dma=True)
                prev_ev[j] = t2
                return t2
            gp.run(p, lambda dt, wn=wn: aps[wn].rearrange("(k p) d -> p k d", p=128)[:, :, ts(dt)],
                   NC, mixed, TCH, evac_fn, act_ready=act_tok)
    cx.pop_scope()


def emit_wkv(cx, aps, kga, vga, y_out, svec_sb):
    p = cx.p
    cx.push_scope()
    TF = 4096
    NCT = 8
    kt = kga.rearrange("(two half ct p) (j i) -> half ct p j two i", two=2, half=2, p=128, i=128)
    vt = vga.rearrange("(two half ct p) (j i) -> half ct p j two i", two=2, half=2, p=128, i=128)
    yo = y_out.rearrange("(ct p) t -> ct p t", p=128)
    ks = cx.sbuf("wk_ks", [128, TF], F32)
    vs = cx.sbuf("wk_vs", [128, TF], F32)
    kb2 = cx.sbuf("wk_kb2", [128, TF], F32)
    es = cx.sbuf("wk_es", [128, TF], F32)
    zs = cx.sbuf("wk_zs", [128, TF], F32)
    rz = cx.sbuf("wk_rz", [128, TF + 1], F32)
    rr = cx.sbuf("wk_rr", [128, TF + 1], F32)
    num = cx.sbuf("wk_num", [128, TF], F32)
    den = cx.sbuf("wk_den", [128, TF], F32)
    tdb = cx.sbuf("wk_tdb", [128, NCT], F32)
    tfb = cx.sbuf("wk_tfb", [128, NCT], F32)
    lam = cx.sbuf("wk_lam", [128, NCT], F32)
    eu = cx.sbuf("wk_eu", [128, NCT], F32)
    sc = cx.sbuf("wk_sc", [128, 2], F32)

    p.sdma(tdb[:], aps["td8"])
    p.sdma(tfb[:], aps["tf8"])
    p.s("scalar", lambda e: e.activation(lam[:], tdb[:], AF.Exp))
    p.s("scalar", lambda e: e.activation(lam[:], lam[:], AF.Exp, scale=-1.0))
    p.s("scalar", lambda e: e.activation(eu[:], tfb[:], AF.Exp))
    for c in range(NCT):
        for src_t, dst in [(kt, ks), (vt, vs)]:
            p.sdma(kb2.ap().rearrange("p (j two i) -> p j two i", two=2, i=128), src_t[0, c])
            p.sdma(dst.ap().rearrange("p (j two i) -> p j two i", two=2, i=128), src_t[1, c])
            p.s("vector", lambda e, dst=dst: e.tensor_tensor(dst[:], dst[:], kb2[:], ALU.subtract))
            p.s("vector", lambda e, dst=dst: e.scalar_tensor_tensor(dst[:], dst[:], svec_sb, kb2[:], ALU.mult, ALU.add))
        p.s("vector", lambda e: e.reduce_max(sc[:, 0:1], ks[:], axis=AX.X))
        p.s("vector", lambda e: e.tensor_scalar(sc[:, 1:2], sc[:, 0:1], -1.0, None, ALU.mult))
        p.s("scalar", lambda e: e.activation(es[:], ks[:], AF.Exp, bias=sc[:, 1:2]))
        p.s("vector", lambda e: e.tensor_tensor(zs[:], es[:], vs[:], ALU.mult))
        p.s("vector", lambda e: e.memset(rz[:, 0:1], 0.0))
        p.s("vector", lambda e: e.memset(rr[:, 0:1], 0.0))
        p.s("vector", lambda e, c=c: e.tensor_tensor_scan(
            rz[:, 1:TF + 1], lam[:, c:c + 1].broadcast_to([128, TF]), zs[:], 0.0, ALU.mult, ALU.add))
        p.s("vector", lambda e, c=c: e.tensor_tensor_scan(
            rr[:, 1:TF + 1], lam[:, c:c + 1].broadcast_to([128, TF]), es[:], 0.0, ALU.mult, ALU.add))
        p.s("vector", lambda e, c=c: e.scalar_tensor_tensor(num[:], zs[:], eu[:, c:c + 1], rz[:, 0:TF], ALU.mult, ALU.add))
        p.s("vector", lambda e, c=c: e.scalar_tensor_tensor(den[:], es[:], eu[:, c:c + 1], rr[:, 0:TF], ALU.mult, ALU.add))
        p.s("scalar", lambda e: e.activation(den[:], den[:], AF.Ln))
        p.s("scalar", lambda e: e.activation(den[:], den[:], AF.Exp, scale=-1.0))
        p.s("vector", lambda e: e.tensor_tensor(num[:], num[:], den[:], ALU.mult))
        p.sdma(yo[c], num[:])
    cx.pop_scope()


def emit_wo_res(cx, cm, wo_ap, att_ap, sr_ap, x_ap, x2_ap, use_sr=True, D=C,
                att_is_gather=False, svec_sb=None, pfx="wo_"):
    p = cx.p
    cx.push_scope()
    if att_is_gather:
        att_g = att_ap.rearrange("(n p) (j two i) -> two p n j i", p=128, two=2, i=128)
    else:
        att_r = att_ap.rearrange("(n p) t -> p n t", p=128)
    sr_r = sr_ap.rearrange("(n p) t -> p n t", p=128) if use_sr else None
    xt = x_ap.rearrange("(n p) c -> n p c", p=128)
    x2t = x2_ap.rearrange("(n p) c -> n p c", p=128)
    wof = cx.sbuf(pfx + "wof", [128, D], F32)
    wob = cx.sbuf(pfx + "wob", [128, NC, D], BF16)
    av = cx.sbuf(pfx + "av", [128, NC, 128], F32)
    av2 = cx.sbuf(pfx + "av2", [128, NC, 128], F32)
    sv = cx.sbuf(pfx + "sv", [128, NC, 128], F32)
    sab = cx.sbuf(pfx + "sab", [128, NC, 128], BF16)
    xs = cx.sbuf(pfx + "xs", [128, D], F32)
    ev = cx.sbuf(pfx + "ev", [128, D], F32)
    ps = cx.psum(pfx + "ps", [128, D], F32)

    for kt2 in range(NC):
        p.sdma(wof[:], wo_ap.rearrange("(n p) d -> n p d", p=128)[kt2])
        p.s("vector", lambda e, kt2=kt2: e.tensor_copy(wob[:, kt2, :], wof[:]))
    for i in range(NT):
        if att_is_gather:
            p.sdma(av2[:], att_g[0, :, :, i, :])
            p.sdma(av[:], att_g[1, :, :, i, :])
            p.s("vector", lambda e: e.tensor_tensor(av[:], av[:], av2[:], ALU.subtract))
            p.s("vector", lambda e: e.scalar_tensor_tensor(av[:], av[:], svec_sb, av2[:], ALU.mult, ALU.add))
        else:
            p.sdma(av[:], att_r[:, :, ts(i)])
        if use_sr:
            p.sdma(sv[:], sr_r[:, :, ts(i)])
            p.s("vector", lambda e: e.tensor_tensor(sab[:], av[:], sv[:], ALU.mult))
        else:
            p.s("vector", lambda e: e.tensor_copy(sab[:], av[:]))
        for kt2 in range(NC):
            for dh in range(D // 512):
                p.s("tensor", lambda e, kt2=kt2, dh=dh: e.matmul(
                    ps[:, dh*512:(dh+1)*512], sab[:, kt2, :], wob[:, kt2, dh*512:(dh+1)*512],
                    start=(kt2 == 0), stop=(kt2 == NC - 1)))
        p.sdma(xs[:], xt[i])
        p.s("vector", lambda e: e.tensor_tensor(ev[:], ps[:], xs[:], ALU.add))
        p.sdma(x2t[i], ev[:])
    cx.pop_scope()


def emit_cmix(cx, cm, aps, xn2_dram, halo_ap, svec, zmask, rkvt_out, H=8192):
    p = cx.p
    cx.push_scope()
    TCH = 512
    NCH = TT // TCH
    NH = H // 128
    NSLAB = 4
    SL = NH // NSLAB
    hrow = cx.sbuf("cmx_hrow", [1, 2 * C], F32)
    A = cx.sbuf("cmx_A", [128, NC, TCH], BF16)
    Bb = cx.sbuf("cmx_B", [128, NC, TCH], BF16)
    ck = cx.sbuf("cmx_ck", [128, NC, TCH], BF16)
    cr = cx.sbuf("cmx_cr", [128, NC, TCH], BF16)
    cmks = cx.sbuf("cmx_cmks", [128, 2, NC], F32)
    hslab = cx.sbuf("cmx_hslab", [128, SL, TCH], BF16)
    kvacc = cx.sbuf("cmx_kvacc", [128, NC, TCH], F32)
    evh = [cx.sbuf(f"cmx_evh{j}", [128, TCH], F32) for j in range(2)]
    sg = [cx.sbuf(f"cmx_sg{j}", [128, TCH], F32) for j in range(2)]
    evr = [cx.sbuf(f"cmx_evr{j}", [128, TCH], F32) for j in range(2)]
    evT = [cx.sbuf(f"cmx_evT{j}", [128, TCH], F32) for j in range(2)]
    psTT = [cx.psum(f"cmx_psTT{j}", [128, TCH], F32) for j in range(2)]
    rk_t = rkvt_out.rearrange("(n p) c -> n p c", p=128)
    tp = TransPipe(cx, "cmxT_")
    gp = GemmPipe(cx, "cmxG_", NC)
    prev_h = [None, None]
    prev_kv = [None] * NC
    prev_sg = [None, None]
    prev_T = [None, None]
    prev_d = [None, None]

    p.sdma(cmks[:, 0, :], aps["cmk"])
    p.sdma(cmks[:, 1, :], aps["cmr"])

    def srcfn_a(base):
        def f(i, buf, deps):
            return p.step("sync", lambda e, i=i, buf=buf: e.dma_start(out=buf[:], in_=xn2_dram[(base+i)*128:(base+i+1)*128, :]),
                          deps + [p.last], dma=True)
        return f

    def srcfn_b(base):
        def f(i, buf, deps):
            ii = base + i
            t1 = p.step("sync", lambda e, ii=ii, buf=buf: e.dma_start(out=buf[1:128, :], in_=xn2_dram[ii*128: ii*128+127, :]), deps + [p.last], dma=True)
            jm1 = max(ii - 1, 0)
            t2 = p.step("sync", lambda e, ii=ii: e.dma_start(out=hrow[0:1, 0:C], in_=halo_ap[0:1, ii, :]), [t1], dma=True)
            t3 = p.step("sync", lambda e, jm1=jm1: e.dma_start(out=hrow[0:1, C:2*C], in_=halo_ap[1:2, jm1, :]), [t2], dma=True)
            t4 = p.step("vector", lambda e: e.tensor_tensor(hrow[0:1, C:2*C], hrow[0:1, C:2*C], hrow[0:1, 0:C], ALU.subtract), [t3])
            t5 = p.step("vector", lambda e, buf=buf: e.scalar_tensor_tensor(buf[0:1, :], hrow[0:1, C:2*C], svec, hrow[0:1, 0:C], ALU.mult, ALU.add), [t4])
            t6 = p.step("vector", lambda e, ii=ii, buf=buf: e.tensor_scalar(buf[0:1, :], buf[0:1, :], zmask[0:1, ii:ii+1], None, ALU.mult), [t5])
            p.last = t6
            return t6
        return f

    for ch in range(NCH):
        base = ch * (TCH // 128)
        tp.run(p, cm, srcfn_a(base), A, TCH // 128)
        tp.run(p, cm, srcfn_b(base), Bb, TCH // 128)
        for c in range(NC):
            p.s("vector", lambda e, c=c: e.tensor_tensor(cr[:, c, :], A[:, c, :], Bb[:, c, :], ALU.subtract))
            p.s("vector", lambda e, c=c: e.scalar_tensor_tensor(ck[:, c, :], cr[:, c, :], cmks[:, 0, c:c+1], Bb[:, c, :], ALU.mult, ALU.add))
            p.s("vector", lambda e, c=c: e.scalar_tensor_tensor(cr[:, c, :], cr[:, c, :], cmks[:, 1, c:c+1], Bb[:, c, :], ALU.mult, ALU.add))
        p.s("vector", lambda e: e.memset(kvacc[:], 0.0))
        act_tok = p.last
        for c in range(NC):
            prev_kv[c] = act_tok
        for sl in range(NSLAB):
            def evac_h(hj, ps, deps):
                j = hj % 2
                t1 = p.step("vector", lambda e, j=j, ps=ps: e.tensor_scalar_max(evh[j][:], ps[:], 0.0), deps + [prev_h[j]])
                t2 = p.step("scalar", lambda e, hj=hj, j=j: e.activation(hslab[:, hj, :], evh[j][:], AF.Square), [t1])
                prev_h[j] = t2
                return t2
            gp.run(p, lambda hj, sl=sl: aps["WckT"].rearrange("(n p) h -> p n h", p=128)[:, :, ts(sl * SL + hj)],
                   SL, ck, TCH, evac_h, act_ready=act_tok)
            hready = p.last
            def evac_kv(ct, ps, deps):
                t1 = p.step("vector", lambda e, ct=ct, ps=ps: e.tensor_tensor(kvacc[:, ct, :], kvacc[:, ct, :], ps[:], ALU.add),
                            deps + [prev_kv[ct]])
                prev_kv[ct] = t1
                return t1
            gp.run(p, lambda ct, sl=sl: aps["WcvT"].rearrange("(n p) c -> p n c", p=128)[:, sl*SL:(sl+1)*SL, ts(ct)],
                   NC, hslab, TCH, evac_kv, wrows=SL, act_ready=hready)
        def evac_r(ct, ps, deps, ch=ch):
            j = ct % 2
            t1 = p.step("scalar", lambda e, j=j, ps=ps: e.activation(sg[j][:], ps[:], AF.Sigmoid), deps + [prev_sg[j]])
            t2 = p.step("vector", lambda e, ct=ct, j=j: e.tensor_tensor(evr[j][:], sg[j][:], kvacc[:, ct, :], ALU.mult), [t1, prev_kv[ct], prev_d[j]])
            ttoks = []
            for tt in range(4):
                d2 = [t2, prev_T[j]] if tt == 0 else []
                ttoks.append(p.step("tensor", lambda e, tt=tt, j=j: e.transpose(psTT[j][:, ts(tt)], evr[j][:, ts(tt)], cm.identf[:]), d2))
            t3 = p.step("scalar", lambda e, j=j: e.activation(evT[j][:], psTT[j][:], AF.Copy), [ttoks[-1]])
            prev_T[j] = t3
            prev_sg[j] = t2
            dtoks = []
            for tt in range(4):
                i = ch * 4 + tt
                dtoks.append(p.step("gpsimd", lambda e, i=i, tt=tt, j=j, ct=ct: e.dma_start(out=rk_t[i][:, ts(ct)], in_=evT[j][:, ts(tt)]), [t3], dma=True))
            prev_d[j] = dtoks[-1]
            p.last = dtoks[-1]
            return dtoks[-1]
        gp.run(p, lambda ct: aps["WcrT"].rearrange("(n p) d -> p n d", p=128)[:, :, ts(ct)],
               NC, cr, TCH, evac_r, act_ready=p.last)
    cx.pop_scope()


def emit_l0ffn(cx, cm, aps, x_dram, ffot_out, I=5632):
    p = cx.p
    cx.push_scope()
    NI = I // 128
    TCH = 512
    NCH = TT // TCH
    x_t = x_dram.rearrange("(n p) c -> n p c", p=128)
    xc = cx.sbuf("ff_xc", [128, NC, TCH], BF16)
    hdn = cx.sbuf("ff_hdn", [128, NI, TCH], BF16)
    scl = cx.sbuf("ff_scl", [128, 3, max(NI, NC)], F32)
    evg = [cx.sbuf(f"ff_evg{j}", [128, TCH], F32) for j in range(2)]
    evu = [cx.sbuf(f"ff_evu{j}", [128, TCH], F32) for j in range(2)]
    evd = [cx.sbuf(f"ff_evd{j}", [128, TCH], F32) for j in range(2)]
    evT = [cx.sbuf(f"ff_evT{j}", [128, TCH], F32) for j in range(2)]
    psTT = [cx.psum(f"ff_psTT{j}", [128, TCH], F32) for j in range(2)]
    ff_t = ffot_out.rearrange("(n p) c -> n p c", p=128)
    tp = TransPipe(cx, "ffT_")
    gp = GemmPipe(cx, "ffG_", NC)
    gpd = GemmPipe(cx, "ffD_", NI)

    p.sdma(scl[:, 0, 0:NI], aps["gs"])
    p.sdma(scl[:, 1, 0:NI], aps["us"])
    p.sdma(scl[:, 2, 0:NC], aps["ds"])
    prev_g = [None, None]
    prev_u = [None, None]
    prev_h = [None, None]
    prev_d = [None, None]
    prev_T = [None, None]

    def srcfn(base):
        def f(i, buf, deps):
            return p.step("sync", lambda e, i=i, buf=buf: e.dma_start(out=buf[:], in_=x_t[base + i]),
                          deps + [p.last], dma=True)
        return f

    for ch in range(NCH):
        tp.run(p, cm, srcfn(ch * (TCH // 128)), xc, TCH // 128)
        act_tok = p.last
        def evac_gate(it, ps, deps):
            j = it % 2
            t1 = p.step("vector", lambda e, it=it, j=j, ps=ps: e.tensor_scalar(evg[j][:], ps[:], scl[:, 0, it:it+1], None, ALU.mult), deps + [prev_g[j]])
            t2 = p.step("scalar", lambda e, j=j: e.activation(evg[j][:], evg[j][:], AF.Silu), [t1])
            t3 = p.step("vector", lambda e, it=it, j=j: e.tensor_copy(hdn[:, it, :], evg[j][:]), [t2])
            prev_g[j] = t3
            return t3
        gp.run(p, lambda dt: aps["gateT"].rearrange("(k p) i -> p k i", p=128)[:, :, ts(dt)],
               NI, xc, TCH, evac_gate, act_ready=act_tok)
        def evac_up(it, ps, deps):
            j = it % 2
            t1 = p.step("vector", lambda e, it=it, j=j, ps=ps: e.tensor_scalar(evu[j][:], ps[:], scl[:, 1, it:it+1], None, ALU.mult), deps + [prev_u[j]])
            t2 = p.step("vector", lambda e, it=it, j=j: e.tensor_tensor(hdn[:, it, :], hdn[:, it, :], evu[j][:], ALU.mult), [t1, prev_h[j]])
            prev_u[j] = t2
            prev_h[j] = t2
            return t2
        gp.run(p, lambda dt: aps["upT"].rearrange("(k p) i -> p k i", p=128)[:, :, ts(dt)],
               NI, xc, TCH, evac_up, act_ready=act_tok)
        def evac_down(ct, ps, deps, ch=ch):
            j = ct % 2
            t1 = p.step("vector", lambda e, ct=ct, j=j, ps=ps: e.tensor_scalar(evd[j][:], ps[:], scl[:, 2, ct:ct+1], None, ALU.mult), deps + [prev_d[j]])
            ttoks = []
            for tt in range(4):
                d2 = [t1, prev_T[j]] if tt == 0 else []
                ttoks.append(p.step("tensor", lambda e, tt=tt, j=j: e.transpose(psTT[j][:, ts(tt)], evd[j][:, ts(tt)], cm.identf[:]), d2))
            t2 = p.step("scalar", lambda e, j=j: e.activation(evT[j][:], psTT[j][:], AF.Copy), [ttoks[-1]])
            prev_T[j] = t2
            dtoks = []
            for tt in range(4):
                i = ch * 4 + tt
                dtoks.append(p.step("gpsimd", lambda e, i=i, tt=tt, j=j, ct=ct: e.dma_start(out=ff_t[i][:, ts(ct)], in_=evT[j][:, ts(tt)]), [t2], dma=True))
            prev_d[j] = dtoks[-1]
            p.last = dtoks[-1]
            return dtoks[-1]
        gpd.run(p, lambda ct: aps["downT"].rearrange("(k p) c -> p k c", p=128)[:, :, ts(ct)],
               NC, hdn, TCH, evac_down, act_ready=p.last)
    cx.pop_scope()


def emit_proj3(cx, cm, aps, x_dram, q_out, kin_out, vin_out):
    p = cx.p
    cx.push_scope()
    TCH = 512
    NCH = TT // TCH
    x_t = x_dram.rearrange("(n p) c -> n p c", p=128)
    xc = cx.sbuf("p3_xc", [128, NC, TCH], BF16)
    ev = [cx.sbuf(f"p3_ev{j}", [128, TCH], F32) for j in range(2)]
    tp = TransPipe(cx, "p3T_")
    gp = GemmPipe(cx, "p3G_", NC)
    prev_ev = [None, None]

    def srcfn(base):
        def f(i, buf, deps):
            return p.step("sync", lambda e, i=i, buf=buf: e.dma_start(out=buf[:], in_=x_t[base + i]),
                          deps + [p.last], dma=True)
        return f

    for ch in range(NCH):
        tp.run(p, cm, srcfn(ch * (TCH // 128)), xc, TCH // 128)
        act_tok = p.last
        for wn, outd in [("Wq1", q_out), ("Wk1", kin_out), ("Wv1", vin_out)]:
            outr = outd.rearrange("(n p) t -> n p t", p=128)

            def evac_fn(dt, ps, deps, outr=outr, ch=ch):
                j = dt % 2
                t1 = p.step("scalar", lambda e, j=j, ps=ps: e.activation(ev[j][:], ps[:], AF.Copy), deps + [prev_ev[j]])
                t2 = p.step("gpsimd", lambda e, dt=dt, j=j: e.dma_start(out=outr[dt, :, ch*TCH:(ch+1)*TCH], in_=ev[j][:]), [t1], dma=True)
                prev_ev[j] = t2
                return t2
            gp.run(p, lambda dt, wn=wn: aps[wn].rearrange("(k p) d -> p k d", p=128)[:, :, ts(dt)],
                   NC, xc, TCH, evac_fn, act_ready=act_tok)
    cx.pop_scope()


def emit_attn(cx, cm, aps, q_dram, kg_dram, vg_dram, o_out, TF=4096, HEADS=16, QG=4):
    p = cx.p
    cx.push_scope()
    NKB = TF // 128
    NQG = NT // QG
    W = QG * 128
    Qr = q_dram.rearrange("(h p) t -> h p t", p=128)
    Kr = kg_dram.rearrange("(two hh p) (j i) -> hh p j two i", two=2, p=128, i=128)
    Vr = vg_dram.rearrange("(two hh p) (j i) -> hh p j two i", two=2, p=128, i=128)
    Or = o_out.rearrange("(h p) t -> h p t", p=128)

    ldf = cx.sbuf("at_ldf", [128, TF], F32)
    vhb = cx.sbuf("at_vhb", [128, TF], BF16)
    vt = cx.sbuf("at_vt", [128, NKB, 128], BF16)
    khb = cx.sbuf("at_khb", [128, TF], BF16)
    qhb = cx.sbuf("at_qhb", [128, TT], BF16)
    m1s = cx.sbuf("at_m1s", [128, 8, W], F32)
    m2s = cx.sbuf("at_m2s", [128, 8, W], BF16)
    m2f = cx.sbuf("at_m2f", [128, 8, W], F32)
    selb = cx.sbuf("at_selb", [32, NT], BF16)
    self_ = cx.sbuf("at_self", [32, NT], F32)
    cnts = cx.sbuf("at_cnts", [1, NT], F32)
    onesk = cx.sbuf("at_onesk", [128, 1], BF16)
    onesr = cx.sbuf("at_onesr", [1, 128], BF16)
    bsum = cx.sbuf("at_bsum", [128, NKB], F32)
    bsumtb = cx.sbuf("at_bsumtb", [32, 128], BF16)
    sufq = cx.sbuf("at_sufq", [128, NT], F32)
    sufev = cx.sbuf("at_sufev", [16, 128], F32)
    pexp = cx.sbuf("at_pexp", [128, W], BF16)
    den_s = cx.sbuf("at_dens", [1, W], F32)
    denb = cx.sbuf("at_denb", [1, W], BF16)
    obuf = cx.sbuf("at_obuf", [128, W], F32)
    psS = cx.psum("at_psS", [128, W], F32)
    psO = cx.psum("at_psO", [128, W], F32)
    psD = cx.psum("at_psD", [1, W], F32)
    psT2 = cx.psum("at_psT2", [128, 128], BF16)
    psTf = cx.psum("at_psTf", [32, 128], F32)
    psSuf = cx.psum("at_psSuf", [16, 128], F32)
    psSufT = cx.psum("at_psSufT", [128, 16], F32)
    psB = cx.psum("at_psB", [128, W], F32)

    p.s("vector", lambda e: e.memset(onesk[:], 1.0))
    p.s("vector", lambda e: e.memset(onesr[:], 1.0))
    p.sdma(m1s[:], aps["m1"].rearrange("m p w -> p m w"))
    p.sdma(m2f[:], aps["m2"].rearrange("m p w -> p m w"))
    p.s("vector", lambda e: e.tensor_copy(m2s[:], m2f[:]))
    p.sdma(self_[:], aps["Sel"])
    p.s("vector", lambda e: e.tensor_copy(selb[:], self_[:]))
    p.sdma(cnts[:], aps["cnt"])
    ldf4 = ldf.ap().rearrange("p (j two i) -> p j two i", two=2, i=128)
    for h in range(HEADS):
        p.sdma(ldf4, Kr[h])
        p.s("vector", lambda e: e.tensor_copy(khb[:], ldf[:]))
        p.sdma(ldf[:, 0:TT], Qr[h])
        p.s("vector", lambda e: e.tensor_scalar(qhb[:], ldf[:, 0:TT], 1.0 / 64.0, None, ALU.mult))
        p.sdma(ldf4, Vr[h])
        p.s("vector", lambda e: e.tensor_copy(vhb[:], ldf[:]))
        p.s("vector", lambda e: e.reduce_sum(bsum[:], ldf.ap().rearrange("p (a b) -> p a b", a=NKB), axis=AX.X))
        for kb in range(NKB):
            p.s("tensor", lambda e, kb=kb: e.transpose(psT2[:], vhb[:, ts(kb)], cm.idb[:]))
            p.s("scalar", lambda e, kb=kb: e.activation(vt[:, kb, :], psT2[:], AF.Copy))
        p.s("tensor", lambda e: e.transpose(psTf[:], bsum[:], cm.identf[:]))
        p.s("vector", lambda e: e.tensor_copy(bsumtb[:], psTf[:]))
        p.s("tensor", lambda e: e.matmul(psSuf[:], selb[:], bsumtb[:], start=True, stop=True))
        p.s("scalar", lambda e: e.activation(sufev[:], psSuf[:], AF.Copy))
        p.s("tensor", lambda e: e.transpose(psSufT[:], sufev[:], cm.identf[0:16, 0:16]))
        p.s("scalar", lambda e: e.activation(sufq[:, 0:NT], psSufT[:], AF.Copy))
        for gi in range(NQG):
            qj0 = gi * QG
            qcols = slice(qj0 * 128, qj0 * 128 + W)
            nkb_c = 2 * qj0
            total_kb = nkb_c + 8
            for kb in range(total_kb):
                mi = kb - nkb_c
                p.s("tensor", lambda e, kb=kb, qcols=qcols: e.matmul(psS[:], khb[:, ts(kb)], qhb[:, qcols], start=True, stop=True))
                if mi >= 0:
                    p.s("vector", lambda e, mi=mi: e.tensor_tensor(psS[:], psS[:], m1s[:, mi, :], ALU.mult))
                p.s("scalar", lambda e: e.activation(pexp[:], psS[:], AF.Exp))
                if mi >= 0:
                    p.s("vector", lambda e, mi=mi: e.tensor_tensor(pexp[:], pexp[:], m2s[:, mi, :], ALU.mult))
                p.s("tensor", lambda e, kb=kb, total_kb=total_kb: e.matmul(psO[:], vt[:, kb, :], pexp[:], start=(kb == 0), stop=(kb == total_kb - 1), skip_group_check=True))
                p.s("tensor", lambda e, kb=kb, total_kb=total_kb: e.matmul(psD[:], onesk[:], pexp[:], start=(kb == 0), stop=(kb == total_kb - 1), skip_group_check=True))
            p.s("vector", lambda e: e.tensor_copy(den_s[:], psD[:]))
            for t in range(QG):
                p.s("vector", lambda e, t=t, qj0=qj0: e.tensor_scalar(den_s[:, ts(t)], den_s[:, ts(t)], cnts[:, qj0 + t: qj0 + t + 1], None, ALU.add))
            p.s("scalar", lambda e: e.activation(den_s[:], den_s[:], AF.Ln))
            p.s("scalar", lambda e: e.activation(den_s[:], den_s[:], AF.Exp, scale=-1.0))
            p.s("vector", lambda e: e.tensor_copy(denb[:], den_s[:]))
            p.s("tensor", lambda e: e.matmul(psB[:], onesr[:], denb[:], start=True, stop=True))
            for t in range(QG):
                p.s("vector", lambda e, t=t, qj0=qj0: e.tensor_scalar(obuf[:, ts(t)], psO[:, ts(t)], sufq[:, qj0 + t: qj0 + t + 1], None, ALU.add))
            p.s("vector", lambda e: e.tensor_tensor(obuf[:], obuf[:], psB[:], ALU.mult))
            p.sdma(Or[h][:, qcols], obuf[:])
    cx.pop_scope()


def emit_round_chain(p, x, s127, z, m, oi):
    p.s("vector", lambda e: e.tensor_scalar(z[:], x[:], s127, 128.5, ALU.mult, ALU.add))
    p.s("vector", lambda e: e.tensor_copy(oi[:], z[:]))
    p.s("vector", lambda e: e.tensor_copy(m[:], oi[:]))
    p.s("vector", lambda e: e.tensor_tensor(z[:], m[:], z[:], ALU.is_gt))
    p.s("vector", lambda e: e.tensor_tensor(m[:], m[:], z[:], ALU.subtract))


def emit_quant(p, out_bf, x, s127, z, m, oi):
    emit_round_chain(p, x, s127, z, m, oi)
    p.s("vector", lambda e: e.tensor_scalar(out_bf[:], m[:], -128.0, None, ALU.add))


def emit_wquant(p, out_bf, w, swv, z, m, oi):
    emit_round_chain(p, w, swv, z, m, oi)
    p.s("vector", lambda e: e.tensor_scalar(m[:], m[:], -128.0, None, ALU.add))
    p.s("vector", lambda e: e.tensor_scalar(out_bf[:], m[:], 1.0, -1.0, ALU.min, ALU.max))


def emit_bitffn(cx, cm, aps, x5_dram, h1_dram, w1t_dram, w2t_dram, x6_out, H=8192):
    emit_bitffn_p1(cx, cm, aps, x5_dram, h1_dram, w1t_dram, H=H)
    emit_bitffn_p2(cx, cm, aps, x5_dram, h1_dram, w2t_dram, x6_out, H=H)


def emit_round_chain_on(p, eng, x, s127, z, m, oi, deps):
    t = p.step(eng, lambda e: e.tensor_scalar(z[:], x[:], s127, 128.5, ALU.mult, ALU.add), deps)
    t = p.step(eng, lambda e: e.tensor_copy(oi[:], z[:]), [t])
    t = p.step(eng, lambda e: e.tensor_copy(m[:], oi[:]), [t])
    t = p.step(eng, lambda e: e.tensor_tensor(z[:], m[:], z[:], ALU.is_gt), [t])
    t = p.step(eng, lambda e: e.tensor_tensor(m[:], m[:], z[:], ALU.subtract), [t])
    return t


def emit_wtern_pipe(cx, cm, pfx, p, w_src_rows, wt_dst_rows, nstrips, strip_w, mean_N, swv, mnv):
    cx.push_scope()
    ldw = [cx.sbuf(f"{pfx}ldw{j}", [128, strip_w], F32) for j in range(4)]
    zb = [cx.sbuf(f"{pfx}zb{j}", [128, strip_w], F32) for j in range(2)]
    mb = [cx.sbuf(f"{pfx}mb{j}", [128, strip_w], F32) for j in range(2)]
    oib = [cx.sbuf(f"{pfx}oib{j}", [128, strip_w], I32) for j in range(2)]
    wtb = [cx.sbuf(f"{pfx}wtb{j}", [128, strip_w], BF16) for j in range(2)]
    acc = cx.sbuf(pfx + "acc", [128, 4], F32)
    onescol = cx.sbuf(pfx + "onesc", [128, 1], F32)
    onesrow = cx.sbuf(pfx + "onesr", [1, 128], F32)
    swsc = cx.sbuf(pfx + "swsc", [1, 2], F32)
    psS1 = cx.psum(pfx + "psS1", [1, 1], F32)
    psRep = cx.psum(pfx + "psRep", [128, 1], F32)
    ENG = ("vector", "vector")
    p.s("vector", lambda e: e.memset(onescol[:], 1.0))
    p.s("vector", lambda e: e.memset(onesrow[:], 1.0))
    p.s("vector", lambda e: e.memset(acc[:, 0:2], 0.0))
    root = p.last
    tok_ld = [None] * 4
    tok_red = [None, None]
    for i in range(nstrips):
        j = i % 2
        tok_ld[j] = p.step("sync", lambda e, i=i, j=j: e.dma_start(out=ldw[j][:], in_=w_src_rows(i)),
                           [tok_red[j] if tok_red[j] else root], dma=True)
        t1 = p.step("vector", lambda e, j=j: e.reduce_sum(acc[:, 2+j:3+j], ldw[j][:], axis=AX.X, apply_absolute_value=True), [tok_ld[j], tok_red[j]])
        tok_red[j] = p.step("vector", lambda e, j=j: e.tensor_tensor(acc[:, j:j+1], acc[:, j:j+1], acc[:, 2+j:3+j], ALU.add), [t1])
    t = p.step("vector", lambda e: e.tensor_tensor(acc[:, 0:1], acc[:, 0:1], acc[:, 1:2], ALU.add), [tok_red[0], tok_red[1]])
    t = p.step("tensor", lambda e: e.matmul(psS1[:], acc[:, 0:1], onescol[:], start=True, stop=True), [t])
    t = p.step("vector", lambda e: e.tensor_scalar(swsc[:, 0:1], psS1[:], 1.0 / mean_N, 1e-5, ALU.mult, ALU.max), [t])
    t = p.step("vector", lambda e: e.reciprocal(swsc[:, 1:2], swsc[:, 0:1]), [t])
    t = p.step("tensor", lambda e: e.matmul(psRep[:], onesrow[:], swsc[:, 1:2], start=True, stop=True), [t])
    t = p.step("scalar", lambda e: e.activation(swv[:], psRep[:], AF.Copy), [t])
    t = p.step("tensor", lambda e: e.matmul(psRep[:], onesrow[:], swsc[:, 0:1], start=True, stop=True), [t])
    t = p.step("scalar", lambda e: e.activation(mnv[:], psRep[:], AF.Copy), [t])
    p.last = t
    root = t
    tok_st = [None, None]
    tok_ld2 = [None, None]
    for i in range(nstrips):
        j = i % 2
        tok_ld2[j] = p.step("sync", lambda e, i=i, j=j: e.dma_start(out=ldw[2+j][:], in_=w_src_rows(i)),
                            [tok_st[j] if tok_st[j] else root], dma=True)
        eng = ENG[j]
        tt = emit_round_chain_on(p, eng, ldw[2+j], swv[:, 0:1], zb[j], mb[j], oib[j], [tok_ld2[j], tok_st[j]])
        tt = p.step(eng, lambda e, j=j: e.tensor_scalar(mb[j][:], mb[j][:], -128.0, 1.0, ALU.add, ALU.min), [tt])
        tt = p.step(eng, lambda e, j=j: e.tensor_scalar(wtb[j][:], mb[j][:], -1.0, None, ALU.max), [tt])
        tok_st[j] = p.step("sync", lambda e, i=i, j=j: e.dma_start(out=wt_dst_rows(i), in_=wtb[j][:]), [tt], dma=True)
    t = p.step("gpsimd", lambda e: e.engine_nop(), [tok_st[0], tok_st[1]])
    p.last = t
    cx.pop_scope()


def emit_bitffn_p1(cx, cm, aps, x5_dram, h1_dram, w1t_dram, H=8192):
    p = cx.p
    NH = H // 128
    cx.push_scope()
    x_t = x5_dram.rearrange("(n p) c -> n p c", p=128)
    w_r = aps["W1T"].rearrange("(n p) h -> n p h", p=128)
    wt_r = w1t_dram.rearrange("(n p) h -> n p h", p=128)
    h1_t = h1_dram.rearrange("(n p) h -> n p h", p=128)
    WCH = 2048
    NS = NC * (H // WCH)
    swv = cx.sbuf("b1_swv", [128, 1], F32)
    mnv = cx.sbuf("b1_mnv", [128, 1], F32)

    def wsrc(i):
        ct, hch = divmod(i, H // WCH)
        return w_r[ct][:, hch*WCH:(hch+1)*WCH]

    def wdst(i):
        ct, hch = divmod(i, H // WCH)
        return wt_r[ct][:, hch*WCH:(hch+1)*WCH]
    emit_wtern_pipe(cx, cm, "b1w_", p, wsrc, wdst, NS, WCH, C * H, swv, mnv)

    xc = cx.sbuf("b1_xc", [128, NC, TT], BF16)
    fct = cx.sbuf("b1_fct", [128, NT], F32)
    cx.push_scope()
    ldf = [cx.sbuf(f"b1_ldf{j}", [128, C], F32) for j in range(2)]
    zb = [cx.sbuf(f"b1_zb{j}", [128, C], F32) for j in range(2)]
    mb = [cx.sbuf(f"b1_mb{j}", [128, C], F32) for j in range(2)]
    oib = [cx.sbuf(f"b1_oib{j}", [128, C], I32) for j in range(2)]
    xqb = [cx.sbuf(f"b1_xqb{j}", [128, C], BF16) for j in range(2)]
    st = cx.sbuf("b1_st", [128, 2, 4], F32)
    psT = [cx.psum(f"b1_psT{j}", [128, 512], BF16) for j in range(2)]
    ENG = ("vector", "vector")
    root = p.last
    tok_q = [root, root]
    tok_tr = [None, None]
    tok_ev = [None, None]
    for i in range(NT):
        j = i % 2
        eng = ENG[j]
        tl = p.step("sync", lambda e, i=i, j=j: e.dma_start(out=ldf[j][:], in_=x_t[i]), [tok_q[j]], dma=True)
        t = p.step("vector", lambda e, j=j: e.reduce_max(st[:, j, 0:1], ldf[j][:], axis=AX.X, apply_absolute_value=True), [tl])
        t = p.step("vector", lambda e, j=j: e.tensor_scalar(st[:, j, 0:1], st[:, j, 0:1], 1e-5, None, ALU.max), [t])
        t = p.step("vector", lambda e, j=j: e.reciprocal(st[:, j, 1:2], st[:, j, 0:1]), [t])
        t = p.step("vector", lambda e, j=j: e.tensor_scalar(st[:, j, 2:3], st[:, j, 1:2], 127.0, None, ALU.mult), [t])
        t = emit_round_chain_on(p, eng, ldf[j], st[:, j, 2:3], zb[j], mb[j], oib[j], [t])
        t = p.step(eng, lambda e, j=j: e.tensor_scalar(xqb[j][:], mb[j][:], -128.0, None, ALU.add), [t])
        t2 = p.step(eng, lambda e, i=i, j=j: e.tensor_scalar(fct[:, i:i+1], st[:, j, 0:1], 1.0 / 127.0, None, ALU.mult), [t])
        tok_q[j] = p.step(eng, lambda e, i=i: e.tensor_tensor(fct[:, i:i+1], fct[:, i:i+1], mnv[:], ALU.mult), [t2])
        evt = None
        for c4 in range(NC // 4):
            ttoks = []
            for cc in range(4):
                c = c4 * 4 + cc
                d2 = [tok_q[j], tok_ev[j]] if cc == 0 else []
                ttoks.append(p.step("tensor", lambda e, c=c, cc=cc, j=j: e.transpose(psT[j][:, ts(cc)], xqb[j][:, ts(c)], cm.idb[:]), d2))
            evt = p.step("scalar", lambda e, c4=c4, i=i, j=j: e.activation(
                xc[:, c4*4:(c4+1)*4, i*128:(i+1)*128],
                psT[j][:].rearrange("p (a b) -> p a b", a=4), AF.Copy), [ttoks[-1]])
            tok_ev[j] = evt
            tok_tr[j] = ttoks[0]
        tok_q[j] = evt
    p.last = tok_ev[0] if tok_ev[1] is None else tok_ev[1]
    act_tok = p.step("gpsimd", lambda e: e.engine_nop(), [tok_ev[0], tok_ev[1], tok_q[0], tok_q[1]])
    p.last = act_tok
    cx.pop_scope()

    wblk = [cx.sbuf(f"b1_wblk{j}", [128, NC, 512], BF16) for j in range(2)]
    psM = [cx.psum(f"b1_psM{j}", [128, 512], F32) for j in range(2)]
    ev = [cx.sbuf(f"b1_ev{j}", [128, 512], F32) for j in range(2)]
    tok_wld = [None, None]
    tok_mm = [None, None]
    tok_evc = [None, None]
    tok_dma = [None, None]
    for hcki in range(H // 512):
        jw = hcki % 2
        tok_wld[jw] = p.step("sync", lambda e, hcki=hcki, jw=jw: e.dma_start(
            out=wblk[jw][:], in_=wt_r[:, :, hcki*512:(hcki+1)*512].rearrange("n p h -> p n h")),
            [tok_mm[jw], act_tok], dma=True)
        for i in range(NT):
            jp = i % 2
            t0 = None
            for kt2 in range(NC):
                deps = [tok_wld[jw], tok_evc[jp]] if kt2 == 0 else []
                t0 = p.step("tensor", lambda e, kt2=kt2, i=i, jw=jw, jp=jp: e.matmul(
                    psM[jp][:], xc[:, kt2, i*128:(i+1)*128], wblk[jw][:, kt2, :],
                    start=(kt2 == 0), stop=(kt2 == NC - 1), skip_group_check=True), deps)
            if i == NT - 1:
                tok_mm[jw] = t0
            t1 = p.step("vector", lambda e, i=i, jp=jp: e.tensor_scalar(ev[jp][:], psM[jp][:], fct[:, i:i+1], None, ALU.mult), [t0, tok_dma[jp]])
            t2 = p.step("scalar", lambda e, jp=jp: e.activation(ev[jp][:], ev[jp][:], AF.Silu), [t1])
            tok_evc[jp] = t2
            tok_dma[jp] = p.step("sync", lambda e, i=i, hcki=hcki, jp=jp: e.dma_start(
                out=h1_t[i][:, hcki*512:(hcki+1)*512], in_=ev[jp][:]), [t2], dma=True)
    t = p.step("gpsimd", lambda e: e.engine_nop(), [tok_dma[0], tok_dma[1], tok_mm[0], tok_mm[1]])
    p.last = t
    cx.pop_scope()


def emit_bitffn_p2(cx, cm, aps, x5_dram, h1_dram, w2t_dram, x6_out, H=8192):
    p = cx.p
    NH = H // 128
    cx.push_scope()
    x_t = x5_dram.rearrange("(n p) c -> n p c", p=128)
    h_t = h1_dram.rearrange("(n p) h -> n p h", p=128)
    w2_r = aps["W2T"].rearrange("(n p) c -> n p c", p=128)
    wt2_r = w2t_dram.rearrange("(n p) c -> n p c", p=128)
    x6_t = x6_out.rearrange("(n p) c -> n p c", p=128)
    QT = 4
    WCH = 2048
    swv = cx.sbuf("b2_swv", [128, 1], F32)
    mnv = cx.sbuf("b2_mnv", [128, 1], F32)
    emit_wtern_pipe(cx, cm, "b2w_", p, lambda i: w2_r[i], lambda i: wt2_r[i], NH, C, C * H, swv, mnv)

    xc2 = cx.sbuf("b2_xc2", [128, NH, QT * 128], BF16)
    obuf = cx.sbuf("b2_obuf", [128, QT, C], F32)
    fct = cx.sbuf("b2_fct", [128, QT], F32)
    ldh = cx.sbuf("b2_ldh", [128, H], F32)
    xqb = cx.sbuf("b2_xqb", [128, H], BF16)
    zb = [cx.sbuf(f"b2_zb{j}", [128, WCH], F32) for j in range(2)]
    mb = [cx.sbuf(f"b2_mb{j}", [128, WCH], F32) for j in range(2)]
    oib = [cx.sbuf(f"b2_oib{j}", [128, WCH], I32) for j in range(2)]
    st = cx.sbuf("b2_st", [128, 4], F32)
    psT = [cx.psum(f"b2_psT{j}", [128, 512], BF16) for j in range(2)]
    wblk = [cx.sbuf(f"b2_wblk{j}", [128, 512], BF16) for j in range(2)]
    psM = [cx.psum(f"b2_psM{t}", [128, 512], F32) for t in range(QT)]
    xs = cx.sbuf("b2_xs", [128, C], F32)
    stl = cx.sbuf("b2_stl", [128, 8], F32)
    ENG = ("vector", "vector")
    tok_evO = [None] * QT
    tok_ldh = p.last
    for qu in range(NT // QT):
        tok_trev = [None, None]
        act_last = None
        for t in range(QT):
            i = qu * QT + t
            tld = p.step("sync", lambda e, i=i: e.dma_start(out=ldh[:], in_=h_t[i]), [tok_ldh], dma=True)
            tv = p.step("vector", lambda e: e.reduce_max(st[:, 0:1], ldh[:], axis=AX.X, apply_absolute_value=True), [tld])
            tv = p.step("vector", lambda e: e.tensor_scalar(st[:, 0:1], st[:, 0:1], 1e-5, None, ALU.max), [tv])
            tv = p.step("vector", lambda e: e.reciprocal(st[:, 1:2], st[:, 0:1]), [tv])
            tv = p.step("vector", lambda e: e.tensor_scalar(st[:, 2:3], st[:, 1:2], 127.0, None, ALU.mult), [tv])
            ch_toks = []
            for hch in range(H // WCH):
                j = hch % 2
                sl2 = slice(hch * WCH, (hch + 1) * WCH)
                tt = emit_round_chain_on(p, ENG[j], ldh[:, sl2], st[:, 2:3], zb[j], mb[j], oib[j],
                                         [tv, ch_toks[-2] if len(ch_toks) >= 2 else None])
                tt = p.step(ENG[j], lambda e, j=j, sl2=sl2: e.tensor_scalar(xqb[:, sl2], mb[j][:], -128.0, None, ALU.add), [tt])
                ch_toks.append(tt)
            tq = p.step("vector", lambda e, t=t: e.tensor_scalar(fct[:, t:t+1], st[:, 0:1], 1.0 / 127.0, None, ALU.mult), ch_toks)
            tok_ldh = p.step("vector", lambda e, t=t: e.tensor_tensor(fct[:, t:t+1], fct[:, t:t+1], mnv[:], ALU.mult), [tq])
            evt = None
            for c4 in range(NH // 4):
                jj = c4 % 2
                ttoks = []
                for cc in range(4):
                    c = c4 * 4 + cc
                    d2 = [tok_ldh, tok_trev[jj]] if cc == 0 else []
                    ttoks.append(p.step("tensor", lambda e, c=c, cc=cc, jj=jj: e.transpose(psT[jj][:, ts(cc)], xqb[:, ts(c)], cm.idb[:]), d2))
                evt = p.step("scalar", lambda e, c4=c4, t=t, jj=jj: e.activation(
                    xc2[:, c4*4:(c4+1)*4, t*128:(t+1)*128],
                    psT[jj][:].rearrange("p (a b) -> p a b", a=4), AF.Copy), [ttoks[-1]])
                tok_trev[jj] = evt
            act_last = evt
            tok_ldh = act_last
        actbar = p.step("gpsimd", lambda e: e.engine_nop(), [tok_trev[0], tok_trev[1]])
        tok_wld = [None, None]
        tok_mmk = [None, None]
        for cck in range(C // 512):
            mm_last = [None] * QT
            for kt2 in range(NH):
                jw = kt2 % 2
                tok_wld[jw] = p.step("sync", lambda e, kt2=kt2, cck=cck, jw=jw: e.dma_start(
                    out=wblk[jw][:], in_=wt2_r[kt2][:, cck*512:(cck+1)*512]), [tok_mmk[jw], actbar], dma=True)
                for t in range(QT):
                    deps = []
                    if t == 0:
                        deps = [tok_wld[jw]]
                    if kt2 == 0:
                        deps = deps + [tok_evO[t], actbar]
                    tm = p.step("tensor", lambda e, kt2=kt2, t=t, jw=jw: e.matmul(
                        psM[t][:], xc2[:, kt2, t*128:(t+1)*128], wblk[jw][:],
                        start=(kt2 == 0), stop=(kt2 == NH - 1), skip_group_check=True), deps)
                    mm_last[t] = tm
                tok_mmk[jw] = mm_last[QT - 1]
            for t in range(QT):
                tok_evO[t] = p.step("vector", lambda e, t=t, cck=cck: e.tensor_scalar(
                    obuf[:, t, cck*512:(cck+1)*512], psM[t][:], fct[:, t:t+1], None, ALU.mult), [mm_last[t], tok_evO[t]])
        pl = p.last
        for t in range(QT):
            i = qu * QT + t
            p.last = tok_evO[t]
            p.s("sync", lambda e, i=i: e.dma_start(out=xs[:], in_=x_t[i]), dma=True)
            p.s("vector", lambda e, t=t: e.tensor_tensor(xs[:], xs[:], obuf[:, t, :], ALU.add))
            emit_ln_stats_tile(p, xs, obuf[:, t, :], stl)
            p.s("vector", lambda e: e.tensor_scalar(xs[:], xs[:], stl[:, 2:3], stl[:, 7:8], ALU.subtract, ALU.mult))
            tok_evO[t] = p.sdma(x6_t[i], xs[:])
    p.last = p.step("gpsimd", lambda e: e.engine_nop(), [tok_evO[0], tok_evO[1], tok_evO[2], tok_evO[3]])
    cx.pop_scope()


def build_fused(H=8192, I=5632):
    nc = bass.Bass("TRN2", target_bir_lowering=False, debug=False)
    ap = {}
    ins_spec = [
        ("x", [TT, C]), ("ident", [128, 128]), ("pv", [1, 2]), ("zmask", [1, NT]),
        ("ln1w", [128, C]), ("ln1b", [128, C]), ("ln2w", [128, C]), ("ln2b", [128, C]),
        ("tmk", [128, NC]), ("tmv", [128, NC]), ("tmr", [128, NC]),
        ("WkT", [C, C]), ("WvT", [C, C]), ("WrT", [C, C]), ("WoT", [C, C]),
        ("td8", [128, 8]), ("tf8", [128, 8]),
        ("cmk", [128, NC]), ("cmr", [128, NC]),
        ("WckT", [C, H]), ("WcvT", [H, C]), ("WcrT", [C, C]),
        ("gateT", [C, I]), ("upT", [C, I]), ("downT", [I, C]),
        ("gs", [128, I // 128]), ("us", [128, I // 128]), ("ds", [128, NC]),
        ("Wq1", [C, C]), ("Wk1", [C, C]), ("Wv1", [C, C]), ("Wo1", [C, C]),
        ("m1", [8, 128, 512]), ("m2", [8, 128, 512]), ("Sel", [32, NT]), ("cnt", [1, NT]),
        ("W1T", [C, H]), ("W2T", [H, C]),
    ]
    for n, sh in ins_spec:
        ap[n] = nc.dram_tensor(n, sh, F32, kind="ExternalInput").ap()
    out = nc.dram_tensor("out", [TT, C], F32, kind="ExternalOutput").ap()
    dr = {}
    for n, sh, dt in [
        ("xn", [TT, C], F32), ("halo1_in", [16, C], F32), ("halo1_out", [32, C], F32),
        ("k_c", [C, TT], F32), ("v_c", [C, TT], F32), ("sr_c", [C, TT], F32),
        ("kga", [2 * C, TT], F32), ("vga", [2 * C, TT], F32),
        ("y_my", [1024, 2 * TT], F32), ("yga", [C, 2 * TT], F32),
        ("x2", [TT, C], F32), ("xn2", [TT, C], F32),
        ("halo2_in", [16, C], F32), ("halo2_out", [32, C], F32),
        ("rkv_t", [TT, C], F32), ("x3", [TT, C], F32), ("ffo_t", [TT, C], F32),
        ("x4", [TT, C], F32), ("Q_c", [C, TT], F32),
        ("K_in", [C, TT], F32), ("V_in", [C, TT], F32),
        ("KG", [2 * C, TT], F32), ("VG", [2 * C, TT], F32),
        ("O_c", [C, TT], F32), ("x5pre", [TT, C], F32), ("x5", [TT, C], F32),
        ("h1", [TT, H], F32), ("w1t", [C, H], BF16), ("w2t", [H, C], BF16),
    ]:
        dr[n] = nc.dram_tensor("i_" + n, sh, dt).ap()

    cx = Ctx(nc)
    p = cx.p
    ccsem_cm = nc.semaphore("ccsem")
    ccsem = ccsem_cm.__enter__()
    ccount = [0]
    cm = Common(cx, ap["ident"])
    pvs = cx.sbuf("cm_pv", [1, 2], F32)
    zms = cx.sbuf("cm_zmask", [1, NT], F32)
    onesr_f = cx.sbuf("cm_onesrf", [1, 128], F32)
    pv128 = cx.sbuf("cm_pv128", [128, 2], F32)
    p.sdma(pvs[:], ap["pv"])
    p.sdma(zms[:], ap["zmask"])
    cx.push_scope()
    psPv = cx.psum("cm_psPv", [128, 2], F32)
    p.s("vector", lambda e: e.memset(onesr_f[:], 1.0))
    p.s("tensor", lambda e: e.matmul(psPv[:], onesr_f[:], pvs[:], start=True, stop=True))
    p.s("scalar", lambda e: e.activation(pv128[:], psPv[:], AF.Copy))
    cx.pop_scope()
    sv_halo = pvs[0:1, 0:1]
    sv_par = pv128[:, 1:2]

    p.stage = "ln1"
    emit_ln(cx, "ln1_", ap["x"], dr["xn"], ap["ln1w"], ap["ln1b"], halo_out=dr["halo1_in"])
    emit_cc(cx, "AllGather", dr["halo1_in"], dr["halo1_out"], ccsem, ccount)
    halo1 = dr["halo1_out"].rearrange("(two n) c -> two n c", two=2)
    p.stage = "mix_kvr"
    emit_mix_kvr(cx, cm, ap, dr["xn"], halo1, sv_halo, zms, dr["k_c"], dr["v_c"], dr["sr_c"])
    emit_cc(cx, "AllGather", dr["k_c"], dr["kga"], ccsem, ccount)
    emit_cc(cx, "AllGather", dr["v_c"], dr["vga"], ccsem, ccount)
    p.stage = "wkv"
    emit_wkv(cx, ap, dr["kga"], dr["vga"], dr["y_my"], sv_par)
    emit_cc(cx, "AllGather", dr["y_my"], dr["yga"], ccsem, ccount)
    p.stage = "wo1"
    emit_wo_res(cx, cm, ap["WoT"], dr["yga"], dr["sr_c"], ap["x"], dr["x2"],
                use_sr=True, att_is_gather=True, svec_sb=sv_par)
    p.stage = "ln2"
    emit_ln(cx, "ln2_", dr["x2"], dr["xn2"], ap["ln2w"], ap["ln2b"], halo_out=dr["halo2_in"])
    emit_cc(cx, "AllGather", dr["halo2_in"], dr["halo2_out"], ccsem, ccount)
    halo2 = dr["halo2_out"].rearrange("(two n) c -> two n c", two=2)
    p.stage = "cmix"
    emit_cmix(cx, cm, ap, dr["xn2"], halo2, sv_halo, zms, dr["rkv_t"], H=H)
    p.stage = "ln3"
    emit_ln(cx, "ln3_", ap["x"], dr["x3"], r_ts=[dr["x2"], dr["rkv_t"]])
    p.stage = "l0ffn"
    emit_l0ffn(cx, cm, ap, dr["x3"], dr["ffo_t"], I=I)
    p.stage = "ln4"
    emit_ln(cx, "ln4_", dr["x3"], dr["x4"], r_ts=[dr["ffo_t"]])
    p.stage = "proj3"
    emit_proj3(cx, cm, ap, dr["x4"], dr["Q_c"], dr["K_in"], dr["V_in"])
    emit_cc(cx, "AllGather", dr["K_in"], dr["KG"], ccsem, ccount)
    emit_cc(cx, "AllGather", dr["V_in"], dr["VG"], ccsem, ccount)
    p.stage = "attn"
    emit_attn(cx, cm, ap, dr["Q_c"], dr["KG"], dr["VG"], dr["O_c"])
    p.stage = "wo2"
    emit_wo_res(cx, cm, ap["Wo1"], dr["O_c"], None, dr["x4"], dr["x5pre"], use_sr=False, pfx="wo2_")
    p.stage = "ln5"
    emit_ln(cx, "ln5_", dr["x5pre"], dr["x5"])
    p.stage = "bitffn"
    emit_bitffn(cx, cm, ap, dr["x5"], dr["h1"], dr["w1t"], dr["w2t"], out, H=H)
    nc_ret = cx.close()
    ccsem_cm.__exit__(None, None, None)
    return nc_ret


class GemmPipe:

    def __init__(self, cx, pfx, NKT, TCH=512):
        self.pfx, self.NKT, self.TCH = pfx, NKT, TCH
        self.ws = [cx.sbuf(f"{pfx}ws{j}", [128, NKT, 128], F32) for j in range(2)]
        self.wsb = [cx.sbuf(f"{pfx}wsb{j}", [128, NKT, 128], BF16) for j in range(2)]
        self.psM = [cx.psum(f"{pfx}psM{j}", [128, TCH], F32) for j in range(2)]
        self.tok_conv = [None, None]
        self.tok_mm = [None, None]
        self.tok_evac = [None, None]

    def run(self, p, w_strip_ap, NDT, act_sb, act_cols, evac_fn, wrows=None, act_ready=None):
        nkt = wrows if wrows is not None else self.NKT
        last = None
        for dt in range(NDT):
            j = dt % 2
            tl = p.step("sync", lambda e, dt=dt, j=j, nkt=nkt: e.dma_start(out=self.ws[j][:, 0:nkt, :], in_=w_strip_ap(dt)),
                        [self.tok_conv[j]], dma=True, tag=self.pfx + "ld")
            self.tok_conv[j] = p.step("vector", lambda e, j=j, nkt=nkt: e.tensor_copy(self.wsb[j][:, 0:nkt, :], self.ws[j][:, 0:nkt, :]),
                                      [tl, self.tok_mm[j]], tag=self.pfx + "cv")
            t0 = None
            for kt in range(nkt):
                deps = [self.tok_conv[j], self.tok_evac[j], act_ready] if kt == 0 else []
                t0 = p.step("tensor", lambda e, kt=kt, j=j, act_cols=act_cols, nkt=nkt: e.matmul(
                    self.psM[j][:, 0:act_cols], self.wsb[j][:, kt, :], act_sb[:, kt, 0:act_cols],
                    start=(kt == 0), stop=(kt == nkt - 1), skip_group_check=True), deps, tag=self.pfx + "mm")
            self.tok_mm[j] = t0
            self.tok_evac[j] = evac_fn(dt, self.psM[j], [t0])
            last = self.tok_evac[j]
        p.last = last
        return last


class TransPipe:
    def __init__(self, cx, pfx):
        self.ldf = [cx.sbuf(f"{pfx}ldf{j}", [128, C], F32) for j in range(2)]
        self.bfa = [cx.sbuf(f"{pfx}bfa{j}", [128, C], BF16) for j in range(2)]
        self.psT = [cx.psum(f"{pfx}psT{j}", [128, 512], BF16) for j in range(2)]
        self.pfx = pfx
        self.tok_cv = [None, None]
        self.tok_tr = [None, None]
        self.tok_ev = [None, None]

    def run(self, p, cm, src_rows_fn, dst_sb, ntiles, dst_col0=0):
        last = None
        for i in range(ntiles):
            j = i % 2
            tl = src_rows_fn(i, self.ldf[j], [self.tok_cv[j]])
            self.tok_cv[j] = p.step("vector", lambda e, j=j: e.tensor_copy(self.bfa[j][:], self.ldf[j][:]),
                                    [tl, self.tok_tr[j]], tag=self.pfx + "cv")
            ev_tok = None
            for c4 in range(NC // 4):
                first_tr = None
                last_tr = None
                for cc in range(4):
                    c = c4 * 4 + cc
                    deps = [self.tok_cv[j], self.tok_ev[j]] if cc == 0 else []
                    t = p.step("tensor", lambda e, c=c, cc=cc, j=j: e.transpose(
                        self.psT[j][:, ts(cc)], self.bfa[j][:, ts(c)], cm.idb[:]), deps, tag=self.pfx + "tr")
                    if cc == 0:
                        first_tr = t
                    last_tr = t
                ev_tok = p.step("scalar", lambda e, c4=c4, i=i, j=j, dst_col0=dst_col0, dst_sb=dst_sb: e.activation(
                    dst_sb[:, c4*4:(c4+1)*4, dst_col0 + i*128: dst_col0 + (i+1)*128],
                    self.psT[j][:].rearrange("p (a b) -> p a b", a=4), AF.Copy), [last_tr], tag=self.pfx + "ev")
                self.tok_tr[j] = first_tr
                self.tok_ev[j] = ev_tok
            last = ev_tok
            p.last = last
        return last


import concourse.bass_utils as bass_utils


def shard_tokens(x):
    B, T, F = x.shape
    xb = x.reshape(B, 32, 128, F)
    out = []
    for p2 in range(B):
        for r in range(2):
            out.append(np.ascontiguousarray(xb[p2, r::2].reshape(2048, F)))
    return out


def unshard_tokens(shards):
    F = shards[0].shape[-1]
    out = np.empty((4, 32, 128, F), np.float32)
    for p2 in range(4):
        for r in range(2):
            out[p2, r::2] = np.asarray(shards[2 * p2 + r], dtype=np.float32).reshape(16, 128, F)
    return out.reshape(4, 4096, F)


def rep128(v):
    return np.ascontiguousarray(np.broadcast_to(np.asarray(v, np.float32)[None, :], (128, len(v))))


def cvec(v, n=16):
    return np.ascontiguousarray(np.asarray(v, np.float32).reshape(n, 128).T)


_PROGS = {}


def kernel(**ins):
    T_ = lambda a: np.ascontiguousarray(np.asarray(a, np.float32).T)
    A_ = lambda a: np.ascontiguousarray(np.asarray(a, np.float32))
    ident = np.eye(128, dtype=np.float32)
    x_s = shard_tokens(np.asarray(ins["L0_attn_x"], np.float32))
    tri = (np.arange(128)[:, None] <= np.arange(128)[None, :]).astype(np.float32)

    def masks(rr):
        m1 = np.zeros((8, 128, 512), np.float32)
        m2 = np.zeros((8, 128, 512), np.float32)
        for mi in range(8):
            for t in range(4):
                g2 = 2 * t + rr
                if mi < g2:
                    m1[mi, :, t*128:(t+1)*128] = 1.0
                    m2[mi, :, t*128:(t+1)*128] = 1.0
                elif mi == g2:
                    m1[mi, :, t*128:(t+1)*128] = tri
                    m2[mi, :, t*128:(t+1)*128] = 1.0
        return m1, m2

    def sel_cnt(rr):
        Sel = np.zeros((32, 16), np.float32)
        cnt = np.zeros((1, 16), np.float32)
        for qj in range(16):
            g = 2 * qj + rr
            Sel[g+1:, qj] = 1.0
            cnt[0, qj] = 4096 - (g + 1) * 128
        return Sel, cnt

    in_maps = []
    for i in range(8):
        r = i % 2
        m1, m2 = masks(r)
        Sel, cnt = sel_cnt(r)
        zmask = np.ones((1, 16), np.float32)
        if r == 0:
            zmask[0, 0] = 0.0
        sl = slice(r * 1024, (r + 1) * 1024)
        in_maps.append({
            "x": x_s[i], "ident": ident, "pv": np.array([[1.0 - r, float(r)]], np.float32), "zmask": zmask,
            "ln1w": rep128(ins["L0_attn_ln1_w"]), "ln1b": rep128(ins["L0_attn_ln1_b"]),
            "ln2w": rep128(ins["L0_attn_ln2_w"]), "ln2b": rep128(ins["L0_attn_ln2_b"]),
            "tmk": cvec(ins["L0_attn_tmk"]), "tmv": cvec(ins["L0_attn_tmv"]), "tmr": cvec(ins["L0_attn_tmr"]),
            "WkT": T_(ins["L0_attn_Wk"]), "WvT": T_(ins["L0_attn_Wv"]), "WrT": T_(ins["L0_attn_Wr"]),
            "WoT": T_(ins["L0_attn_Wo"]),
            "td8": cvec(ins["L0_attn_time_decay"][sl], 8), "tf8": cvec(ins["L0_attn_time_first"][sl], 8),
            "cmk": cvec(ins["L0_attn_cmk"]), "cmr": cvec(ins["L0_attn_cmr"]),
            "WckT": T_(ins["L0_attn_Wck"]), "WcvT": T_(ins["L0_attn_Wcv"]), "WcrT": T_(ins["L0_attn_Wcr"]),
            "gateT": T_(ins["L0_ffn_gate_w"]), "upT": T_(ins["L0_ffn_up_w"]), "downT": T_(ins["L0_ffn_down_w"]),
            "gs": cvec(ins["L0_ffn_gate_s"], 44), "us": cvec(ins["L0_ffn_up_s"], 44), "ds": cvec(ins["L0_ffn_down_s"], 16),
            "Wq1": A_(ins["L1_attn_Wq"]), "Wk1": A_(ins["L1_attn_Wk"]), "Wv1": A_(ins["L1_attn_Wv"]),
            "Wo1": A_(ins["L1_attn_Wo"]),
            "m1": m1, "m2": m2, "Sel": Sel, "cnt": cnt,
            "W1T": T_(ins["L1_ffn_W1"]), "W2T": T_(ins["L1_ffn_W2"]),
        })
    if "nc" not in _PROGS:
        _PROGS["nc"] = build_fused()
    res = bass_utils.run_bass_kernel_spmd(_PROGS["nc"], in_maps, core_ids=list(range(8)))
    return unshard_tokens([res.results[i]["out"] for i in range(8)])


# version 11
# speedup vs baseline: 2.0400x; 1.0566x over previous
import numpy as np
import concourse.bass as bass
import concourse.mybir as mybir

F32 = mybir.dt.float32
BF16 = mybir.dt.bfloat16
I32 = mybir.dt.int32
ALU = mybir.AluOpType
AF = mybir.ActivationFunctionType
AX = mybir.AxisListType

ENGINES = ("sync", "vector", "scalar", "tensor", "gpsimd")
C = 2048
TT = 2048
NT = TT // 128
NC = C // 128


def ts(i, n=128):
    return slice(i * n, (i + 1) * n)


class Prog:
    def __init__(self, nc):
        self.nc = nc
        self.steps = {e: [] for e in ENGINES}
        self.count = {e: 0 for e in ENGINES}
        self.last = None
        self.stage = "init"

    def step(self, engine, fn, deps=(), dma=False, tag="", raw=False):
        inc = 0 if raw else (16 if dma else 1)
        waits = [d for d in deps if d is not None]
        self.count[engine] += inc
        tok = (engine, self.count[engine])
        self.steps[engine].append((waits, fn, inc, self.stage))
        self.last = tok
        return tok

    def s(self, engine, fn, extra=(), dma=False, tag="", raw=False):
        return self.step(engine, fn, [self.last] + list(extra), dma=dma, tag=tag, raw=raw)

    def sdma(self, out, in_, engine="sync", extra=(), tag=""):
        return self.s(engine, lambda e: e.dma_start(out=out, in_=in_), extra, dma=True, tag=tag)

    def build(self):
        nc = self.nc
        sem_cms = {e: nc.semaphore("psem_" + e) for e in ENGINES}
        self.sems = {e: cm.__enter__() for e, cm in sem_cms.items()}
        block_cm = nc.Block()
        block = block_cm.__enter__()
        sems = self.sems

        def mk(engine):
            steps = self.steps[engine]

            def body(eng):
                waited = {}
                cur_stage = None
                cur_sid = None
                for waits, fn, inc, stage in steps:
                    if stage != cur_stage:
                        if cur_stage is not None:
                            nc.leave_named_scope(cur_stage, cur_sid, False)
                        cur_sid, _ = nc.enter_named_scope(stage, False)
                        cur_stage = stage
                    for (de, dc) in waits:
                        if waited.get(de, 0) >= dc:
                            continue
                        eng.wait_ge(sems[de], dc)
                        waited[de] = max(waited.get(de, 0), dc)
                    inst = fn(eng)
                    if inc:
                        inst.then_inc(sems[engine], inc)
                if cur_stage is not None:
                    nc.leave_named_scope(cur_stage, cur_sid, False)
            return body

        for e in ENGINES:
            if self.steps[e]:
                getattr(block, e)(mk(e))
        block_cm.__exit__(None, None, None)
        for cm in sem_cms.values():
            cm.__exit__(None, None, None)


class Ctx:
    def __init__(self, nc):
        self.nc = nc
        self._cms = []
        self._scopes = []
        self.p = Prog(nc)

    def sbuf(self, name, shape, dtype):
        cm = self.nc.sbuf_tensor(name, shape, dtype)
        t = cm.__enter__()
        (self._scopes[-1] if self._scopes else self._cms).append(cm)
        return t

    def psum(self, name, shape, dtype):
        cm = self.nc.psum_tensor(name, shape, dtype)
        t = cm.__enter__()
        (self._scopes[-1] if self._scopes else self._cms).append(cm)
        return t

    def push_scope(self):
        self._scopes.append([])

    def pop_scope(self):
        for cm in reversed(self._scopes.pop()):
            cm.__exit__(None, None, None)

    def close(self):
        self.p.s("gpsimd", lambda e: e.engine_nop(), tag="final")
        self.p.build()
        for cm in reversed(self._cms):
            cm.__exit__(None, None, None)
        return self.nc


class Common:
    def __init__(self, cx, ident_ap):
        p = cx.p
        self.identf = cx.sbuf("cm_identf", [128, 128], F32)
        self.idb = cx.sbuf("cm_idb", [128, 128], BF16)
        p.sdma(self.identf[:], ident_ap)
        p.s("vector", lambda e: e.tensor_copy(self.idb[:], self.identf[:]))


def emit_ln_stats_tile(p, xs, sq, st, Cw=C):
    p.s("vector", lambda e: e.reduce_sum(st[:, 0:1], xs[:], axis=AX.X))
    p.s("scalar", lambda e: e.activation(sq[:], xs[:], AF.Square, accum_out=st[:, 1:2]))
    p.s("vector", lambda e: e.tensor_scalar(st[:, 2:3], st[:, 0:1], 1.0 / Cw, None, ALU.mult))
    p.s("vector", lambda e: e.tensor_scalar(st[:, 3:4], st[:, 1:2], 1.0 / Cw, None, ALU.mult))
    p.s("vector", lambda e: e.tensor_tensor(st[:, 4:5], st[:, 2:3], st[:, 2:3], ALU.mult))
    p.s("vector", lambda e: e.tensor_tensor(st[:, 5:6], st[:, 3:4], st[:, 4:5], ALU.subtract))
    p.s("vector", lambda e: e.tensor_scalar(st[:, 5:6], st[:, 5:6], 1e-5, None, ALU.add))
    p.s("scalar", lambda e: e.activation(st[:, 6:7], st[:, 5:6], AF.Ln))
    p.s("scalar", lambda e: e.activation(st[:, 7:8], st[:, 6:7], AF.Exp, scale=-0.5))


def emit_ln(cx, pfx, x_t, o_t, w_ap=None, b_ap=None, r_ts=(), halo_out=None):
    p = cx.p
    cx.push_scope()
    xt = x_t.rearrange("(n p) c -> n p c", p=128)
    rts = [r.rearrange("(n p) c -> n p c", p=128) for r in r_ts]
    ot = o_t.rearrange("(n p) c -> n p c", p=128)
    xs = cx.sbuf(pfx + "xs", [128, C], F32)
    rsb = [cx.sbuf(pfx + f"rs{j}", [128, C], F32) for j in range(len(r_ts))]
    osb = cx.sbuf(pfx + "osb", [128, C], F32)
    sq = cx.sbuf(pfx + "sq", [128, C], F32)
    st = cx.sbuf(pfx + "st", [128, 8], F32)
    if w_ap is not None:
        wt = cx.sbuf(pfx + "wt", [128, C], F32)
        bt = cx.sbuf(pfx + "bt", [128, C], F32)
        p.sdma(wt[:], w_ap)
        p.sdma(bt[:], b_ap)
    for i in range(NT):
        p.sdma(xs[:], xt[i])
        for ri in range(len(r_ts)):
            p.sdma(rsb[ri][:], rts[ri][i])
            p.s("vector", lambda e, ri=ri: e.tensor_tensor(xs[:], xs[:], rsb[ri][:], ALU.add))
        emit_ln_stats_tile(p, xs, sq, st)
        p.s("vector", lambda e: e.tensor_scalar(osb[:], xs[:], st[:, 2:3], st[:, 7:8], ALU.subtract, ALU.mult))
        if w_ap is not None:
            p.s("vector", lambda e: e.tensor_tensor(osb[:], osb[:], wt[:], ALU.mult))
            p.s("vector", lambda e: e.tensor_tensor(osb[:], osb[:], bt[:], ALU.add))
        p.sdma(ot[i], osb[:])
        if halo_out is not None:
            p.sdma(halo_out[i:i+1, :], osb[127:128, :])
    cx.pop_scope()


def emit_cc(cx, kind, in_ap, out_ap, ccsem, ccount, alu=None, groups8=False):
    p = cx.p
    op = alu if alu is not None else ALU.bypass
    groups = [[0, 1, 2, 3, 4, 5, 6, 7]] if groups8 else [[0, 1], [2, 3], [4, 5], [6, 7]]

    def fn(e):
        return e.collective_compute(kind, op, replica_groups=groups,
                                    ins=[in_ap.opt()], outs=[out_ap.opt()]).then_inc(ccsem, 1)
    p.s("gpsimd", fn, tag="cc", raw=True)
    ccount[0] += 1
    need = ccount[0]

    def fn2(e):
        e.wait_ge(ccsem, need)
        return e.engine_nop()
    p.s("gpsimd", fn2, tag="ccwait")


def emit_shift_tile(cx, pfx_bufs, xsrc_dram, halo_ap, i, dst_sb, svec, zmask, hrow):
    p = cx.p
    p.sdma(dst_sb[1:128, :], xsrc_dram[i * 128: i * 128 + 127, :])
    jm1 = max(i - 1, 0)
    p.sdma(hrow[0:1, 0:C], halo_ap[0:1, i, :])
    p.sdma(hrow[0:1, C:2*C], halo_ap[1:2, jm1, :])
    p.s("vector", lambda e: e.tensor_tensor(hrow[0:1, C:2*C], hrow[0:1, C:2*C], hrow[0:1, 0:C], ALU.subtract))
    p.s("vector", lambda e: e.scalar_tensor_tensor(dst_sb[0:1, :], hrow[0:1, C:2*C], svec[0:1, 0:1], hrow[0:1, 0:C], ALU.mult, ALU.add))
    p.s("vector", lambda e, i=i: e.tensor_scalar(dst_sb[0:1, :], dst_sb[0:1, :], zmask[0:1, i:i+1], None, ALU.mult))


def emit_mix_kvr(cx, cm, aps, xn_dram, halo_ap, svec, zmask, k_out, v_out, sr_out):
    p = cx.p
    cx.push_scope()
    TCH = 512
    NCH = TT // TCH
    xn_t = xn_dram.rearrange("(n p) c -> n p c", p=128)
    hrow = cx.sbuf("mk_hrow", [1, 2 * C], F32)
    xn_c = cx.sbuf("mk_xn_c", [128, NC, TCH], BF16)
    xx_c = cx.sbuf("mk_xx_c", [128, NC, TCH], BF16)
    mixed = cx.sbuf("mk_mixed", [128, NC, TCH], BF16)
    tmv_s = cx.sbuf("mk_tmv", [128, 3, NC], F32)
    ev = [cx.sbuf(f"mk_ev{j}", [128, TCH], F32) for j in range(2)]

    for wi, tm in enumerate(["tmk", "tmv", "tmr"]):
        p.sdma(tmv_s[:, wi, :], aps[tm])
    tpA = TransPipe(cx, "mkT_")
    gp = GemmPipe(cx, "mkG_", NC)
    prev_ev = [None, None]

    def srcfn_xn(base):
        def f(i, buf, deps):
            return p.step("sync", lambda e, i=i, buf=buf: e.dma_start(out=buf[:], in_=xn_t[base + i]),
                          deps + [p.last], dma=True)
        return f

    def srcfn_xx(base):
        def f(i, buf, deps):
            ii = base + i
            t1 = p.step("sync", lambda e, ii=ii, buf=buf: e.dma_start(out=buf[1:128, :], in_=xn_dram[ii*128: ii*128+127, :]), deps + [p.last], dma=True)
            jm1 = max(ii - 1, 0)
            t2 = p.step("sync", lambda e, ii=ii: e.dma_start(out=hrow[0:1, 0:C], in_=halo_ap[0:1, ii, :]), [t1], dma=True)
            t3 = p.step("sync", lambda e, jm1=jm1: e.dma_start(out=hrow[0:1, C:2*C], in_=halo_ap[1:2, jm1, :]), [t2], dma=True)
            t4 = p.step("vector", lambda e: e.tensor_tensor(hrow[0:1, C:2*C], hrow[0:1, C:2*C], hrow[0:1, 0:C], ALU.subtract), [t3])
            t5 = p.step("vector", lambda e, buf=buf: e.scalar_tensor_tensor(buf[0:1, :], hrow[0:1, C:2*C], svec, hrow[0:1, 0:C], ALU.mult, ALU.add), [t4])
            t6 = p.step("vector", lambda e, ii=ii, buf=buf: e.tensor_scalar(buf[0:1, :], buf[0:1, :], zmask[0:1, ii:ii+1], None, ALU.mult), [t5])
            p.last = t6
            return t6
        return f

    for ch in range(NCH):
        base = ch * (TCH // 128)
        tpA.run(p, cm, srcfn_xn(base), xn_c, TCH // 128)
        tpA.run(p, cm, srcfn_xx(base), xx_c, TCH // 128)
        for wi, (wn, outd, sig) in enumerate([("WkT", k_out, False), ("WvT", v_out, False), ("WrT", sr_out, True)]):
            for c in range(NC):
                p.s("vector", lambda e, c=c: e.tensor_tensor(mixed[:, c, :], xn_c[:, c, :], xx_c[:, c, :], ALU.subtract))
                p.s("vector", lambda e, c=c, wi=wi: e.scalar_tensor_tensor(mixed[:, c, :], mixed[:, c, :], tmv_s[:, wi, c:c+1], xx_c[:, c, :], ALU.mult, ALU.add))
            act_tok = p.last
            outr = outd.rearrange("(n p) t -> n p t", p=128)

            def evac_fn(dt, ps, deps, sig=sig, outr=outr, ch=ch):
                j = dt % 2
                fn = AF.Sigmoid if sig else AF.Copy
                t1 = p.step("scalar", lambda e, j=j, ps=ps, fn=fn: e.activation(ev[j][:], ps[:], fn), deps + [prev_ev[j]])
                t2 = p.step("gpsimd", lambda e, dt=dt, j=j: e.dma_start(out=outr[dt, :, ch*TCH:(ch+1)*TCH], in_=ev[j][:]), [t1], dma=True)
                prev_ev[j] = t2
                return t2
            gp.run(p, lambda dt, wn=wn: aps[wn].rearrange("(k p) d -> p k d", p=128)[:, :, ts(dt)],
                   NC, mixed, TCH, evac_fn, act_ready=act_tok)
    cx.pop_scope()


def emit_wkv(cx, aps, kga, vga, y_out, svec_sb):
    p = cx.p
    cx.push_scope()
    TF = 4096
    NCT = 8
    kt = kga.rearrange("(two half ct p) (j i) -> half ct p j two i", two=2, half=2, p=128, i=128)
    vt = vga.rearrange("(two half ct p) (j i) -> half ct p j two i", two=2, half=2, p=128, i=128)
    yo = y_out.rearrange("(ct p) t -> ct p t", p=128)
    ks = cx.sbuf("wk_ks", [128, TF], F32)
    vs = cx.sbuf("wk_vs", [128, TF], F32)
    kb2 = cx.sbuf("wk_kb2", [128, TF], F32)
    es = cx.sbuf("wk_es", [128, TF], F32)
    zs = cx.sbuf("wk_zs", [128, TF], F32)
    rz = cx.sbuf("wk_rz", [128, TF + 1], F32)
    rr = cx.sbuf("wk_rr", [128, TF + 1], F32)
    num = cx.sbuf("wk_num", [128, TF], F32)
    den = cx.sbuf("wk_den", [128, TF], F32)
    tdb = cx.sbuf("wk_tdb", [128, NCT], F32)
    tfb = cx.sbuf("wk_tfb", [128, NCT], F32)
    lam = cx.sbuf("wk_lam", [128, NCT], F32)
    eu = cx.sbuf("wk_eu", [128, NCT], F32)
    sc = cx.sbuf("wk_sc", [128, 2], F32)

    p.sdma(tdb[:], aps["td8"])
    p.sdma(tfb[:], aps["tf8"])
    p.s("scalar", lambda e: e.activation(lam[:], tdb[:], AF.Exp))
    p.s("scalar", lambda e: e.activation(lam[:], lam[:], AF.Exp, scale=-1.0))
    p.s("scalar", lambda e: e.activation(eu[:], tfb[:], AF.Exp))
    for c in range(NCT):
        for src_t, dst in [(kt, ks), (vt, vs)]:
            p.sdma(kb2.ap().rearrange("p (j two i) -> p j two i", two=2, i=128), src_t[0, c])
            p.sdma(dst.ap().rearrange("p (j two i) -> p j two i", two=2, i=128), src_t[1, c])
            p.s("vector", lambda e, dst=dst: e.tensor_tensor(dst[:], dst[:], kb2[:], ALU.subtract))
            p.s("vector", lambda e, dst=dst: e.scalar_tensor_tensor(dst[:], dst[:], svec_sb, kb2[:], ALU.mult, ALU.add))
        p.s("vector", lambda e: e.reduce_max(sc[:, 0:1], ks[:], axis=AX.X))
        p.s("vector", lambda e: e.tensor_scalar(sc[:, 1:2], sc[:, 0:1], -1.0, None, ALU.mult))
        p.s("scalar", lambda e: e.activation(es[:], ks[:], AF.Exp, bias=sc[:, 1:2]))
        p.s("vector", lambda e: e.tensor_tensor(zs[:], es[:], vs[:], ALU.mult))
        p.s("vector", lambda e: e.memset(rz[:, 0:1], 0.0))
        p.s("vector", lambda e: e.memset(rr[:, 0:1], 0.0))
        p.s("vector", lambda e, c=c: e.tensor_tensor_scan(
            rz[:, 1:TF + 1], lam[:, c:c + 1].broadcast_to([128, TF]), zs[:], 0.0, ALU.mult, ALU.add))
        p.s("vector", lambda e, c=c: e.tensor_tensor_scan(
            rr[:, 1:TF + 1], lam[:, c:c + 1].broadcast_to([128, TF]), es[:], 0.0, ALU.mult, ALU.add))
        p.s("vector", lambda e, c=c: e.scalar_tensor_tensor(num[:], zs[:], eu[:, c:c + 1], rz[:, 0:TF], ALU.mult, ALU.add))
        p.s("vector", lambda e, c=c: e.scalar_tensor_tensor(den[:], es[:], eu[:, c:c + 1], rr[:, 0:TF], ALU.mult, ALU.add))
        p.s("scalar", lambda e: e.activation(den[:], den[:], AF.Ln))
        p.s("scalar", lambda e: e.activation(den[:], den[:], AF.Exp, scale=-1.0))
        p.s("vector", lambda e: e.tensor_tensor(num[:], num[:], den[:], ALU.mult))
        p.sdma(yo[c], num[:])
    cx.pop_scope()


def emit_wo_res(cx, cm, wo_ap, att_ap, sr_ap, x_ap, x2_ap, use_sr=True, D=C,
                att_is_gather=False, svec_sb=None, pfx="wo_"):
    p = cx.p
    cx.push_scope()
    if att_is_gather:
        att_g = att_ap.rearrange("(n p) (j two i) -> two p n j i", p=128, two=2, i=128)
    else:
        att_r = att_ap.rearrange("(n p) t -> p n t", p=128)
    sr_r = sr_ap.rearrange("(n p) t -> p n t", p=128) if use_sr else None
    xt = x_ap.rearrange("(n p) c -> n p c", p=128)
    x2t = x2_ap.rearrange("(n p) c -> n p c", p=128)
    wof = cx.sbuf(pfx + "wof", [128, D], F32)
    wob = cx.sbuf(pfx + "wob", [128, NC, D], BF16)
    av = [cx.sbuf(pfx + f"av{j}", [128, NC, 128], F32) for j in range(2)]
    av2 = [cx.sbuf(pfx + f"av2{j}", [128, NC, 128], F32) for j in range(2)]
    sv = [cx.sbuf(pfx + f"sv{j}", [128, NC, 128], F32) for j in range(2)]
    sab = [cx.sbuf(pfx + f"sab{j}", [128, NC, 128], BF16) for j in range(2)]
    xs = [cx.sbuf(pfx + f"xs{j}", [128, D], F32) for j in range(2)]
    ev = [cx.sbuf(pfx + f"ev{j}", [128, D], F32) for j in range(2)]
    ps = [cx.psum(pfx + f"ps{j}", [128, D], F32) for j in range(2)]

    t_w = p.last
    for kt2 in range(NC):
        t_w = p.step("sync", lambda e, kt2=kt2: e.dma_start(out=wof[:] if False else wof[:], in_=wo_ap.rearrange("(n p) d -> n p d", p=128)[kt2]), [t_w], dma=True) if False else p.sdma(wof[:], wo_ap.rearrange("(n p) d -> n p d", p=128)[kt2])
        t_w = p.s("vector", lambda e, kt2=kt2: e.tensor_copy(wob[:, kt2, :], wof[:]))
    root = p.last
    tok_mul = [root, root]
    tok_mm = [None, None]
    tok_add = [None, None]
    tok_st = [None, None]
    for i in range(NT):
        j = i % 2
        if att_is_gather:
            tl1 = p.step("sync", lambda e, i=i, j=j: e.dma_start(out=av2[j][:], in_=att_g[0, :, :, i, :]), [tok_mul[j]], dma=True)
            tl2 = p.step("sync", lambda e, i=i, j=j: e.dma_start(out=av[j][:], in_=att_g[1, :, :, i, :]), [tok_mul[j]], dma=True)
            tm = p.step("vector", lambda e, j=j: e.tensor_tensor(av[j][:], av[j][:], av2[j][:], ALU.subtract), [tl1, tl2])
            tm = p.step("vector", lambda e, j=j: e.scalar_tensor_tensor(av[j][:], av[j][:], svec_sb, av2[j][:], ALU.mult, ALU.add), [tm])
        else:
            tm = p.step("sync", lambda e, i=i, j=j: e.dma_start(out=av[j][:], in_=att_r[:, :, ts(i)]), [tok_mul[j]], dma=True)
        if use_sr:
            tsv = p.step("sync", lambda e, i=i, j=j: e.dma_start(out=sv[j][:], in_=sr_r[:, :, ts(i)]), [tok_mul[j]], dma=True)
            tok_mul[j] = p.step("vector", lambda e, j=j: e.tensor_tensor(sab[j][:], av[j][:], sv[j][:], ALU.mult), [tm, tsv, tok_mm[j]])
        else:
            tok_mul[j] = p.step("vector", lambda e, j=j: e.tensor_copy(sab[j][:], av[j][:]), [tm, tok_mm[j]])
        t0 = None
        for kt2 in range(NC):
            for dh in range(D // 512):
                deps = [tok_mul[j], tok_add[j]] if (kt2 == 0 and dh == 0) else []
                t0 = p.step("tensor", lambda e, kt2=kt2, dh=dh, j=j: e.matmul(
                    ps[j][:, dh*512:(dh+1)*512], sab[j][:, kt2, :], wob[:, kt2, dh*512:(dh+1)*512],
                    start=(kt2 == 0), stop=(kt2 == NC - 1), skip_group_check=True), deps)
        tok_mm[j] = t0
        tx = p.step("sync", lambda e, i=i, j=j: e.dma_start(out=xs[j][:], in_=xt[i]), [tok_add[j]], dma=True)
        tok_add[j] = p.step("vector", lambda e, j=j: e.tensor_tensor(ev[j][:], ps[j][:], xs[j][:], ALU.add), [t0, tx, tok_st[j]])
        tok_st[j] = p.step("sync", lambda e, i=i, j=j: e.dma_start(out=x2t[i], in_=ev[j][:]), [tok_add[j]], dma=True)
    p.last = p.step("gpsimd", lambda e: e.engine_nop(), [tok_st[0], tok_st[1]])
    cx.pop_scope()


def emit_cmix(cx, cm, aps, xn2_dram, halo_ap, svec, zmask, rkvt_out, H=8192):
    p = cx.p
    cx.push_scope()
    TCH = 512
    NCH = TT // TCH
    NH = H // 128
    NSLAB = 4
    SL = NH // NSLAB
    hrow = cx.sbuf("cmx_hrow", [1, 2 * C], F32)
    A = cx.sbuf("cmx_A", [128, NC, TCH], BF16)
    Bb = cx.sbuf("cmx_B", [128, NC, TCH], BF16)
    ck = cx.sbuf("cmx_ck", [128, NC, TCH], BF16)
    cr = cx.sbuf("cmx_cr", [128, NC, TCH], BF16)
    cmks = cx.sbuf("cmx_cmks", [128, 2, NC], F32)
    hslab = cx.sbuf("cmx_hslab", [128, SL, TCH], BF16)
    kvacc = cx.sbuf("cmx_kvacc", [128, NC, TCH], F32)
    evh = [cx.sbuf(f"cmx_evh{j}", [128, TCH], F32) for j in range(2)]
    sg = [cx.sbuf(f"cmx_sg{j}", [128, TCH], F32) for j in range(2)]
    evr = [cx.sbuf(f"cmx_evr{j}", [128, TCH], F32) for j in range(2)]
    evT = [cx.sbuf(f"cmx_evT{j}", [128, TCH], F32) for j in range(2)]
    psTT = [cx.psum(f"cmx_psTT{j}", [128, TCH], F32) for j in range(2)]
    rk_t = rkvt_out.rearrange("(n p) c -> n p c", p=128)
    tp = TransPipe(cx, "cmxT_")
    gp = GemmPipe(cx, "cmxG_", NC)
    prev_h = [None, None]
    prev_kv = [None] * NC
    prev_sg = [None, None]
    prev_T = [None, None]
    prev_d = [None, None]

    p.sdma(cmks[:, 0, :], aps["cmk"])
    p.sdma(cmks[:, 1, :], aps["cmr"])

    def srcfn_a(base):
        def f(i, buf, deps):
            return p.step("sync", lambda e, i=i, buf=buf: e.dma_start(out=buf[:], in_=xn2_dram[(base+i)*128:(base+i+1)*128, :]),
                          deps + [p.last], dma=True)
        return f

    def srcfn_b(base):
        def f(i, buf, deps):
            ii = base + i
            t1 = p.step("sync", lambda e, ii=ii, buf=buf: e.dma_start(out=buf[1:128, :], in_=xn2_dram[ii*128: ii*128+127, :]), deps + [p.last], dma=True)
            jm1 = max(ii - 1, 0)
            t2 = p.step("sync", lambda e, ii=ii: e.dma_start(out=hrow[0:1, 0:C], in_=halo_ap[0:1, ii, :]), [t1], dma=True)
            t3 = p.step("sync", lambda e, jm1=jm1: e.dma_start(out=hrow[0:1, C:2*C], in_=halo_ap[1:2, jm1, :]), [t2], dma=True)
            t4 = p.step("vector", lambda e: e.tensor_tensor(hrow[0:1, C:2*C], hrow[0:1, C:2*C], hrow[0:1, 0:C], ALU.subtract), [t3])
            t5 = p.step("vector", lambda e, buf=buf: e.scalar_tensor_tensor(buf[0:1, :], hrow[0:1, C:2*C], svec, hrow[0:1, 0:C], ALU.mult, ALU.add), [t4])
            t6 = p.step("vector", lambda e, ii=ii, buf=buf: e.tensor_scalar(buf[0:1, :], buf[0:1, :], zmask[0:1, ii:ii+1], None, ALU.mult), [t5])
            p.last = t6
            return t6
        return f

    for ch in range(NCH):
        base = ch * (TCH // 128)
        tp.run(p, cm, srcfn_a(base), A, TCH // 128)
        tp.run(p, cm, srcfn_b(base), Bb, TCH // 128)
        for c in range(NC):
            p.s("vector", lambda e, c=c: e.tensor_tensor(cr[:, c, :], A[:, c, :], Bb[:, c, :], ALU.subtract))
            p.s("vector", lambda e, c=c: e.scalar_tensor_tensor(ck[:, c, :], cr[:, c, :], cmks[:, 0, c:c+1], Bb[:, c, :], ALU.mult, ALU.add))
            p.s("vector", lambda e, c=c: e.scalar_tensor_tensor(cr[:, c, :], cr[:, c, :], cmks[:, 1, c:c+1], Bb[:, c, :], ALU.mult, ALU.add))
        p.s("vector", lambda e: e.memset(kvacc[:], 0.0))
        act_tok = p.last
        for c in range(NC):
            prev_kv[c] = act_tok
        for sl in range(NSLAB):
            def evac_h(hj, ps, deps):
                j = hj % 2
                t1 = p.step("vector", lambda e, j=j, ps=ps: e.tensor_scalar_max(evh[j][:], ps[:], 0.0), deps + [prev_h[j]])
                t2 = p.step("scalar", lambda e, hj=hj, j=j: e.activation(hslab[:, hj, :], evh[j][:], AF.Square), [t1])
                prev_h[j] = t2
                return t2
            gp.run(p, lambda hj, sl=sl: aps["WckT"].rearrange("(n p) h -> p n h", p=128)[:, :, ts(sl * SL + hj)],
                   SL, ck, TCH, evac_h, act_ready=act_tok)
            hready = p.last
            def evac_kv(ct, ps, deps):
                t1 = p.step("vector", lambda e, ct=ct, ps=ps: e.tensor_tensor(kvacc[:, ct, :], kvacc[:, ct, :], ps[:], ALU.add),
                            deps + [prev_kv[ct]])
                prev_kv[ct] = t1
                return t1
            gp.run(p, lambda ct, sl=sl: aps["WcvT"].rearrange("(n p) c -> p n c", p=128)[:, sl*SL:(sl+1)*SL, ts(ct)],
                   NC, hslab, TCH, evac_kv, wrows=SL, act_ready=hready)
        def evac_r(ct, ps, deps, ch=ch):
            j = ct % 2
            t1 = p.step("scalar", lambda e, j=j, ps=ps: e.activation(sg[j][:], ps[:], AF.Sigmoid), deps + [prev_sg[j]])
            t2 = p.step("vector", lambda e, ct=ct, j=j: e.tensor_tensor(evr[j][:], sg[j][:], kvacc[:, ct, :], ALU.mult), [t1, prev_kv[ct], prev_d[j]])
            ttoks = []
            for tt in range(4):
                d2 = [t2, prev_T[j]] if tt == 0 else []
                ttoks.append(p.step("tensor", lambda e, tt=tt, j=j: e.transpose(psTT[j][:, ts(tt)], evr[j][:, ts(tt)], cm.identf[:]), d2))
            t3 = p.step("scalar", lambda e, j=j: e.activation(evT[j][:], psTT[j][:], AF.Copy), [ttoks[-1]])
            prev_T[j] = t3
            prev_sg[j] = t2
            dtoks = []
            for tt in range(4):
                i = ch * 4 + tt
                dtoks.append(p.step("gpsimd", lambda e, i=i, tt=tt, j=j, ct=ct: e.dma_start(out=rk_t[i][:, ts(ct)], in_=evT[j][:, ts(tt)]), [t3], dma=True))
            prev_d[j] = dtoks[-1]
            p.last = dtoks[-1]
            return dtoks[-1]
        gp.run(p, lambda ct: aps["WcrT"].rearrange("(n p) d -> p n d", p=128)[:, :, ts(ct)],
               NC, cr, TCH, evac_r, act_ready=p.last)
    cx.pop_scope()


def emit_l0ffn(cx, cm, aps, x_dram, ffot_out, I=5632):
    p = cx.p
    cx.push_scope()
    NI = I // 128
    TCH = 512
    NCH = TT // TCH
    x_t = x_dram.rearrange("(n p) c -> n p c", p=128)
    xc = cx.sbuf("ff_xc", [128, NC, TCH], BF16)
    hdn = cx.sbuf("ff_hdn", [128, NI, TCH], BF16)
    scl = cx.sbuf("ff_scl", [128, 3, max(NI, NC)], F32)
    evg = [cx.sbuf(f"ff_evg{j}", [128, TCH], F32) for j in range(2)]
    evu = [cx.sbuf(f"ff_evu{j}", [128, TCH], F32) for j in range(2)]
    evd = [cx.sbuf(f"ff_evd{j}", [128, TCH], F32) for j in range(2)]
    evT = [cx.sbuf(f"ff_evT{j}", [128, TCH], F32) for j in range(2)]
    psTT = [cx.psum(f"ff_psTT{j}", [128, TCH], F32) for j in range(2)]
    ff_t = ffot_out.rearrange("(n p) c -> n p c", p=128)
    tp = TransPipe(cx, "ffT_")
    gp = GemmPipe(cx, "ffG_", NC)
    gpd = GemmPipe(cx, "ffD_", NI)

    p.sdma(scl[:, 0, 0:NI], aps["gs"])
    p.sdma(scl[:, 1, 0:NI], aps["us"])
    p.sdma(scl[:, 2, 0:NC], aps["ds"])
    prev_g = [None, None]
    prev_u = [None, None]
    prev_h = [None, None]
    prev_d = [None, None]
    prev_T = [None, None]

    def srcfn(base):
        def f(i, buf, deps):
            return p.step("sync", lambda e, i=i, buf=buf: e.dma_start(out=buf[:], in_=x_t[base + i]),
                          deps + [p.last], dma=True)
        return f

    for ch in range(NCH):
        tp.run(p, cm, srcfn(ch * (TCH // 128)), xc, TCH // 128)
        act_tok = p.last
        def evac_gate(it, ps, deps):
            j = it % 2
            t1 = p.step("vector", lambda e, it=it, j=j, ps=ps: e.tensor_scalar(evg[j][:], ps[:], scl[:, 0, it:it+1], None, ALU.mult), deps + [prev_g[j]])
            t2 = p.step("scalar", lambda e, j=j: e.activation(evg[j][:], evg[j][:], AF.Silu), [t1])
            t3 = p.step("vector", lambda e, it=it, j=j: e.tensor_copy(hdn[:, it, :], evg[j][:]), [t2])
            prev_g[j] = t3
            return t3
        gp.run(p, lambda dt: aps["gateT"].rearrange("(k p) i -> p k i", p=128)[:, :, ts(dt)],
               NI, xc, TCH, evac_gate, act_ready=act_tok)
        def evac_up(it, ps, deps):
            j = it % 2
            t1 = p.step("vector", lambda e, it=it, j=j, ps=ps: e.tensor_scalar(evu[j][:], ps[:], scl[:, 1, it:it+1], None, ALU.mult), deps + [prev_u[j]])
            t2 = p.step("vector", lambda e, it=it, j=j: e.tensor_tensor(hdn[:, it, :], hdn[:, it, :], evu[j][:], ALU.mult), [t1, prev_h[j]])
            prev_u[j] = t2
            prev_h[j] = t2
            return t2
        gp.run(p, lambda dt: aps["upT"].rearrange("(k p) i -> p k i", p=128)[:, :, ts(dt)],
               NI, xc, TCH, evac_up, act_ready=act_tok)
        def evac_down(ct, ps, deps, ch=ch):
            j = ct % 2
            t1 = p.step("vector", lambda e, ct=ct, j=j, ps=ps: e.tensor_scalar(evd[j][:], ps[:], scl[:, 2, ct:ct+1], None, ALU.mult), deps + [prev_d[j]])
            ttoks = []
            for tt in range(4):
                d2 = [t1, prev_T[j]] if tt == 0 else []
                ttoks.append(p.step("tensor", lambda e, tt=tt, j=j: e.transpose(psTT[j][:, ts(tt)], evd[j][:, ts(tt)], cm.identf[:]), d2))
            t2 = p.step("scalar", lambda e, j=j: e.activation(evT[j][:], psTT[j][:], AF.Copy), [ttoks[-1]])
            prev_T[j] = t2
            dtoks = []
            for tt in range(4):
                i = ch * 4 + tt
                dtoks.append(p.step("gpsimd", lambda e, i=i, tt=tt, j=j, ct=ct: e.dma_start(out=ff_t[i][:, ts(ct)], in_=evT[j][:, ts(tt)]), [t2], dma=True))
            prev_d[j] = dtoks[-1]
            p.last = dtoks[-1]
            return dtoks[-1]
        gpd.run(p, lambda ct: aps["downT"].rearrange("(k p) c -> p k c", p=128)[:, :, ts(ct)],
               NC, hdn, TCH, evac_down, act_ready=p.last)
    cx.pop_scope()


def emit_proj3(cx, cm, aps, x_dram, q_out, kin_out, vin_out):
    p = cx.p
    cx.push_scope()
    TCH = 512
    NCH = TT // TCH
    x_t = x_dram.rearrange("(n p) c -> n p c", p=128)
    xc = cx.sbuf("p3_xc", [128, NC, TCH], BF16)
    ev = [cx.sbuf(f"p3_ev{j}", [128, TCH], F32) for j in range(2)]
    tp = TransPipe(cx, "p3T_")
    gp = GemmPipe(cx, "p3G_", NC)
    prev_ev = [None, None]

    def srcfn(base):
        def f(i, buf, deps):
            return p.step("sync", lambda e, i=i, buf=buf: e.dma_start(out=buf[:], in_=x_t[base + i]),
                          deps + [p.last], dma=True)
        return f

    for ch in range(NCH):
        tp.run(p, cm, srcfn(ch * (TCH // 128)), xc, TCH // 128)
        act_tok = p.last
        for wn, outd in [("Wq1", q_out), ("Wk1", kin_out), ("Wv1", vin_out)]:
            outr = outd.rearrange("(n p) t -> n p t", p=128)

            def evac_fn(dt, ps, deps, outr=outr, ch=ch):
                j = dt % 2
                t1 = p.step("scalar", lambda e, j=j, ps=ps: e.activation(ev[j][:], ps[:], AF.Copy), deps + [prev_ev[j]])
                t2 = p.step("gpsimd", lambda e, dt=dt, j=j: e.dma_start(out=outr[dt, :, ch*TCH:(ch+1)*TCH], in_=ev[j][:]), [t1], dma=True)
                prev_ev[j] = t2
                return t2
            gp.run(p, lambda dt, wn=wn: aps[wn].rearrange("(k p) d -> p k d", p=128)[:, :, ts(dt)],
                   NC, xc, TCH, evac_fn, act_ready=act_tok)
    cx.pop_scope()


def emit_attn(cx, cm, aps, q_dram, kg_dram, vg_dram, o_out, TF=4096, HEADS=16, QG=4):
    p = cx.p
    cx.push_scope()
    NKB = TF // 128
    NQG = NT // QG
    W = QG * 128
    Qr = q_dram.rearrange("(h p) t -> h p t", p=128)
    Kr = kg_dram.rearrange("(two hh p) (j i) -> hh p j two i", two=2, p=128, i=128)
    Vr = vg_dram.rearrange("(two hh p) (j i) -> hh p j two i", two=2, p=128, i=128)
    Or = o_out.rearrange("(h p) t -> h p t", p=128)

    ldf = cx.sbuf("at_ldf", [128, TF], F32)
    vhb = cx.sbuf("at_vhb", [128, TF], BF16)
    vt = cx.sbuf("at_vt", [128, NKB, 128], BF16)
    khb = cx.sbuf("at_khb", [128, TF], BF16)
    qhb = cx.sbuf("at_qhb", [128, TT], BF16)
    m1s = cx.sbuf("at_m1s", [128, 8, W], F32)
    m2s = cx.sbuf("at_m2s", [128, 8, W], BF16)
    m2f = cx.sbuf("at_m2f", [128, 8, W], F32)
    selb = cx.sbuf("at_selb", [32, NT], BF16)
    self_ = cx.sbuf("at_self", [32, NT], F32)
    cnts = cx.sbuf("at_cnts", [1, NT], F32)
    onesk = cx.sbuf("at_onesk", [128, 1], BF16)
    onesr = cx.sbuf("at_onesr", [1, 128], BF16)
    bsum = cx.sbuf("at_bsum", [128, NKB], F32)
    bsumtb = cx.sbuf("at_bsumtb", [32, 128], BF16)
    sufq = cx.sbuf("at_sufq", [128, NT], F32)
    sufev = cx.sbuf("at_sufev", [16, 128], F32)
    pexp = [cx.sbuf(f"at_pexp{j}", [128, W], BF16) for j in range(2)]
    den_s = cx.sbuf("at_dens", [1, W], F32)
    denb = cx.sbuf("at_denb", [1, W], BF16)
    obuf = cx.sbuf("at_obuf", [128, W], F32)
    psS = [cx.psum(f"at_psS{j}", [128, W], F32) for j in range(2)]
    psO = cx.psum("at_psO", [128, W], F32)
    psDfull = cx.psum("at_psD", [128, W], F32)
    psD = psDfull[0:1, :]
    psT2 = cx.psum("at_psT2", [128, 128], BF16)
    psTf = psDfull[0:32, 0:128]
    psSuf = psDfull[0:16, 128:256]
    psSufT = psDfull[:, 256:272]
    psB = psDfull[:, 0:W]

    p.s("vector", lambda e: e.memset(onesk[:], 1.0))
    p.s("vector", lambda e: e.memset(onesr[:], 1.0))
    p.sdma(m1s[:], aps["m1"].rearrange("m p w -> p m w"))
    p.sdma(m2f[:], aps["m2"].rearrange("m p w -> p m w"))
    p.s("vector", lambda e: e.tensor_copy(m2s[:], m2f[:]))
    p.sdma(self_[:], aps["Sel"])
    p.s("vector", lambda e: e.tensor_copy(selb[:], self_[:]))
    p.sdma(cnts[:], aps["cnt"])
    ldf4 = ldf.ap().rearrange("p (j two i) -> p j two i", two=2, i=128)
    for h in range(HEADS):
        p.sdma(ldf4, Kr[h])
        p.s("vector", lambda e: e.tensor_copy(khb[:], ldf[:]))
        p.sdma(ldf[:, 0:TT], Qr[h])
        p.s("vector", lambda e: e.tensor_scalar(qhb[:], ldf[:, 0:TT], 1.0 / 64.0, None, ALU.mult))
        p.sdma(ldf4, Vr[h])
        p.s("vector", lambda e: e.tensor_copy(vhb[:], ldf[:]))
        p.s("vector", lambda e: e.reduce_sum(bsum[:], ldf.ap().rearrange("p (a b) -> p a b", a=NKB), axis=AX.X))
        for kb in range(NKB):
            p.s("tensor", lambda e, kb=kb: e.transpose(psT2[:], vhb[:, ts(kb)], cm.idb[:]))
            p.s("scalar", lambda e, kb=kb: e.activation(vt[:, kb, :], psT2[:], AF.Copy))
        p.s("tensor", lambda e: e.transpose(psTf[:], bsum[:], cm.identf[:]))
        p.s("vector", lambda e: e.tensor_copy(bsumtb[:], psTf[:]))
        p.s("tensor", lambda e: e.matmul(psSuf[:], selb[:], bsumtb[:], start=True, stop=True))
        p.s("scalar", lambda e: e.activation(sufev[:], psSuf[:], AF.Copy))
        p.s("tensor", lambda e: e.transpose(psSufT[:], sufev[:], cm.identf[0:16, 0:16]))
        p.s("scalar", lambda e: e.activation(sufq[:, 0:NT], psSufT[:], AF.Copy))
        for gi in range(NQG):
            qj0 = gi * QG
            qcols = slice(qj0 * 128, qj0 * 128 + W)
            nkb_c = 2 * qj0
            total_kb = nkb_c + 8
            t_prev = p.last
            tok_exp = [t_prev, t_prev]
            tok_av = [None, None]
            for kb in range(total_kb):
                mi = kb - nkb_c
                j = kb % 2
                tS = p.step("tensor", lambda e, kb=kb, qcols=qcols, j=j: e.matmul(psS[j][:], khb[:, ts(kb)], qhb[:, qcols], start=True, stop=True), [tok_exp[j]])
                if mi >= 0:
                    tS = p.step("vector", lambda e, mi=mi, j=j: e.tensor_tensor(psS[j][:], psS[j][:], m1s[:, mi, :], ALU.mult), [tS])
                tE = p.step("scalar", lambda e, j=j: e.activation(pexp[j][:], psS[j][:], AF.Exp), [tS, tok_av[j]])
                if mi >= 0:
                    tE = p.step("vector", lambda e, mi=mi, j=j: e.tensor_tensor(pexp[j][:], pexp[j][:], m2s[:, mi, :], ALU.mult), [tE])
                tok_exp[j] = tE
                tA = p.step("tensor", lambda e, kb=kb, total_kb=total_kb, j=j: e.matmul(psO[:], vt[:, kb, :], pexp[j][:], start=(kb == 0), stop=(kb == total_kb - 1), skip_group_check=True), [tE])
                tD = p.step("tensor", lambda e, kb=kb, total_kb=total_kb, j=j: e.matmul(psD[:], onesk[:], pexp[j][:], start=(kb == 0), stop=(kb == total_kb - 1), skip_group_check=True), [])
                tok_av[j] = tD
            p.last = tok_av[(total_kb - 1) % 2]
            p.s("vector", lambda e: e.tensor_copy(den_s[:], psD[:]))
            for t in range(QG):
                p.s("vector", lambda e, t=t, qj0=qj0: e.tensor_scalar(den_s[:, ts(t)], den_s[:, ts(t)], cnts[:, qj0 + t: qj0 + t + 1], None, ALU.add))
            p.s("scalar", lambda e: e.activation(den_s[:], den_s[:], AF.Ln))
            p.s("scalar", lambda e: e.activation(den_s[:], den_s[:], AF.Exp, scale=-1.0))
            p.s("vector", lambda e: e.tensor_copy(denb[:], den_s[:]))
            p.s("tensor", lambda e: e.matmul(psB[:], onesr[:], denb[:], start=True, stop=True))
            for t in range(QG):
                p.s("vector", lambda e, t=t, qj0=qj0: e.tensor_scalar(obuf[:, ts(t)], psO[:, ts(t)], sufq[:, qj0 + t: qj0 + t + 1], None, ALU.add))
            p.s("vector", lambda e: e.tensor_tensor(obuf[:], obuf[:], psB[:], ALU.mult))
            p.sdma(Or[h][:, qcols], obuf[:])
    cx.pop_scope()


def emit_round_chain(p, x, s127, z, m, oi):
    p.s("vector", lambda e: e.tensor_scalar(z[:], x[:], s127, 128.5, ALU.mult, ALU.add))
    p.s("vector", lambda e: e.tensor_copy(oi[:], z[:]))
    p.s("vector", lambda e: e.tensor_copy(m[:], oi[:]))
    p.s("vector", lambda e: e.tensor_tensor(z[:], m[:], z[:], ALU.is_gt))
    p.s("vector", lambda e: e.tensor_tensor(m[:], m[:], z[:], ALU.subtract))


def emit_quant(p, out_bf, x, s127, z, m, oi):
    emit_round_chain(p, x, s127, z, m, oi)
    p.s("vector", lambda e: e.tensor_scalar(out_bf[:], m[:], -128.0, None, ALU.add))


def emit_wquant(p, out_bf, w, swv, z, m, oi):
    emit_round_chain(p, w, swv, z, m, oi)
    p.s("vector", lambda e: e.tensor_scalar(m[:], m[:], -128.0, None, ALU.add))
    p.s("vector", lambda e: e.tensor_scalar(out_bf[:], m[:], 1.0, -1.0, ALU.min, ALU.max))


def emit_bitffn(cx, cm, aps, x5_dram, h1_dram, w1t_dram, w2t_dram, x6_out, H=8192):
    emit_bitffn_p1(cx, cm, aps, x5_dram, h1_dram, w1t_dram, H=H)
    emit_bitffn_p2(cx, cm, aps, x5_dram, h1_dram, w2t_dram, x6_out, H=H)


def emit_round_chain_on(p, eng, x, s127, z, m, oi, deps):
    t = p.step(eng, lambda e: e.tensor_scalar(z[:], x[:], s127, 128.5, ALU.mult, ALU.add), deps)
    t = p.step(eng, lambda e: e.tensor_copy(oi[:], z[:]), [t])
    t = p.step(eng, lambda e: e.tensor_copy(m[:], oi[:]), [t])
    t = p.step(eng, lambda e: e.tensor_tensor(z[:], m[:], z[:], ALU.is_gt), [t])
    t = p.step(eng, lambda e: e.tensor_tensor(m[:], m[:], z[:], ALU.subtract), [t])
    return t


def emit_wtern_pipe(cx, cm, pfx, p, w_src_rows, wt_dst_rows, nstrips, strip_w, mean_N, swv, mnv):
    cx.push_scope()
    ldw = [cx.sbuf(f"{pfx}ldw{j}", [128, strip_w], F32) for j in range(4)]
    zb = [cx.sbuf(f"{pfx}zb{j}", [128, strip_w], F32) for j in range(2)]
    mb = [cx.sbuf(f"{pfx}mb{j}", [128, strip_w], F32) for j in range(2)]
    oib = [cx.sbuf(f"{pfx}oib{j}", [128, strip_w], I32) for j in range(2)]
    wtb = [cx.sbuf(f"{pfx}wtb{j}", [128, strip_w], BF16) for j in range(2)]
    acc = cx.sbuf(pfx + "acc", [128, 4], F32)
    onescol = cx.sbuf(pfx + "onesc", [128, 1], F32)
    onesrow = cx.sbuf(pfx + "onesr", [1, 128], F32)
    swsc = cx.sbuf(pfx + "swsc", [1, 2], F32)
    psS1 = cx.psum(pfx + "psS1", [1, 1], F32)
    psRep = cx.psum(pfx + "psRep", [128, 1], F32)
    ENG = ("vector", "vector")
    p.s("vector", lambda e: e.memset(onescol[:], 1.0))
    p.s("vector", lambda e: e.memset(onesrow[:], 1.0))
    p.s("vector", lambda e: e.memset(acc[:, 0:2], 0.0))
    root = p.last
    tok_ld = [None] * 4
    tok_red = [None, None]
    for i in range(nstrips):
        j = i % 2
        tok_ld[j] = p.step("sync", lambda e, i=i, j=j: e.dma_start(out=ldw[j][:], in_=w_src_rows(i)),
                           [tok_red[j] if tok_red[j] else root], dma=True)
        t1 = p.step("vector", lambda e, j=j: e.reduce_sum(acc[:, 2+j:3+j], ldw[j][:], axis=AX.X, apply_absolute_value=True), [tok_ld[j], tok_red[j]])
        tok_red[j] = p.step("vector", lambda e, j=j: e.tensor_tensor(acc[:, j:j+1], acc[:, j:j+1], acc[:, 2+j:3+j], ALU.add), [t1])
    t = p.step("vector", lambda e: e.tensor_tensor(acc[:, 0:1], acc[:, 0:1], acc[:, 1:2], ALU.add), [tok_red[0], tok_red[1]])
    t = p.step("tensor", lambda e: e.matmul(psS1[:], acc[:, 0:1], onescol[:], start=True, stop=True), [t])
    t = p.step("vector", lambda e: e.tensor_scalar(swsc[:, 0:1], psS1[:], 1.0 / mean_N, 1e-5, ALU.mult, ALU.max), [t])
    t = p.step("vector", lambda e: e.reciprocal(swsc[:, 1:2], swsc[:, 0:1]), [t])
    t = p.step("tensor", lambda e: e.matmul(psRep[:], onesrow[:], swsc[:, 1:2], start=True, stop=True), [t])
    t = p.step("scalar", lambda e: e.activation(swv[:], psRep[:], AF.Copy), [t])
    t = p.step("tensor", lambda e: e.matmul(psRep[:], onesrow[:], swsc[:, 0:1], start=True, stop=True), [t])
    t = p.step("scalar", lambda e: e.activation(mnv[:], psRep[:], AF.Copy), [t])
    p.last = t
    root = t
    tok_st = [None, None]
    tok_ld2 = [None, None]
    for i in range(nstrips):
        j = i % 2
        tok_ld2[j] = p.step("sync", lambda e, i=i, j=j: e.dma_start(out=ldw[2+j][:], in_=w_src_rows(i)),
                            [tok_st[j] if tok_st[j] else root], dma=True)
        eng = ENG[j]
        tt = emit_round_chain_on(p, eng, ldw[2+j], swv[:, 0:1], zb[j], mb[j], oib[j], [tok_ld2[j], tok_st[j]])
        tt = p.step(eng, lambda e, j=j: e.tensor_scalar(mb[j][:], mb[j][:], -128.0, 1.0, ALU.add, ALU.min), [tt])
        tt = p.step(eng, lambda e, j=j: e.tensor_scalar(wtb[j][:], mb[j][:], -1.0, None, ALU.max), [tt])
        tok_st[j] = p.step("sync", lambda e, i=i, j=j: e.dma_start(out=wt_dst_rows(i), in_=wtb[j][:]), [tt], dma=True)
    t = p.step("gpsimd", lambda e: e.engine_nop(), [tok_st[0], tok_st[1]])
    p.last = t
    cx.pop_scope()


def emit_bitffn_p1(cx, cm, aps, x5_dram, h1_dram, w1t_dram, H=8192):
    p = cx.p
    NH = H // 128
    cx.push_scope()
    x_t = x5_dram.rearrange("(n p) c -> n p c", p=128)
    w_r = aps["W1T"].rearrange("(n p) h -> n p h", p=128)
    wt_r = w1t_dram.rearrange("(n p) h -> n p h", p=128)
    h1_t = h1_dram.rearrange("(n p) h -> n p h", p=128)
    WCH = 2048
    NS = NC * (H // WCH)
    swv = cx.sbuf("b1_swv", [128, 1], F32)
    mnv = cx.sbuf("b1_mnv", [128, 1], F32)

    def wsrc(i):
        ct, hch = divmod(i, H // WCH)
        return w_r[ct][:, hch*WCH:(hch+1)*WCH]

    def wdst(i):
        ct, hch = divmod(i, H // WCH)
        return wt_r[ct][:, hch*WCH:(hch+1)*WCH]
    emit_wtern_pipe(cx, cm, "b1w_", p, wsrc, wdst, NS, WCH, C * H, swv, mnv)

    xc = cx.sbuf("b1_xc", [128, NC, TT], BF16)
    fct = cx.sbuf("b1_fct", [128, NT], F32)
    cx.push_scope()
    ldf = [cx.sbuf(f"b1_ldf{j}", [128, C], F32) for j in range(2)]
    zb = [cx.sbuf(f"b1_zb{j}", [128, C], F32) for j in range(2)]
    mb = [cx.sbuf(f"b1_mb{j}", [128, C], F32) for j in range(2)]
    oib = [cx.sbuf(f"b1_oib{j}", [128, C], I32) for j in range(2)]
    xqb = [cx.sbuf(f"b1_xqb{j}", [128, C], BF16) for j in range(2)]
    st = cx.sbuf("b1_st", [128, 2, 4], F32)
    psT = [cx.psum(f"b1_psT{j}", [128, 512], BF16) for j in range(2)]
    ENG = ("vector", "vector")
    root = p.last
    tok_q = [root, root]
    tok_tr = [None, None]
    tok_ev = [None, None]
    for i in range(NT):
        j = i % 2
        eng = ENG[j]
        tl = p.step("sync", lambda e, i=i, j=j: e.dma_start(out=ldf[j][:], in_=x_t[i]), [tok_q[j]], dma=True)
        t = p.step("vector", lambda e, j=j: e.reduce_max(st[:, j, 0:1], ldf[j][:], axis=AX.X, apply_absolute_value=True), [tl])
        t = p.step("vector", lambda e, j=j: e.tensor_scalar(st[:, j, 0:1], st[:, j, 0:1], 1e-5, None, ALU.max), [t])
        t = p.step("vector", lambda e, j=j: e.reciprocal(st[:, j, 1:2], st[:, j, 0:1]), [t])
        t = p.step("vector", lambda e, j=j: e.tensor_scalar(st[:, j, 2:3], st[:, j, 1:2], 127.0, None, ALU.mult), [t])
        t = emit_round_chain_on(p, eng, ldf[j], st[:, j, 2:3], zb[j], mb[j], oib[j], [t])
        t = p.step(eng, lambda e, j=j: e.tensor_scalar(xqb[j][:], mb[j][:], -128.0, None, ALU.add), [t])
        t2 = p.step(eng, lambda e, i=i, j=j: e.tensor_scalar(fct[:, i:i+1], st[:, j, 0:1], 1.0 / 127.0, None, ALU.mult), [t])
        tok_q[j] = p.step(eng, lambda e, i=i: e.tensor_tensor(fct[:, i:i+1], fct[:, i:i+1], mnv[:], ALU.mult), [t2])
        evt = None
        for c4 in range(NC // 4):
            ttoks = []
            for cc in range(4):
                c = c4 * 4 + cc
                d2 = [tok_q[j], tok_ev[j]] if cc == 0 else []
                ttoks.append(p.step("tensor", lambda e, c=c, cc=cc, j=j: e.transpose(psT[j][:, ts(cc)], xqb[j][:, ts(c)], cm.idb[:]), d2))
            evt = p.step("scalar", lambda e, c4=c4, i=i, j=j: e.activation(
                xc[:, c4*4:(c4+1)*4, i*128:(i+1)*128],
                psT[j][:].rearrange("p (a b) -> p a b", a=4), AF.Copy), [ttoks[-1]])
            tok_ev[j] = evt
            tok_tr[j] = ttoks[0]
        tok_q[j] = evt
    p.last = tok_ev[0] if tok_ev[1] is None else tok_ev[1]
    act_tok = p.step("gpsimd", lambda e: e.engine_nop(), [tok_ev[0], tok_ev[1], tok_q[0], tok_q[1]])
    p.last = act_tok
    cx.pop_scope()

    wblk = [cx.sbuf(f"b1_wblk{j}", [128, NC, 512], BF16) for j in range(2)]
    psM = [cx.psum(f"b1_psM{j}", [128, 512], F32) for j in range(2)]
    ev = [cx.sbuf(f"b1_ev{j}", [128, 512], F32) for j in range(2)]
    tok_wld = [None, None]
    tok_mm = [None, None]
    tok_evc = [None, None]
    tok_dma = [None, None]
    for hcki in range(H // 512):
        jw = hcki % 2
        tok_wld[jw] = p.step("sync", lambda e, hcki=hcki, jw=jw: e.dma_start(
            out=wblk[jw][:], in_=wt_r[:, :, hcki*512:(hcki+1)*512].rearrange("n p h -> p n h")),
            [tok_mm[jw], act_tok], dma=True)
        for i in range(NT):
            jp = i % 2
            t0 = None
            for kt2 in range(NC):
                deps = [tok_wld[jw], tok_evc[jp]] if kt2 == 0 else []
                t0 = p.step("tensor", lambda e, kt2=kt2, i=i, jw=jw, jp=jp: e.matmul(
                    psM[jp][:], xc[:, kt2, i*128:(i+1)*128], wblk[jw][:, kt2, :],
                    start=(kt2 == 0), stop=(kt2 == NC - 1), skip_group_check=True), deps)
            if i == NT - 1:
                tok_mm[jw] = t0
            t1 = p.step("vector", lambda e, i=i, jp=jp: e.tensor_scalar(ev[jp][:], psM[jp][:], fct[:, i:i+1], None, ALU.mult), [t0, tok_dma[jp]])
            t2 = p.step("scalar", lambda e, jp=jp: e.activation(ev[jp][:], ev[jp][:], AF.Silu), [t1])
            tok_evc[jp] = t2
            tok_dma[jp] = p.step("sync", lambda e, i=i, hcki=hcki, jp=jp: e.dma_start(
                out=h1_t[i][:, hcki*512:(hcki+1)*512], in_=ev[jp][:]), [t2], dma=True)
    t = p.step("gpsimd", lambda e: e.engine_nop(), [tok_dma[0], tok_dma[1], tok_mm[0], tok_mm[1]])
    p.last = t
    cx.pop_scope()


def emit_bitffn_p2(cx, cm, aps, x5_dram, h1_dram, w2t_dram, x6_out, H=8192):
    p = cx.p
    NH = H // 128
    cx.push_scope()
    x_t = x5_dram.rearrange("(n p) c -> n p c", p=128)
    h_t = h1_dram.rearrange("(n p) h -> n p h", p=128)
    w2_r = aps["W2T"].rearrange("(n p) c -> n p c", p=128)
    wt2_r = w2t_dram.rearrange("(n p) c -> n p c", p=128)
    x6_t = x6_out.rearrange("(n p) c -> n p c", p=128)
    QT = 4
    WCH = 2048
    swv = cx.sbuf("b2_swv", [128, 1], F32)
    mnv = cx.sbuf("b2_mnv", [128, 1], F32)
    emit_wtern_pipe(cx, cm, "b2w_", p, lambda i: w2_r[i], lambda i: wt2_r[i], NH, C, C * H, swv, mnv)

    xc2 = cx.sbuf("b2_xc2", [128, NH, QT * 128], BF16)
    obuf = cx.sbuf("b2_obuf", [128, QT, C], F32)
    fct = cx.sbuf("b2_fct", [128, QT], F32)
    ldh = cx.sbuf("b2_ldh", [128, H], F32)
    xqb = cx.sbuf("b2_xqb", [128, H], BF16)
    zb = [cx.sbuf(f"b2_zb{j}", [128, WCH], F32) for j in range(2)]
    mb = [cx.sbuf(f"b2_mb{j}", [128, WCH], F32) for j in range(2)]
    oib = [cx.sbuf(f"b2_oib{j}", [128, WCH], I32) for j in range(2)]
    st = cx.sbuf("b2_st", [128, 4], F32)
    psT = [cx.psum(f"b2_psT{j}", [128, 512], BF16) for j in range(2)]
    wblk = [cx.sbuf(f"b2_wblk{j}", [128, 512], BF16) for j in range(2)]
    psM = [cx.psum(f"b2_psM{t}", [128, 512], F32) for t in range(QT)]
    xs = cx.sbuf("b2_xs", [128, C], F32)
    stl = cx.sbuf("b2_stl", [128, 8], F32)
    ENG = ("vector", "vector")
    tok_evO = [None] * QT
    tok_ldh = p.last
    for qu in range(NT // QT):
        tok_trev = [None, None]
        act_last = None
        for t in range(QT):
            i = qu * QT + t
            tld = p.step("sync", lambda e, i=i: e.dma_start(out=ldh[:], in_=h_t[i]), [tok_ldh], dma=True)
            tv = p.step("vector", lambda e: e.reduce_max(st[:, 0:1], ldh[:], axis=AX.X, apply_absolute_value=True), [tld])
            tv = p.step("vector", lambda e: e.tensor_scalar(st[:, 0:1], st[:, 0:1], 1e-5, None, ALU.max), [tv])
            tv = p.step("vector", lambda e: e.reciprocal(st[:, 1:2], st[:, 0:1]), [tv])
            tv = p.step("vector", lambda e: e.tensor_scalar(st[:, 2:3], st[:, 1:2], 127.0, None, ALU.mult), [tv])
            ch_toks = []
            for hch in range(H // WCH):
                j = hch % 2
                sl2 = slice(hch * WCH, (hch + 1) * WCH)
                tt = emit_round_chain_on(p, ENG[j], ldh[:, sl2], st[:, 2:3], zb[j], mb[j], oib[j],
                                         [tv, ch_toks[-2] if len(ch_toks) >= 2 else None])
                tt = p.step(ENG[j], lambda e, j=j, sl2=sl2: e.tensor_scalar(xqb[:, sl2], mb[j][:], -128.0, None, ALU.add), [tt])
                ch_toks.append(tt)
            tq = p.step("vector", lambda e, t=t: e.tensor_scalar(fct[:, t:t+1], st[:, 0:1], 1.0 / 127.0, None, ALU.mult), ch_toks)
            tok_ldh = p.step("vector", lambda e, t=t: e.tensor_tensor(fct[:, t:t+1], fct[:, t:t+1], mnv[:], ALU.mult), [tq])
            evt = None
            for c4 in range(NH // 4):
                jj = c4 % 2
                ttoks = []
                for cc in range(4):
                    c = c4 * 4 + cc
                    d2 = [tok_ldh, tok_trev[jj]] if cc == 0 else []
                    ttoks.append(p.step("tensor", lambda e, c=c, cc=cc, jj=jj: e.transpose(psT[jj][:, ts(cc)], xqb[:, ts(c)], cm.idb[:]), d2))
                evt = p.step("scalar", lambda e, c4=c4, t=t, jj=jj: e.activation(
                    xc2[:, c4*4:(c4+1)*4, t*128:(t+1)*128],
                    psT[jj][:].rearrange("p (a b) -> p a b", a=4), AF.Copy), [ttoks[-1]])
                tok_trev[jj] = evt
            act_last = evt
            tok_ldh = act_last
        actbar = p.step("gpsimd", lambda e: e.engine_nop(), [tok_trev[0], tok_trev[1]])
        tok_wld = [None, None]
        tok_mmk = [None, None]
        for cck in range(C // 512):
            mm_last = [None] * QT
            for kt2 in range(NH):
                jw = kt2 % 2
                tok_wld[jw] = p.step("sync", lambda e, kt2=kt2, cck=cck, jw=jw: e.dma_start(
                    out=wblk[jw][:], in_=wt2_r[kt2][:, cck*512:(cck+1)*512]), [tok_mmk[jw], actbar], dma=True)
                for t in range(QT):
                    deps = []
                    if t == 0:
                        deps = [tok_wld[jw]]
                    if kt2 == 0:
                        deps = deps + [tok_evO[t], actbar]
                    tm = p.step("tensor", lambda e, kt2=kt2, t=t, jw=jw: e.matmul(
                        psM[t][:], xc2[:, kt2, t*128:(t+1)*128], wblk[jw][:],
                        start=(kt2 == 0), stop=(kt2 == NH - 1), skip_group_check=True), deps)
                    mm_last[t] = tm
                tok_mmk[jw] = mm_last[QT - 1]
            for t in range(QT):
                tok_evO[t] = p.step("vector", lambda e, t=t, cck=cck: e.tensor_scalar(
                    obuf[:, t, cck*512:(cck+1)*512], psM[t][:], fct[:, t:t+1], None, ALU.mult), [mm_last[t], tok_evO[t]])
        pl = p.last
        for t in range(QT):
            i = qu * QT + t
            p.last = tok_evO[t]
            p.s("sync", lambda e, i=i: e.dma_start(out=xs[:], in_=x_t[i]), dma=True)
            p.s("vector", lambda e, t=t: e.tensor_tensor(xs[:], xs[:], obuf[:, t, :], ALU.add))
            emit_ln_stats_tile(p, xs, obuf[:, t, :], stl)
            p.s("vector", lambda e: e.tensor_scalar(xs[:], xs[:], stl[:, 2:3], stl[:, 7:8], ALU.subtract, ALU.mult))
            tok_evO[t] = p.sdma(x6_t[i], xs[:])
    p.last = p.step("gpsimd", lambda e: e.engine_nop(), [tok_evO[0], tok_evO[1], tok_evO[2], tok_evO[3]])
    cx.pop_scope()


def build_fused(H=8192, I=5632):
    nc = bass.Bass("TRN2", target_bir_lowering=False, debug=False)
    ap = {}
    ins_spec = [
        ("x", [TT, C]), ("ident", [128, 128]), ("pv", [1, 2]), ("zmask", [1, NT]),
        ("ln1w", [128, C]), ("ln1b", [128, C]), ("ln2w", [128, C]), ("ln2b", [128, C]),
        ("tmk", [128, NC]), ("tmv", [128, NC]), ("tmr", [128, NC]),
        ("WkT", [C, C]), ("WvT", [C, C]), ("WrT", [C, C]), ("WoT", [C, C]),
        ("td8", [128, 8]), ("tf8", [128, 8]),
        ("cmk", [128, NC]), ("cmr", [128, NC]),
        ("WckT", [C, H]), ("WcvT", [H, C]), ("WcrT", [C, C]),
        ("gateT", [C, I]), ("upT", [C, I]), ("downT", [I, C]),
        ("gs", [128, I // 128]), ("us", [128, I // 128]), ("ds", [128, NC]),
        ("Wq1", [C, C]), ("Wk1", [C, C]), ("Wv1", [C, C]), ("Wo1", [C, C]),
        ("m1", [8, 128, 512]), ("m2", [8, 128, 512]), ("Sel", [32, NT]), ("cnt", [1, NT]),
        ("W1T", [C, H]), ("W2T", [H, C]),
    ]
    for n, sh in ins_spec:
        ap[n] = nc.dram_tensor(n, sh, F32, kind="ExternalInput").ap()
    out = nc.dram_tensor("out", [TT, C], F32, kind="ExternalOutput").ap()
    dr = {}
    for n, sh, dt in [
        ("xn", [TT, C], F32), ("halo1_in", [16, C], F32), ("halo1_out", [32, C], F32),
        ("k_c", [C, TT], F32), ("v_c", [C, TT], F32), ("sr_c", [C, TT], F32),
        ("kga", [2 * C, TT], F32), ("vga", [2 * C, TT], F32),
        ("y_my", [1024, 2 * TT], F32), ("yga", [C, 2 * TT], F32),
        ("x2", [TT, C], F32), ("xn2", [TT, C], F32),
        ("halo2_in", [16, C], F32), ("halo2_out", [32, C], F32),
        ("rkv_t", [TT, C], F32), ("x3", [TT, C], F32), ("ffo_t", [TT, C], F32),
        ("x4", [TT, C], F32), ("Q_c", [C, TT], F32),
        ("K_in", [C, TT], F32), ("V_in", [C, TT], F32),
        ("KG", [2 * C, TT], F32), ("VG", [2 * C, TT], F32),
        ("O_c", [C, TT], F32), ("x5pre", [TT, C], F32), ("x5", [TT, C], F32),
        ("h1", [TT, H], F32), ("w1t", [C, H], BF16), ("w2t", [H, C], BF16),
    ]:
        dr[n] = nc.dram_tensor("i_" + n, sh, dt).ap()

    cx = Ctx(nc)
    p = cx.p
    ccsem_cm = nc.semaphore("ccsem")
    ccsem = ccsem_cm.__enter__()
    ccount = [0]
    cm = Common(cx, ap["ident"])
    pvs = cx.sbuf("cm_pv", [1, 2], F32)
    zms = cx.sbuf("cm_zmask", [1, NT], F32)
    onesr_f = cx.sbuf("cm_onesrf", [1, 128], F32)
    pv128 = cx.sbuf("cm_pv128", [128, 2], F32)
    p.sdma(pvs[:], ap["pv"])
    p.sdma(zms[:], ap["zmask"])
    cx.push_scope()
    psPv = cx.psum("cm_psPv", [128, 2], F32)
    p.s("vector", lambda e: e.memset(onesr_f[:], 1.0))
    p.s("tensor", lambda e: e.matmul(psPv[:], onesr_f[:], pvs[:], start=True, stop=True))
    p.s("scalar", lambda e: e.activation(pv128[:], psPv[:], AF.Copy))
    cx.pop_scope()
    sv_halo = pvs[0:1, 0:1]
    sv_par = pv128[:, 1:2]

    p.stage = "ln1"
    emit_ln(cx, "ln1_", ap["x"], dr["xn"], ap["ln1w"], ap["ln1b"], halo_out=dr["halo1_in"])
    emit_cc(cx, "AllGather", dr["halo1_in"], dr["halo1_out"], ccsem, ccount)
    halo1 = dr["halo1_out"].rearrange("(two n) c -> two n c", two=2)
    p.stage = "mix_kvr"
    emit_mix_kvr(cx, cm, ap, dr["xn"], halo1, sv_halo, zms, dr["k_c"], dr["v_c"], dr["sr_c"])
    emit_cc(cx, "AllGather", dr["k_c"], dr["kga"], ccsem, ccount)
    emit_cc(cx, "AllGather", dr["v_c"], dr["vga"], ccsem, ccount)
    p.stage = "wkv"
    emit_wkv(cx, ap, dr["kga"], dr["vga"], dr["y_my"], sv_par)
    emit_cc(cx, "AllGather", dr["y_my"], dr["yga"], ccsem, ccount)
    p.stage = "wo1"
    emit_wo_res(cx, cm, ap["WoT"], dr["yga"], dr["sr_c"], ap["x"], dr["x2"],
                use_sr=True, att_is_gather=True, svec_sb=sv_par)
    p.stage = "ln2"
    emit_ln(cx, "ln2_", dr["x2"], dr["xn2"], ap["ln2w"], ap["ln2b"], halo_out=dr["halo2_in"])
    emit_cc(cx, "AllGather", dr["halo2_in"], dr["halo2_out"], ccsem, ccount)
    halo2 = dr["halo2_out"].rearrange("(two n) c -> two n c", two=2)
    p.stage = "cmix"
    emit_cmix(cx, cm, ap, dr["xn2"], halo2, sv_halo, zms, dr["rkv_t"], H=H)
    p.stage = "ln3"
    emit_ln(cx, "ln3_", ap["x"], dr["x3"], r_ts=[dr["x2"], dr["rkv_t"]])
    p.stage = "l0ffn"
    emit_l0ffn(cx, cm, ap, dr["x3"], dr["ffo_t"], I=I)
    p.stage = "ln4"
    emit_ln(cx, "ln4_", dr["x3"], dr["x4"], r_ts=[dr["ffo_t"]])
    p.stage = "proj3"
    emit_proj3(cx, cm, ap, dr["x4"], dr["Q_c"], dr["K_in"], dr["V_in"])
    emit_cc(cx, "AllGather", dr["K_in"], dr["KG"], ccsem, ccount)
    emit_cc(cx, "AllGather", dr["V_in"], dr["VG"], ccsem, ccount)
    p.stage = "attn"
    emit_attn(cx, cm, ap, dr["Q_c"], dr["KG"], dr["VG"], dr["O_c"])
    p.stage = "wo2"
    emit_wo_res(cx, cm, ap["Wo1"], dr["O_c"], None, dr["x4"], dr["x5pre"], use_sr=False, pfx="wo2_")
    p.stage = "ln5"
    emit_ln(cx, "ln5_", dr["x5pre"], dr["x5"])
    p.stage = "bitffn"
    emit_bitffn(cx, cm, ap, dr["x5"], dr["h1"], dr["w1t"], dr["w2t"], out, H=H)
    nc_ret = cx.close()
    ccsem_cm.__exit__(None, None, None)
    return nc_ret


class GemmPipe:

    def __init__(self, cx, pfx, NKT, TCH=512):
        self.pfx, self.NKT, self.TCH = pfx, NKT, TCH
        self.ws = [cx.sbuf(f"{pfx}ws{j}", [128, NKT, 128], F32) for j in range(2)]
        self.wsb = [cx.sbuf(f"{pfx}wsb{j}", [128, NKT, 128], BF16) for j in range(2)]
        self.psM = [cx.psum(f"{pfx}psM{j}", [128, TCH], F32) for j in range(2)]
        self.tok_conv = [None, None]
        self.tok_mm = [None, None]
        self.tok_evac = [None, None]

    def run(self, p, w_strip_ap, NDT, act_sb, act_cols, evac_fn, wrows=None, act_ready=None):
        nkt = wrows if wrows is not None else self.NKT
        last = None
        for dt in range(NDT):
            j = dt % 2
            tl = p.step("sync", lambda e, dt=dt, j=j, nkt=nkt: e.dma_start(out=self.ws[j][:, 0:nkt, :], in_=w_strip_ap(dt)),
                        [self.tok_conv[j]], dma=True, tag=self.pfx + "ld")
            self.tok_conv[j] = p.step("vector", lambda e, j=j, nkt=nkt: e.tensor_copy(self.wsb[j][:, 0:nkt, :], self.ws[j][:, 0:nkt, :]),
                                      [tl, self.tok_mm[j]], tag=self.pfx + "cv")
            t0 = None
            for kt in range(nkt):
                deps = [self.tok_conv[j], self.tok_evac[j], act_ready] if kt == 0 else []
                t0 = p.step("tensor", lambda e, kt=kt, j=j, act_cols=act_cols, nkt=nkt: e.matmul(
                    self.psM[j][:, 0:act_cols], self.wsb[j][:, kt, :], act_sb[:, kt, 0:act_cols],
                    start=(kt == 0), stop=(kt == nkt - 1), skip_group_check=True), deps, tag=self.pfx + "mm")
            self.tok_mm[j] = t0
            self.tok_evac[j] = evac_fn(dt, self.psM[j], [t0])
            last = self.tok_evac[j]
        p.last = last
        return last


class TransPipe:
    def __init__(self, cx, pfx):
        self.ldf = [cx.sbuf(f"{pfx}ldf{j}", [128, C], F32) for j in range(2)]
        self.bfa = [cx.sbuf(f"{pfx}bfa{j}", [128, C], BF16) for j in range(2)]
        self.psT = [cx.psum(f"{pfx}psT{j}", [128, 512], BF16) for j in range(2)]
        self.pfx = pfx
        self.tok_cv = [None, None]
        self.tok_tr = [None, None]
        self.tok_ev = [None, None]

    def run(self, p, cm, src_rows_fn, dst_sb, ntiles, dst_col0=0):
        last = None
        for i in range(ntiles):
            j = i % 2
            tl = src_rows_fn(i, self.ldf[j], [self.tok_cv[j]])
            self.tok_cv[j] = p.step("vector", lambda e, j=j: e.tensor_copy(self.bfa[j][:], self.ldf[j][:]),
                                    [tl, self.tok_tr[j]], tag=self.pfx + "cv")
            ev_tok = None
            for c4 in range(NC // 4):
                first_tr = None
                last_tr = None
                for cc in range(4):
                    c = c4 * 4 + cc
                    deps = [self.tok_cv[j], self.tok_ev[j]] if cc == 0 else []
                    t = p.step("tensor", lambda e, c=c, cc=cc, j=j: e.transpose(
                        self.psT[j][:, ts(cc)], self.bfa[j][:, ts(c)], cm.idb[:]), deps, tag=self.pfx + "tr")
                    if cc == 0:
                        first_tr = t
                    last_tr = t
                ev_tok = p.step("scalar", lambda e, c4=c4, i=i, j=j, dst_col0=dst_col0, dst_sb=dst_sb: e.activation(
                    dst_sb[:, c4*4:(c4+1)*4, dst_col0 + i*128: dst_col0 + (i+1)*128],
                    self.psT[j][:].rearrange("p (a b) -> p a b", a=4), AF.Copy), [last_tr], tag=self.pfx + "ev")
                self.tok_tr[j] = first_tr
                self.tok_ev[j] = ev_tok
            last = ev_tok
            p.last = last
        return last


import concourse.bass_utils as bass_utils


def shard_tokens(x):
    B, T, F = x.shape
    xb = x.reshape(B, 32, 128, F)
    out = []
    for p2 in range(B):
        for r in range(2):
            out.append(np.ascontiguousarray(xb[p2, r::2].reshape(2048, F)))
    return out


def unshard_tokens(shards):
    F = shards[0].shape[-1]
    out = np.empty((4, 32, 128, F), np.float32)
    for p2 in range(4):
        for r in range(2):
            out[p2, r::2] = np.asarray(shards[2 * p2 + r], dtype=np.float32).reshape(16, 128, F)
    return out.reshape(4, 4096, F)


def rep128(v):
    return np.ascontiguousarray(np.broadcast_to(np.asarray(v, np.float32)[None, :], (128, len(v))))


def cvec(v, n=16):
    return np.ascontiguousarray(np.asarray(v, np.float32).reshape(n, 128).T)


_PROGS = {}


def kernel(**ins):
    T_ = lambda a: np.ascontiguousarray(np.asarray(a, np.float32).T)
    A_ = lambda a: np.ascontiguousarray(np.asarray(a, np.float32))
    ident = np.eye(128, dtype=np.float32)
    x_s = shard_tokens(np.asarray(ins["L0_attn_x"], np.float32))
    tri = (np.arange(128)[:, None] <= np.arange(128)[None, :]).astype(np.float32)

    def masks(rr):
        m1 = np.zeros((8, 128, 512), np.float32)
        m2 = np.zeros((8, 128, 512), np.float32)
        for mi in range(8):
            for t in range(4):
                g2 = 2 * t + rr
                if mi < g2:
                    m1[mi, :, t*128:(t+1)*128] = 1.0
                    m2[mi, :, t*128:(t+1)*128] = 1.0
                elif mi == g2:
                    m1[mi, :, t*128:(t+1)*128] = tri
                    m2[mi, :, t*128:(t+1)*128] = 1.0
        return m1, m2

    def sel_cnt(rr):
        Sel = np.zeros((32, 16), np.float32)
        cnt = np.zeros((1, 16), np.float32)
        for qj in range(16):
            g = 2 * qj + rr
            Sel[g+1:, qj] = 1.0
            cnt[0, qj] = 4096 - (g + 1) * 128
        return Sel, cnt

    in_maps = []
    for i in range(8):
        r = i % 2
        m1, m2 = masks(r)
        Sel, cnt = sel_cnt(r)
        zmask = np.ones((1, 16), np.float32)
        if r == 0:
            zmask[0, 0] = 0.0
        sl = slice(r * 1024, (r + 1) * 1024)
        in_maps.append({
            "x": x_s[i], "ident": ident, "pv": np.array([[1.0 - r, float(r)]], np.float32), "zmask": zmask,
            "ln1w": rep128(ins["L0_attn_ln1_w"]), "ln1b": rep128(ins["L0_attn_ln1_b"]),
            "ln2w": rep128(ins["L0_attn_ln2_w"]), "ln2b": rep128(ins["L0_attn_ln2_b"]),
            "tmk": cvec(ins["L0_attn_tmk"]), "tmv": cvec(ins["L0_attn_tmv"]), "tmr": cvec(ins["L0_attn_tmr"]),
            "WkT": T_(ins["L0_attn_Wk"]), "WvT": T_(ins["L0_attn_Wv"]), "WrT": T_(ins["L0_attn_Wr"]),
            "WoT": T_(ins["L0_attn_Wo"]),
            "td8": cvec(ins["L0_attn_time_decay"][sl], 8), "tf8": cvec(ins["L0_attn_time_first"][sl], 8),
            "cmk": cvec(ins["L0_attn_cmk"]), "cmr": cvec(ins["L0_attn_cmr"]),
            "WckT": T_(ins["L0_attn_Wck"]), "WcvT": T_(ins["L0_attn_Wcv"]), "WcrT": T_(ins["L0_attn_Wcr"]),
            "gateT": T_(ins["L0_ffn_gate_w"]), "upT": T_(ins["L0_ffn_up_w"]), "downT": T_(ins["L0_ffn_down_w"]),
            "gs": cvec(ins["L0_ffn_gate_s"], 44), "us": cvec(ins["L0_ffn_up_s"], 44), "ds": cvec(ins["L0_ffn_down_s"], 16),
            "Wq1": A_(ins["L1_attn_Wq"]), "Wk1": A_(ins["L1_attn_Wk"]), "Wv1": A_(ins["L1_attn_Wv"]),
            "Wo1": A_(ins["L1_attn_Wo"]),
            "m1": m1, "m2": m2, "Sel": Sel, "cnt": cnt,
            "W1T": T_(ins["L1_ffn_W1"]), "W2T": T_(ins["L1_ffn_W2"]),
        })
    if "nc" not in _PROGS:
        _PROGS["nc"] = build_fused()
    res = bass_utils.run_bass_kernel_spmd(_PROGS["nc"], in_maps, core_ids=list(range(8)))
    return unshard_tokens([res.results[i]["out"] for i in range(8)])
